# version 32
# baseline (speedup 1.0000x reference)
import math
import numpy as np
import ml_dtypes
import concourse.bass as bass
import concourse.mybir as mybir
from concourse.bass_utils import run_bass_kernel_spmd

F32 = mybir.dt.float32
BF16 = mybir.dt.bfloat16
AF = mybir.ActivationFunctionType
OP = mybir.AluOpType
MULT = OP.mult
ADD = OP.add

E = 1024
P = 128
NS = 8
D = 64
H = 16
CH = 64
EPS = 1e-5
SCALE = D ** -0.5

ENGINES = ("sp", "pe", "act", "dve", "pool")


class _Op:
    __slots__ = ("idx", "eng", "fn", "deps", "dma", "need_inc", "inc_val", "waits", "tag")

    def __init__(self, idx, eng, fn, dma, tag):
        self.idx = idx
        self.eng = eng
        self.fn = fn
        self.deps = set()
        self.dma = dma
        self.need_inc = False
        self.inc_val = None
        self.waits = []
        self.tag = tag


class Emitter:
    def __init__(self):
        self.ops = []
        self.writer = {}
        self.readers = {}

    def op(self, eng, fn, rd=(), wr=(), dma=None, tag=""):
        o = _Op(len(self.ops), eng, fn, dma, tag)
        for r in rd:
            w = self.writer.get(r)
            if w is not None:
                o.deps.add(w)
            self.readers.setdefault(r, []).append(o.idx)
        for r in wr:
            w = self.writer.get(r)
            if w is not None:
                o.deps.add(w)
            for ri in self.readers.get(r, ()):
                o.deps.add(ri)
            self.writer[r] = o.idx
            self.readers[r] = []
        o.deps.discard(o.idx)
        self.ops.append(o)
        return o

    def finalize(self, sems, dsems):
        for o in self.ops:
            for d in o.deps:
                p = self.ops[d]
                if p.dma is not None or p.eng != o.eng:
                    p.need_inc = True
        counts = {e: 0 for e in ENGINES}
        dcounts = {q: 0 for q in dsems}
        for o in self.ops:
            if o.dma is not None:
                dcounts[o.dma] += 16
                o.inc_val = dcounts[o.dma]
            elif o.need_inc:
                counts[o.eng] += 1
                o.inc_val = counts[o.eng]
        for o in self.ops:
            need = {}
            if o.dma is not None and o.inc_val > 16:
                need[("d", o.dma)] = o.inc_val - 16
            for d in sorted(o.deps):
                p = self.ops[d]
                if p.dma is not None:
                    k = ("d", p.dma)
                    need[k] = max(need.get(k, 0), p.inc_val)
                elif p.eng != o.eng:
                    k = ("e", p.eng)
                    need[k] = max(need.get(k, 0), p.inc_val)
            o.waits = sorted(need.items())
        self._sems = sems
        self._dsems = dsems
        self.dcounts = dcounts

    def emit(self, eng_name, eng):
        water = {}
        for o in self.ops:
            if o.eng != eng_name:
                continue
            for (kind, key), val in o.waits:
                sem = self._dsems[key] if kind == "d" else self._sems[key]
                if val > water.get((kind, key), 0):
                    eng.wait_ge(sem, val)
                    water[(kind, key)] = val
            inst = o.fn(eng)
            if o.dma is not None:
                inst.then_inc(self._dsems[o.dma], 16)
            elif o.need_inc:
                inst.then_inc(self._sems[eng_name], 1)


def build_program(T, TT, limit=None):
    assert T % TT == 0 and TT % CH == 0 and TT % P == 0
    NTILE = T // TT
    C = TT // CH
    TT1 = TT + 1
    NSUB = TT // P
    nc = bass.Bass("TRN2", target_bir_lowering=False, debug=False,
                   detect_race_conditions=False)

    dt = {}

    def din(name, shape, dtype=F32):
        dt[name] = nc.dram_tensor(name, list(shape), dtype, kind="ExternalInput").ap()
        return dt[name]

    din("xT", (E, T))
    outT = nc.dram_tensor("outT", [E, T // 2], F32, kind="ExternalOutput").ap()
    for w in ("wr", "wk", "wv", "wq", "wka", "wva", "wo"):
        din(w, (E, E))
    din("wf1", (E, D)); din("wg1", (E, D))
    din("wf2", (D, E)); din("wg2", (D, E))
    din("mu_r", (P, NS)); din("mu_k", (P, NS)); din("nw", (P, NS))
    din("ident", (P, P), BF16)
    din("mask", (P, 256), BF16)
    din("gate", (P, TT), BF16)
    din("ones1", (P, 1), BF16)
    din("onesb", (1, P), BF16)
    din("gnacc", (P, P), BF16)
    din("gnsel", (16, NS * P), BF16)

    em = Emitter()
    import contextlib
    ctx = contextlib.ExitStack()
    sb = {}

    def sbuf(name, shape, dtype):
        sb[name] = ctx.enter_context(nc.sbuf_tensor(name, list(shape), dtype))

    for w in ("wr", "wk", "wv", "wq", "wka", "wva", "wo"):
        sbuf(w + "_s", (P, NS * E), BF16)
    sbuf("wf1_s", (P, NS * D), BF16); sbuf("wg1_s", (P, NS * D), BF16)
    sbuf("wf2_s", (D, E), BF16); sbuf("wg2_s", (D, E), BF16)
    for v in ("mu_r", "mu_k", "nw"):
        sbuf(v + "_s", (P, NS), F32)
    sbuf("ident_s", (P, P), BF16)
    sbuf("mask_s", (P, 256), BF16)
    sbuf("gate_s", (P, TT), BF16)
    sbuf("ones1_s", (P, 1), BF16)
    sbuf("onesb_s", (1, P), BF16)
    sbuf("gnacc_s", (P, P), BF16)
    sbuf("gnsel_s", (16, NS * P), BF16)

    sbuf("xio0", (P, NS * TT), F32)
    sbuf("xio1", (P, NS * TT), F32)
    sbuf("lnx", (P, NS * TT1), BF16)
    sbuf("lnx2", (P, NS * TT1), BF16)
    for s in ("sA", "sB", "sC", "sD", "sF", "sG", "sH"):
        sbuf(s, (P, NS * TT), BF16)
    sbuf("sEx", (P, 2048), BF16)
    sbuf("kt", (P, 2048), BF16)
    sbuf("st1", (P, 512), F32)
    sbuf("st2", (P, 512), F32)
    sbuf("Ssb", (P, NS * 5 * D), BF16)
    sbuf("lam", (P, NS * C), F32)
    sbuf("mid2", (D, 2 * TT), BF16)
    sbuf("lnm", (1, TT), F32); sbuf("lnv", (1, TT), F32); sbuf("lnt", (1, TT), F32)
    for s_ in (0, 1):
        sbuf("lnAb%d" % s_, (1, TT), BF16); sbuf("lnBb%d" % s_, (1, TT), BF16)
    sbuf("gns", (16, TT), F32); sbuf("gnsb", (16, TT), BF16)
    sbuf("cEps", (P, 1), F32); sbuf("cOne", (P, 1), F32); sbuf("cLnS", (P, 1), F32)
    sbuf("kp", (P, 1024), BF16)
    sbuf("lnab", (P, 4 * TT), BF16)

    ps = {}
    for i in range(6):
        ps[i] = ctx.enter_context(nc.psum_tensor("pb%d" % i, [P, 512], F32))
    ps[6] = ctx.enter_context(nc.psum_tensor("pt6", [P, 1024], BF16))
    ps[7] = ctx.enter_context(nc.psum_tensor("pb7", [P, 512], F32))

    sems = {e: ctx.enter_context(nc.semaphore("sem_" + e)) for e in ENGINES}
    dsems = {"sp": ctx.enter_context(nc.semaphore("dma_sp")),
             "spc": ctx.enter_context(nc.semaphore("dma_spc")),
             "pool": ctx.enter_context(nc.semaphore("dma_pool"))}
    for _w in ("wr", "wk", "wv", "wq", "wka", "wva", "wo", "wsm"):
        dsems["w_" + _w] = ctx.enter_context(nc.semaphore("dma_" + _w))

    A = {k: v.ap() for k, v in sb.items()}
    rotc = [0]

    def nextbank():
        b = (4, 5, 7)[rotc[0] % 3]
        rotc[0] += 1
        return b

    rotc6 = [0]

    def nextbank6():
        b = (4, 5, 7, 0, 1, 3)[rotc6[0] % 6]
        rotc6[0] += 1
        return b
    STRIDE1 = ("lnx", "lnx2")

    def fm(name, g, c0, n):
        stride = TT1 if name in STRIDE1 else TT
        return A[name][:, g * stride + c0: g * stride + c0 + n]

    def fmh(name, j, g, c0, n):
        stride = TT1 if name in STRIDE1 else TT
        return A[name][j * D:(j + 1) * D, g * stride + c0: g * stride + c0 + n]

    def vtm_keys(sub, half):
        return [("sA", 4 * sub + 2 * half), ("sA", 4 * sub + 2 * half + 1), ("vtm", sub, half)]

    def pm_keys(p_):
        return [("sEx", 2 * p_), ("sEx", 2 * p_ + 1)]

    def wload(name, src_key, rearr, q):
        src = dt[src_key]
        dst = A[name]
        if rearr:
            src = src.rearrange("(c p) d -> p c d", p=P)
            dst = dst.rearrange("p (c d) -> p c d", c=NS)
        eng_name = "sp" if q == "spc" else "pool"
        em.op(eng_name, lambda eng, s=src, d=dst: eng.dma_start(out=d, in_=s),
              wr=[(name,)], dma=q, tag="wload:" + name)

    _prev_w = [None]

    def wload_chain(name, src_key, rearr, q):
        wload(name, src_key, rearr, q)
        if _prev_w[0] is not None:
            em.ops[-1].deps.add(_prev_w[0].idx)
        _prev_w[0] = em.ops[-1]

    for w in ("wr", "wk", "wv"):
        wload_chain(w + "_s", w, True, "w_" + w)
    wload_chain("wf1_s", "wf1", True, "w_wsm")
    wload_chain("wf2_s", "wf2", False, "w_wsm")
    for w in ("wka", "wva", "wq", "wo"):
        wload_chain(w + "_s", w, True, "w_" + w)
    wload_chain("wg1_s", "wg1", True, "w_wsm")
    wload_chain("wg2_s", "wg2", False, "w_wsm")
    for v in ("mu_r", "mu_k", "nw"):
        em.op("sp", lambda eng, s=dt[v], d=A[v + "_s"]: eng.dma_start(out=d, in_=s),
              wr=[(v,)], dma="spc", tag="vload:" + v)
    for cname in ("ones1", "onesb", "gate", "mask", "ident", "gnacc", "gnsel"):
        wload(cname + "_s", cname, False, "spc")

    em.op("dve", lambda eng: eng.memset(A["Ssb"][:, :], 0.0),
          wr=[("S", g) for g in range(NS)], tag="S0")
    em.op("dve", lambda eng: eng.memset(A["st1"][:, :], 0.0), wr=[("st1", 0), ("st1", 1)], tag="z0")
    em.op("dve", lambda eng: eng.memset(A["cEps"][:, :], EPS), wr=[("cEps",)], tag="zE")
    em.op("dve", lambda eng: eng.memset(A["cOne"][:, :], 1.0), wr=[("cOne",)], tag="z1")
    em.op("dve", lambda eng: eng.memset(A["cLnS"][:, :], math.log(SCALE)), wr=[("cLnS",)], tag="zS")
    for i in (0, 1, 2, 3):
        em.op("dve", lambda eng, i=i: eng.tensor_copy(out=ps[i].ap()[:, :], in_=A["st1"][:, :]),
              rd=[("st1", 0), ("st1", 1)],
              wr=[("pb%d" % i,), ("pb%d" % i, "d0"), ("pb%d" % i, "d1")], tag="psz%d" % i)
    em.op("dve", lambda eng: eng.memset(A["sEx"][:, 0:512], 0.0),
          wr=[("sEx", 0), ("sEx", 1)], tag="zb")
    for k in range(8):
        em.op("pe", lambda eng, k=k: eng.transpose(
            ps[6].ap()[:, k * P:(k + 1) * P], A["sEx"][:, 0:P], A["ident_s"][:, :]),
            rd=[("sEx", 0), ("sEx", 1), ("ident_s",)], wr=[("pt6", k // 4)], tag="ptz%d" % k)
    ln1_3 = A["lnx"].rearrange("p (s n) -> p s n", n=TT1)
    ln2_3 = A["lnx2"].rearrange("p (s n) -> p s n", n=TT1)
    em.op("dve", lambda eng: eng.memset(ln1_3[:, :, 0:1], 0.0), wr=[("lnxc0",)], tag="c0a")
    em.op("dve", lambda eng: eng.memset(ln2_3[:, :, 0:1], 0.0), wr=[("lnx2c0",)], tag="c0b")

    xT3 = dt["xT"].rearrange("(s p) n -> p s n", p=P)
    outT3 = outT.rearrange("(s p) n -> p s n", p=P)

    lncnt = [0]

    ln_ctx = {}

    def layer_norm(src, out_name, out_c0, tag, after_slab=None, part=0,
                   cast="sD", sqb="sG"):
        if part in (0, 1):
            si_ = lncnt[0] % 2
            lncnt[0] += 1
            ln_ctx[tag] = (si_, cast)
            LM, LV, LT = "lnm", "lnv", "lnt"
            LA, LB = "lnAb%d" % si_, "lnBb%d" % si_
            for g in range(NS):
                em.op("act", lambda eng, g=g: eng.activation(
                    out=fm(cast, g, 0, TT), in_=fm(src, g, 0, TT), func=AF.Copy),
                    rd=[(src, g)], wr=[(cast, g)], tag=tag + "cast%d" % g)
            for g in range(NS):
                em.op("act", lambda eng, g=g: eng.activation(
                    out=fm(sqb, g, 0, TT), in_=fm(cast, g, 0, TT), func=AF.Square),
                    rd=[(cast, g)], wr=[(sqb, g)], tag=tag + "sq%d" % g)
            for g in range(NS):
                em.op("pe", lambda eng, g=g: eng.matmul(
                    ps[4].ap()[0:1, 0:TT], A["ones1_s"][:, :], fm(cast, g, 0, TT),
                    start=(g == 0), stop=(g == NS - 1)),
                    rd=[(cast, g), ("ones1_s",)],
                    wr=[("pb4",)] if g in (0, NS - 1) else [], tag=tag + "sum%d" % g)
            for g in range(NS):
                em.op("pe", lambda eng, g=g: eng.matmul(
                    ps[5].ap()[0:1, 0:TT], A["ones1_s"][:, :], fm(sqb, g, 0, TT),
                    start=(g == 0), stop=(g == NS - 1)),
                    rd=[(sqb, g), ("ones1_s",)],
                    wr=[("pb5",)] if g in (0, NS - 1) else [], tag=tag + "ssq%d" % g)
            em.op("dve", lambda eng: eng.tensor_scalar_mul(
                out=A[LM][:, :], in0=ps[4].ap()[0:1, 0:TT], scalar1=1.0 / E),
                rd=[("pb4",)], wr=[(LM,)], tag=tag + "mean")
            em.op("dve", lambda eng: eng.tensor_mul(
                out=A[LT][:, :], in0=A[LM][:, :], in1=A[LM][:, :]),
                rd=[(LM,)], wr=[(LT,)], tag=tag + "msq")
            em.op("dve", lambda eng: eng.scalar_tensor_tensor(
                out=A[LV][:, :], in0=ps[5].ap()[0:1, 0:TT], scalar=1.0 / E,
                in1=A[LT][:, :], op0=MULT, op1=OP.subtract),
                rd=[("pb5",), (LT,)], wr=[(LV,)], tag=tag + "var")
            em.op("act", lambda eng: eng.activation(
                out=A[LT][:, :], in_=A[LV][:, :], func=AF.Ln, bias=A["cEps"][0:1, 0:1]),
                rd=[(LV,), ("cEps",)], wr=[(LT,)], tag=tag + "log")
            em.op("act", lambda eng: eng.activation(
                out=A[LA][:, :], in_=A[LT][:, :], func=AF.Exp, scale=-0.5),
                rd=[(LT,)], wr=[(LA,)], tag=tag + "rstd")
            em.op("dve", lambda eng: eng.tensor_mul(
                out=A[LB][:, :], in0=A[LM][:, :], in1=A[LA][:, :]),
                rd=[(LM,), (LA,)], wr=[(LB,)], tag=tag + "Bb")
        if part in (0, 2):
            si_, cast = ln_ctx[tag]
            LA, LB = "lnAb%d" % si_, "lnBb%d" % si_
            abo = si_ * 512
            em.op("pe", lambda eng: eng.matmul(
                ps[7].ap()[0:P, 0:TT], A["onesb_s"][:, :], A[LA][:, :], start=True, stop=True),
                rd=[(LA,), ("onesb_s",)], wr=[("pb7",)], tag=tag + "bcA")
            em.op("pe", lambda eng: eng.matmul(
                ps[2].ap()[0:P, 0:TT], A["onesb_s"][:, :], A[LB][:, :], start=True, stop=True),
                rd=[(LB,), ("onesb_s",)],
                wr=[("pb2",), ("pb2", "d0"), ("pb2", "d1")], tag=tag + "bcB")
            em.op("act", lambda eng: eng.activation(
                out=A["lnab"][:, abo:abo + TT], in_=ps[7].ap()[0:P, 0:TT], func=AF.Copy),
                rd=[("pb7",)], wr=[("lnabA", si_)], tag=tag + "evA")
            em.op("act", lambda eng: eng.activation(
                out=A["lnab"][:, abo + TT:abo + 2 * TT], in_=ps[2].ap()[0:P, 0:TT], func=AF.Copy),
                rd=[("pb2",)], wr=[("lnabB", si_)], tag=tag + "evB")
            for g in range(NS):
                em.op("dve", lambda eng, g=g: eng.tensor_mul(
                    out=fm(cast, g, 0, TT), in0=fm(cast, g, 0, TT), in1=A["lnab"][:, abo:abo + TT]),
                    rd=[(cast, g), ("lnabA", si_)], wr=[(cast, g)], tag=tag + "ap1_%d" % g)
                em.op("dve", lambda eng, g=g: eng.tensor_sub(
                    out=fm(out_name, g, out_c0, TT), in0=fm(cast, g, 0, TT),
                    in1=A["lnab"][:, abo + TT:abo + 2 * TT]),
                    rd=[(cast, g), ("lnabB", si_)], wr=[(out_name, g)], tag=tag + "ap2_%d" % g)
                if after_slab is not None:
                    after_slab(g)

    def ffn_mixes(src_ln, c0key, tag, bufR, bufK):
        xkeys = [("vtm", 0, 0), ("vtm", 0, 1), ("vtm", 1, 0), ("vtm", 1, 1)] if bufR == "sA" else []

        def mix(g):
            em.op("dve", lambda eng, g=g: eng.tensor_sub(
                out=fm("sG", g, 0, TT), in0=fm(src_ln, g, 1, TT), in1=fm(src_ln, g, 0, TT)),
                rd=[(src_ln, g), (c0key,)], wr=[("sG", g)], tag=tag + "d%d" % g)
            em.op("dve", lambda eng, g=g: eng.scalar_tensor_tensor(
                out=fm(bufR, g, 0, TT), in0=fm("sG", g, 0, TT),
                scalar=A["mu_r_s"][:, g:g + 1], in1=fm(src_ln, g, 0, TT), op0=MULT, op1=ADD),
                rd=[("sG", g), (src_ln, g), (c0key,), ("mu_r",)],
                wr=[(bufR, g)] + ([("vtm", g // 4, (g % 4) // 2)] if bufR == "sA" else []),
                tag=tag + "xr%d" % g)
            em.op("dve", lambda eng, g=g: eng.scalar_tensor_tensor(
                out=fm(bufK, g, 0, TT), in0=fm("sG", g, 0, TT),
                scalar=A["mu_k_s"][:, g:g + 1], in1=fm(src_ln, g, 0, TT), op0=MULT, op1=ADD),
                rd=[("sG", g), (src_ln, g), (c0key,), ("mu_k",)],
                wr=[(bufK, g)], tag=tag + "xk%d" % g)
        return mix

    def _proj(wname, rhs_name, out_cb, tag2):
        banks = [nextbank6() for _ in range(4)]
        for j in range(2):
            for c in range(NS):
                for bp in range(4):
                    bank = banks[bp]
                    do = 2 * bp + j
                    em.op("pe", lambda eng, bank=bank, j=j, do=do, c=c: eng.matmul(
                        ps[bank].ap()[:, j * TT:j * TT + TT],
                        A[wname][:, c * E + do * P: c * E + do * P + P],
                        fm(rhs_name, c, 0, TT),
                        start=(c == 0), stop=(c == NS - 1)),
                        rd=[(rhs_name, c), (wname,)],
                        wr=[("pb%d" % bank,)] if ((c == 0 and j == 0) or (c == NS - 1 and j == 1)) else [],
                        tag=tag2 + "%d_%d_%d" % (bp, j, c))
        for bp in range(4):
            out_cb(bp, banks[bp])

    def ffn_a(tag, bufR="sA", bufK="sB", rbuf="sH", kbuf="sC", k_first=False):
        def r_out(bp, bank):
            em.op("act", lambda eng, bp=bp, bank=bank: eng.activation(
                out=fm(rbuf, 2 * bp, 0, 2 * TT), in_=ps[bank].ap()[:, 0:2 * TT], func=AF.Sigmoid),
                rd=[("pb%d" % bank,)], wr=[(rbuf, 2 * bp), (rbuf, 2 * bp + 1)], tag=tag + "rs%d" % bp)

        def k_out(bp, bank):
            em.op("act", lambda eng, bp=bp, bank=bank: eng.activation(
                out=fm(kbuf, 2 * bp, 0, 2 * TT), in_=ps[bank].ap()[:, 0:2 * TT], func=AF.Relu),
                rd=[("pb%d" % bank,)], wr=[(kbuf, 2 * bp), (kbuf, 2 * bp + 1)], tag=tag + "kr%d" % bp)
        if k_first:
            _proj("wk_s", bufK, k_out, tag + "k")
            _proj("wr_s", bufR, r_out, tag + "r")
        else:
            _proj("wr_s", bufR, r_out, tag + "r")
            _proj("wk_s", bufK, k_out, tag + "k")

    def ffn_b(dst_name, resid_name, tag, rbuf="sH", kbuf="sC", ts_="st1"):
        em.op("dve", lambda eng: eng.tensor_mul(
            out=A[kbuf][:, 0:NS * TT], in0=A[kbuf][:, 0:NS * TT], in1=A[kbuf][:, 0:NS * TT]),
            rd=[(kbuf, g) for g in range(NS)], wr=[(kbuf, g) for g in range(NS)],
            tag=tag + "k2")

        def v_out(bp, bank):
            for j in range(2):
                do = 2 * bp + j
                em.op("dve", lambda eng, j=j, bank=bank, do=do: eng.tensor_mul(
                    out=A[ts_][:, 0:TT], in0=ps[bank].ap()[:, j * TT:j * TT + TT],
                    in1=fm(rbuf, do, 0, TT)),
                    rd=[("pb%d" % bank,), (rbuf, do)], wr=[(ts_, 0)], tag=tag + "vm%d_%d" % (bp, j))
                em.op("dve", lambda eng, do=do: eng.scalar_tensor_tensor(
                    out=fm(dst_name, do, 0, TT), in0=A[ts_][:, 0:TT], scalar=0.5,
                    in1=fm(resid_name, do, 0, TT), op0=MULT, op1=ADD),
                    rd=[(ts_, 0), (resid_name, do)], wr=[(dst_name, do)], tag=tag + "vo%d_%d" % (bp, j))
        _proj("wv_s", kbuf, v_out, tag + "v")

    def tile_ctx(t):
        xio = "xio0" if t % 2 == 0 else "xio1"
        return xio, t * TT, A[xio].rearrange("p (s n) -> p s n", n=TT)

    def emit_tile_front_a(t, mode):
        xio, base, xio3 = tile_ctx(t)
        em.op("sp", lambda eng, b=base, d=xio3: eng.dma_start(
            out=d[:, :, :], in_=xT3[:, :, b:b + TT]),
            wr=[(xio, g) for g in range(NS)], dma="sp", tag="t%dxin" % t)
        if t > 0:
            em.op("dve", lambda eng: eng.tensor_copy(
                out=ln1_3[:, :, 0:1], in_=ln1_3[:, :, TT:TT1]),
                rd=[("lnx", g) for g in range(NS)], wr=[("lnxc0",)], tag="t%dc0a" % t)
        layer_norm(xio, "lnx", 1, "t%dln1_" % t, part=1)

    def emit_tile_front_b(t, mode):
        layer_norm(None, "lnx", 1, "t%dln1_" % t, part=2,
                   after_slab=ffn_mixes("lnx", "lnxc0", "t%df1" % t, "sA", "sB"))

    def emit_tile_front2a(t, mode):
        ffn_a("t%df1" % t, bufR="sA", bufK="sB", rbuf="sB", kbuf="sD", k_first=True)

    def emit_tile_front2b(t, mode):
        xio, base, xio3 = tile_ctx(t)
        ffn_b(xio, xio, "t%df1" % t, rbuf="sB", kbuf="sD", ts_="st1")
        layer_norm(xio, "lnx", 0, "t%dln2_" % t, part=1)

    def emit_tile_front2c(t, mode):
        layer_norm(None, "lnx", 0, "t%dln2_" % t, part=2)

    def emit_tile_mid(t, mode):
        full = mode != "pre"
        xio, base, xio3 = tile_ctx(t)
        LN1 = "lnx"
        for g in range(NS):
            em.op("pe", lambda eng, g=g: eng.matmul(
                ps[7].ap()[0:D, 0:TT], A["wf1_s"][:, g * D:(g + 1) * D], fm(LN1, g, 0, TT),
                start=(g == 0), stop=(g == NS - 1)),
                rd=[(LN1, g), ("wf1_s",)],
                wr=[("pb7",)] if g in (0, NS - 1) else [], tag="t%dfm%d" % (t, g))
        em.op("act", lambda eng: eng.activation(
            out=A["mid2"][0:D, 0:TT], in_=ps[7].ap()[0:D, 0:TT], func=AF.Copy),
            rd=[("pb7",)], wr=[("mid2f",)], tag="t%dfmc" % t)
        if full:
            for g in range(NS):
                em.op("pe", lambda eng, g=g: eng.matmul(
                    ps[7].ap()[0:D, 256:256 + TT], A["wg1_s"][:, g * D:(g + 1) * D], fm(LN1, g, 0, TT),
                    start=(g == 0), stop=(g == NS - 1)),
                    rd=[(LN1, g), ("wg1_s",)],
                    wr=[("pb7",)] if g in (0, NS - 1) else [], tag="t%dgm%d" % (t, g))
            em.op("act", lambda eng: eng.activation(
                out=A["mid2"][0:D, TT:2 * TT], in_=ps[7].ap()[0:D, 256:256 + TT], func=AF.Copy),
                rd=[("pb7",)], wr=[("mid2g",)], tag="t%dgmc" % t)

        def aproj(wname, rhs_name, out_cb, tag2):
            for bp in range(4):
                bank = nextbank()
                for j in range(2):
                    do = 2 * bp + j
                    for c in range(NS):
                        em.op("pe", lambda eng, bank=bank, j=j, do=do, c=c: eng.matmul(
                            ps[bank].ap()[:, j * TT:j * TT + TT],
                            A[wname][:, c * E + do * P: c * E + do * P + P],
                            fm(rhs_name, c, 0, TT),
                            start=(c == 0), stop=(c == NS - 1)),
                            rd=[(rhs_name, c), (wname,)],
                            wr=[("pb%d" % bank,)] if ((c == 0 and j == 0) or (c == NS - 1 and j == 1)) else [],
                            tag=tag2 + "%d_%d_%d" % (bp, j, c))
                out_cb(bp, bank)

        def q_out(bp, bank):
            em.op("act", lambda eng, bp=bp, bank=bank: eng.activation(
                out=fm("sC", 2 * bp, 0, 2 * TT), in_=ps[bank].ap()[:, 0:2 * TT], func=AF.Silu),
                rd=[("pb%d" % bank,)], wr=[("sC", 2 * bp), ("sC", 2 * bp + 1)],
                tag="t%dqs%d" % (t, bp))
        if full:
            aproj("wq_s", LN1, q_out, "t%dq" % t)

        def ka_out(bp, bank):
            em.op("act", lambda eng, bp=bp, bank=bank: eng.activation(
                out=fm("sD", 2 * bp, 0, 2 * TT), in_=ps[bank].ap()[:, 0:2 * TT], func=AF.Silu),
                rd=[("pb%d" % bank,)], wr=[("sD", 2 * bp), ("sD", 2 * bp + 1)],
                tag="t%dks%d" % (t, bp))
        aproj("wka_s", LN1, ka_out, "t%dk" % t)

        for bp in range(4):
            bank = nextbank()
            for j in range(2):
                do = 2 * bp + j
                em.op("pe", lambda eng, bank=bank, j=j, do=do: eng.matmul(
                    ps[bank].ap()[:, j * TT:j * TT + TT],
                    A["wf2_s"][0:D, do * P:do * P + P], A["mid2"][0:D, 0:TT],
                    start=True, stop=True),
                    rd=[("mid2f",), ("wf2_s",)], wr=[("pb%d" % bank,)],
                    tag="t%dfz%d_%d" % (t, bp, j))
            em.op("act", lambda eng, bank=bank, bp=bp: eng.activation(
                out=fm("sH", 2 * bp, 0, 2 * TT), in_=ps[bank].ap()[:, 0:512], func=AF.Copy),
                rd=[("pb%d" % bank,)], wr=[("sH", 2 * bp), ("sH", 2 * bp + 1)], tag="t%dfze%d" % (t, bp))
        for bp in (range(4) if full else ()):
            bank = nextbank()
            for j in range(2):
                do = 2 * bp + j
                em.op("pe", lambda eng, bank=bank, j=j, do=do: eng.matmul(
                    ps[bank].ap()[:, j * TT:j * TT + TT],
                    A["wg2_s"][0:D, do * P:do * P + P], A["mid2"][0:D, TT:2 * TT],
                    start=True, stop=True),
                    rd=[("mid2g",), ("wg2_s",)], wr=[("pb%d" % bank,)],
                    tag="t%dgz%d_%d" % (t, bp, j))
            em.op("act", lambda eng, bank=bank, bp=bp: eng.activation(
                out=fm("sB", 2 * bp, 0, 2 * TT), in_=ps[bank].ap()[:, 0:512], func=AF.Sigmoid),
                rd=[("pb%d" % bank,)], wr=[("sB", 2 * bp), ("sB", 2 * bp + 1)],
                tag="t%dgs%d" % (t, bp))

        for bp in (range(4) if full else ()):
            bank = nextbank()
            for j in range(2):
                do = 2 * bp + j
                em.op("pe", lambda eng, bank=bank, j=j, do=do: eng.matmul(
                    ps[bank].ap()[:, j * TT:j * TT + TT],
                    A["wg2_s"][0:D, do * P:do * P + P], A["mid2"][0:D, TT:2 * TT],
                    start=True, stop=True),
                    rd=[("mid2g",), ("wg2_s",)], wr=[("pb%d" % bank,)],
                    tag="t%dgz%d_%d" % (t, bp, j))
            em.op("act", lambda eng, bank=bank, bp=bp: eng.activation(
                out=fm("sB", 2 * bp, 0, 2 * TT), in_=ps[bank].ap()[:, 0:512], func=AF.Sigmoid),
                rd=[("pb%d" % bank,)], wr=[("sB", 2 * bp), ("sB", 2 * bp + 1)],
                tag="t%dgs%d" % (t, bp))

        def emit_kp(bp):
            for g in (2 * bp, 2 * bp + 1):
                p_ = g % 2
                sidx = (bp % 2) * 2 + p_
                for c in range(C):
                    em.op("dve", lambda eng, g=g, c=c, sidx=sidx: eng.tensor_scalar_mul(
                        out=A["kp"][:, sidx * 256 + c * CH: sidx * 256 + c * CH + CH],
                        in0=fm("sD", g, c * CH, CH),
                        scalar1=A["lam"][:, g * C + c: g * C + c + 1]),
                        rd=[("sD", g), ("lam",)], wr=[("kp", sidx)], tag="t%dKp%d_%d" % (t, g, c))


        def emit_chain_pair(bp):
                for j in range(2):
                    g = 2 * bp + j
                    em.op("act", lambda eng, g=g, j=j: eng.activation(
                        out=A["st1"][:, j * TT:j * TT + TT], in_=fm("sH", g, 0, TT),
                        func=AF.Exp, scale=-1.0),
                        rd=[("sH", g)], wr=[("st1", j)], tag="t%dfe%d_%d" % (t, bp, j))
                    em.op("act", lambda eng, j=j: eng.activation(
                        out=A["st2"][:, j * TT:j * TT + TT], in_=A["st1"][:, j * TT:j * TT + TT],
                        func=AF.Ln, bias=A["cOne"][:, 0:1]),
                        rd=[("st1", j), ("cOne",)], wr=[("st2", j)], tag="t%dfl%d_%d" % (t, bp, j))
                    em.op("dve", lambda eng, g=g, j=j: eng.tensor_tensor_scan(
                        out=A["st1"][:, j * TT:j * TT + TT], data0=A["gate_s"][:, 0:TT],
                        data1=A["st2"][:, j * TT:j * TT + TT], initial=0.0, op0=MULT, op1=ADD),
                        rd=[("st2", j), ("gate_s",)], wr=[("st1", j)], tag="t%dsc%d" % (t, g))
                pr = [("sEx", 0, 2), ("sEx", 0, 3)]
                if full:
                    for j in range(2):
                        em.op("act", lambda eng, j=j: eng.activation(
                            out=A["sEx"][:, 512 + j * TT:512 + j * TT + TT],
                            in_=A["st1"][:, j * TT:j * TT + TT],
                            func=AF.Exp, scale=-1.0, bias=A["cLnS"][:, 0:1]),
                            rd=[("st1", j), ("cLnS",)], wr=[("sEx", 0, 2 + j)],
                            tag="t%dEp%d_%d" % (t, bp, j))
                    em.op("dve", lambda eng, bp=bp: eng.tensor_mul(
                        out=fm("sC", 2 * bp, 0, 2 * TT), in0=fm("sC", 2 * bp, 0, 2 * TT),
                        in1=A["sEx"][:, 512:1024]),
                        rd=[("sC", 2 * bp), ("sC", 2 * bp + 1)] + pr,
                        wr=[("sC", 2 * bp), ("sC", 2 * bp + 1)], tag="t%dQh%d" % (t, bp))
                em.op("act", lambda eng, bp=bp: eng.activation(
                    out=A["sEx"][:, 512:1024], in_=A["st1"][:, 0:512], func=AF.Exp),
                    rd=[("st1", 0), ("st1", 1)], wr=pr, tag="t%dEm%d" % (t, bp))
                em.op("dve", lambda eng, bp=bp: eng.tensor_mul(
                    out=fm("sD", 2 * bp, 0, 2 * TT), in0=fm("sD", 2 * bp, 0, 2 * TT),
                    in1=A["sEx"][:, 512:1024]),
                    rd=[("sD", 2 * bp), ("sD", 2 * bp + 1)] + pr,
                    wr=[("sD", 2 * bp), ("sD", 2 * bp + 1)], tag="t%dKh%d" % (t, bp))
                st1v = A["st1"].rearrange("p (s c l) -> p s c l", s=2, c=C)
                lam3 = A["lam"].rearrange("p (g c) -> p g c", g=NS)
                em.op("act", lambda eng, bp=bp, sv=st1v, l3=lam3: eng.activation(
                    out=l3[:, 2 * bp:2 * bp + 2, :], in_=sv[:, :, :, CH - 1],
                    func=AF.Exp, scale=-1.0),
                    rd=[("st1", 0), ("st1", 1)], wr=[("lam",)], tag="t%dlam%d" % (t, bp))
                if bp < 2:
                    emit_kp(bp)

        def emit_vtm_fill(sub, half):
            bank = nextbank()
            for c in range(NS):
                em.op("pe", lambda eng, bank=bank, sub=sub, half=half, c=c: eng.matmul(
                    ps[bank].ap()[:, 0:512], fm(LN1, c, sub * P, P),
                    A["wva_s"][:, c * E + half * 512: c * E + half * 512 + 512],
                    start=(c == 0), stop=(c == NS - 1)),
                    rd=[(LN1, c), ("wva_s",)],
                    wr=[("pb%d" % bank,)] if c in (0, NS - 1) else [],
                    tag="t%dvt%d_%d_%d" % (t, sub, half, c))
            em.op("dve", lambda eng, bank=bank, sub=sub, half=half: eng.tensor_copy(
                out=A["sA"][:, sub * E + half * 512: sub * E + half * 512 + 512],
                in_=ps[bank].ap()[:, 0:512]),
                rd=[("pb%d" % bank,)], wr=vtm_keys(sub, half),
                tag="t%dve%d_%d" % (t, sub, half))

        for p4 in range(4):
            emit_chain_pair(p4)
            emit_vtm_fill(p4 // 2, p4 % 2)

        slot0 = (4 * t) % 5

        late = []

        def scan_A(gp):
            pair = (2 * gp, 2 * gp + 1)
            kb = (gp % 2) * 1024
            for g in pair:
                b0 = 4 * (g % 2)
                for c in (range(C) if full else ()):
                    for j in range(2):
                        em.op("pe", lambda eng, g=g, c=c, j=j, b0=b0: eng.matmul(
                            ps[b0 + j].ap()[D * (c % 2):D * (c % 2) + D, c * CH: c * CH + CH],
                            fmh("sD", j, g, c * CH, CH), fmh("sC", j, g, c * CH, CH),
                            start=True, stop=True, tile_position=(D * j, D * (c % 2))),
                            rd=[("sD", g), ("sC", g)], wr=[("pb%d" % (b0 + j),)],
                            tag="t%dP%d_%d_%d" % (t, g, c, j))
            eb = (gp % 2) * 1024
            for g in (pair if full else ()):
                p_ = g % 2
                b0 = 4 * p_
                for j in range(2):
                    em.op("dve", lambda eng, j=j, b0=b0, p_=p_, eb=eb: eng.tensor_mul(
                        out=A["sEx"][:, eb + p_ * 512 + j * 256: eb + p_ * 512 + j * 256 + 256],
                        in0=ps[b0 + j].ap()[:, 0:256], in1=A["mask_s"][:, 0:256]),
                        rd=[("pb%d" % (b0 + j),), ("mask_s",)], wr=[("sEx", gp % 2, 2 * p_ + j)],
                        tag="t%dPM%d_%d" % (t, g, j))
            for g in pair:
                p_ = g % 2
                sidx = (gp % 2) * 2 + p_
                for c in range(C):
                    em.op("pe", lambda eng, g=g, c=c, p_=p_, sidx=sidx: eng.transpose(
                        ps[6].ap()[D * (c % 2):D * (c % 2) + D, p_ * 512 + c * P: p_ * 512 + c * P + P],
                        A["kp"][:, sidx * 256 + c * CH: sidx * 256 + c * CH + CH], A["ident_s"][:, :],
                        tile_position=(0, D * (c % 2))),
                        rd=[("kp", sidx), ("ident_s",)], wr=[("pt6", p_)], tag="t%dT%d_%d" % (t, g, c))
            for p_ in range(2):
                em.op("dve", lambda eng, p_=p_, kb=kb: eng.tensor_copy(
                    out=A["kt"][:, kb + p_ * 512: kb + p_ * 512 + 512],
                    in_=ps[6].ap()[:, p_ * 512: p_ * 512 + 512]),
                    rd=[("pt6", 0), ("pt6", 1)], wr=[("kt", gp % 2, p_)],
                    tag="t%dTe%d_%d" % (t, gp, p_))

        def scan_B(gp):
            pair = (2 * gp, 2 * gp + 1)
            kb = (gp % 2) * 1024
            eb = (gp % 2) * 1024
            for g in pair:
                p_ = g % 2
                for c in range(C):
                    for j in range(2):
                        hh = 2 * g + j
                        em.op("pe", lambda eng, g=g, c=c, j=j, p_=p_, hh=hh, kb=kb: eng.matmul(
                            ps[2 + (c % 2)].ap()[D * j:D * j + D,
                                                 p_ * 128 + (c // 2) * CH: p_ * 128 + (c // 2) * CH + CH],
                            A["kt"][D * (c % 2):D * (c % 2) + D,
                                    kb + p_ * 512 + c * P + j * D: kb + p_ * 512 + c * P + j * D + D],
                            A["sA"][D * (c % 2):D * (c % 2) + D,
                                    (c // 2) * E + hh * D: (c // 2) * E + hh * D + D],
                            start=True, stop=True, tile_position=(D * (c % 2), D * j)),
                            rd=[("kt", gp % 2, p_), ("vtm", c // 2, hh // 8)],
                            wr=[("pb%d" % (2 + c % 2),), ("pb%d" % (2 + c % 2), "d0"), ("pb%d" % (2 + c % 2), "d1")],
                            tag="t%dD%d_%d_%d" % (t, g, c, j))
            for g in pair:
                p_ = g % 2
                for c in range(C):
                    si = (slot0 + c) % 5
                    so = (slot0 + c + 1) % 5
                    em.op("dve", lambda eng, g=g, c=c, si=si, so=so, p_=p_: eng.scalar_tensor_tensor(
                        out=A["Ssb"][:, (g * 5 + so) * D: (g * 5 + so) * D + D],
                        in0=A["Ssb"][:, (g * 5 + si) * D: (g * 5 + si) * D + D],
                        scalar=A["lam"][:, g * C + c: g * C + c + 1],
                        in1=ps[2 + (c % 2)].ap()[:, p_ * 128 + (c // 2) * CH: p_ * 128 + (c // 2) * CH + CH],
                        op0=MULT, op1=ADD),
                        rd=[("S", g), ("lam",), ("pb%d" % (2 + c % 2), "d%d" % p_)], wr=[("S", g)],
                        tag="t%dSu%d_%d" % (t, g, c))
            if gp < 2:
                late.append(lambda gp=gp: emit_kp(gp + 2))
            for g in (pair if full else ()):
                p_ = g % 2
                for c in range(C):
                    for j in range(2):
                        hh = 2 * g + j
                        em.op("pe", lambda eng, g=g, c=c, j=j, p_=p_, hh=hh: eng.matmul(
                            ps[c % 2].ap()[D * j:D * j + D,
                                           256 + p_ * 128 + (c // 2) * CH: 256 + p_ * 128 + (c // 2) * CH + CH],
                            A["sA"][D * (c % 2):D * (c % 2) + D,
                                    (c // 2) * E + hh * D: (c // 2) * E + hh * D + D],
                            A["sEx"][D * (c % 2):D * (c % 2) + D,
                                     eb + p_ * 512 + j * 256 + c * CH: eb + p_ * 512 + j * 256 + c * CH + CH],
                            start=True, stop=True, tile_position=(D * (c % 2), D * j)),
                            rd=[("vtm", c // 2, hh // 8), ("sEx", gp % 2, 2 * p_ + j)],
                            wr=[("pb%d" % (c % 2),)], tag="t%dOI%d_%d_%d" % (t, g, c, j))
            for g in (pair if full else ()):
                p_ = g % 2
                ob = 2 + 2 * p_
                for c in range(C):
                    for j in range(2):
                        si = (slot0 + c) % 5
                        em.op("pe", lambda eng, g=g, c=c, j=j, si=si, ob=ob: eng.matmul(
                            ps[ob + j].ap()[D * j:D * j + D, 256 + c * CH: 256 + c * CH + CH],
                            A["Ssb"][D * j:D * j + D, (g * 5 + si) * D: (g * 5 + si) * D + D],
                            fmh("sC", j, g, c * CH, CH),
                            start=True, stop=True, tile_position=(D * j, D * j)),
                            rd=[("S", g), ("sC", g)],
                            wr=[("pb%d" % (ob + j),), ("pb%d" % (ob + j), "d0"),
                                ("pb%d" % (ob + j), "d1")],
                            tag="t%dOC%d_%d_%d" % (t, g, c, j))
            if full:
                def oie(g):
                    p_ = g % 2
                    sf3 = fm("sF", g, 0, TT).rearrange("p (a l) -> p a l", l=CH)
                    for p2 in range(2):
                        em.op("act", lambda eng, p2=p2, p_=p_, o3=sf3: eng.activation(
                            out=o3[:, p2::2, :],
                            in_=ps[p2].ap()[:, 256 + p_ * 128: 256 + p_ * 128 + 128], func=AF.Copy),
                            rd=[("pb%d" % p2,)], wr=[("sF", g)], tag="t%doie%d_%d" % (t, g, p2))

                def ocv(g):
                    ob = 2 + 2 * (g % 2)
                    for j in range(2):
                        em.op("act", lambda eng, g=g, j=j, ob=ob: eng.activation(
                            out=fmh("sG", j, g, 0, TT), in_=ps[ob + j].ap()[D * j:D * j + D, 256:512],
                            func=AF.Copy),
                            rd=[("pb%d" % (ob + j),)], wr=[("sG", g)], tag="t%doc%d_%d" % (t, g, j))

                def omg(g):
                    em.op("dve", lambda eng, g=g: eng.tensor_add(
                        out=fm("sF", g, 0, TT), in0=fm("sF", g, 0, TT), in1=fm("sG", g, 0, TT)),
                        rd=[("sF", g), ("sG", g)], wr=[("sF", g)], tag="t%dom%d" % (t, g))
                    em.op("dve", lambda eng, g=g: eng.tensor_mul(
                        out=fm("sF", g, 0, TT), in0=fm("sF", g, 0, TT), in1=fm("sB", g, 0, TT)),
                        rd=[("sF", g), ("sB", g)], wr=[("sF", g)], tag="t%dog%d" % (t, g))
                g0, g1 = pair
                oie(g0)
                oie(g1)
                ocv(g0)
                ocv(g1)
                late.append(lambda g0=g0: omg(g0))
                late.append(lambda g1=g1: omg(g1))

        scan_A(0)
        scan_A(1)
        scan_B(0)
        for th in late:
            th()
        late = []
        scan_A(2)
        scan_B(1)
        for th in late:
            th()
        late = []
        scan_A(3)
        scan_B(2)
        for th in late:
            th()
        late = []
        scan_B(3)

        for th in late:
            th()
        late = []

    def emit_tile_back_a(t, mode):
        full = mode != "pre"
        xio, base, xio3 = tile_ctx(t)
        if not full:
            return
        for g in range(NS):
            em.op("act", lambda eng, g=g: eng.activation(
                out=fm("sG", g, 0, TT), in_=fm("sF", g, 0, TT), func=AF.Square),
                rd=[("sF", g)], wr=[("sG", g)], tag="t%dq2_%d" % (t, g))
        for g in range(NS):
            em.op("pe", lambda eng, g=g: eng.matmul(
                ps[4].ap()[0:16, 0:TT], A["gnacc_s"][:, g * 16:(g + 1) * 16],
                fm("sG", g, 0, TT), start=(g == 0), stop=(g == NS - 1)),
                rd=[("sG", g), ("gnacc_s",)],
                wr=[("pb4",)] if g in (0, NS - 1) else [], tag="t%dgn%d" % (t, g))
        em.op("act", lambda eng: eng.activation(
            out=A["gns"][:, :], in_=ps[4].ap()[0:16, 0:TT], func=AF.Ln,
            scale=1.0 / D, bias=A["cEps"][0:16, 0:1]),
            rd=[("pb4",), ("cEps",)], wr=[("gns",)], tag="t%dgl" % t)
        em.op("act", lambda eng: eng.activation(
            out=A["gnsb"][:, :], in_=A["gns"][:, :], func=AF.Exp, scale=-0.5),
            rd=[("gns",)], wr=[("gnsb",)], tag="t%dgr" % t)
        for g in range(NS):
            bank = 7 if g % 2 == 0 else 2
            em.op("pe", lambda eng, g=g, bank=bank: eng.matmul(
                ps[bank].ap()[0:P, 0:TT], A["gnsel_s"][0:16, g * P:(g + 1) * P],
                A["gnsb"][0:16, 0:TT], start=True, stop=True),
                rd=[("gnsb",), ("gnsel_s",)],
                wr=[("pb%d" % bank,), ("pb%d" % bank, "d0"), ("pb%d" % bank, "d1")],
                tag="t%dgb%d" % (t, g))
            em.op("dve", lambda eng, g=g, bank=bank: eng.scalar_tensor_tensor(
                out=fm("sF", g, 0, TT), in0=fm("sF", g, 0, TT),
                scalar=A["nw_s"][:, g:g + 1], in1=ps[bank].ap()[0:P, 0:TT],
                op0=MULT, op1=MULT),
                rd=[("sF", g), ("nw",), ("pb%d" % bank,)], wr=[("sF", g)],
                tag="t%dga%d" % (t, g))
        for bp in range(4):
            bank = (4, 5, 0, 1)[bp]
            for j in range(2):
                do = 2 * bp + j
                for c in range(NS):
                    em.op("pe", lambda eng, bank=bank, j=j, do=do, c=c: eng.matmul(
                        ps[bank].ap()[:, j * TT:j * TT + TT],
                        A["wo_s"][:, c * E + do * P: c * E + do * P + P],
                        fm("sF", c, 0, TT), start=(c == 0), stop=(c == NS - 1)),
                        rd=[("sF", c), ("wo_s",)],
                        wr=[("pb%d" % bank,)] if ((c == 0 and j == 0) or (c == NS - 1 and j == 1)) else [],
                        tag="t%dwo%d_%d_%d" % (t, bp, j, c))
            for j in range(2):
                do = 2 * bp + j
                em.op("dve", lambda eng, bank=bank, j=j, do=do, xio=xio: eng.tensor_add(
                    out=fm(xio, do, 0, TT), in0=fm(xio, do, 0, TT),
                    in1=ps[bank].ap()[:, j * TT:j * TT + TT]),
                    rd=[("pb%d" % bank,), (xio, do)], wr=[(xio, do)],
                    tag="t%dx2_%d_%d" % (t, bp, j))
        if t > 0:
            em.op("dve", lambda eng: eng.tensor_copy(
                out=ln2_3[:, :, 0:1], in_=ln2_3[:, :, TT:TT1]),
                rd=[("lnx2", g) for g in range(NS)], wr=[("lnx2c0",)], tag="t%dc0b" % t)
        layer_norm(xio, "lnx2", 1, "t%dln3_" % t, part=1, cast="sH", sqb="sG")

    def emit_tile_back_b(t, mode):
        if mode == "pre":
            return
        layer_norm(None, "lnx2", 1, "t%dln3_" % t, part=2,
                   after_slab=ffn_mixes("lnx2", "lnx2c0", "t%df3" % t, "sC", "sF"))

    def emit_tile_back_c(t, mode):
        full = mode != "pre"
        xio, base, xio3 = tile_ctx(t)
        if full:
            ffn_a("t%df3" % t, bufR="sC", bufK="sF", rbuf="sH", kbuf="sC", k_first=False)
            ffn_b(xio, xio, "t%df3" % t, rbuf="sH", kbuf="sC", ts_="st2")
        if mode == "own":
            ob = base - (NTILE // 2) * TT
            em.op("sp", lambda eng, b=ob, s=xio3: eng.dma_start(
                out=outT3[:, :, b:b + TT], in_=s[:, :, :]),
                rd=[(xio, g) for g in range(NS)], dma="sp", tag="t%dout" % t)

    NPRE = NTILE // 2

    def mode_of(t):
        return "pre" if t < NPRE - 1 else ("pre_last" if t == NPRE - 1 else "own")

    emit_tile_front_a(0, mode_of(0))
    emit_tile_front_b(0, mode_of(0))
    emit_tile_front2a(0, mode_of(0))
    emit_tile_front2b(0, mode_of(0))
    emit_tile_front2c(0, mode_of(0))
    for t in range(NTILE):
        m1 = mode_of(t + 1) if t + 1 < NTILE else None
        emit_tile_mid(t, mode_of(t))
        if m1:
            emit_tile_front_a(t + 1, m1)
        emit_tile_back_a(t, mode_of(t))
        if m1:
            emit_tile_front_b(t + 1, m1)
            emit_tile_front2a(t + 1, m1)
        if m1:
            emit_tile_front2b(t + 1, m1)
        emit_tile_back_b(t, mode_of(t))
        emit_tile_back_c(t, mode_of(t))
        if m1:
            emit_tile_front2c(t + 1, m1)

    if limit is not None:
        import sys
        for i, o in enumerate(em.ops):
            if i % 50 == 0:
                print("op[%d] %s %s" % (i, o.eng, o.tag), file=sys.stderr)
        em.ops = em.ops[:limit]
    em.finalize(sems, dsems)
    with nc.Block() as block:
        @block.sync
        def _(eng):
            em.emit("sp", eng)
            for q in sorted(dsems):
                if em.dcounts.get(q):
                    eng.wait_ge(dsems[q], em.dcounts[q])

        @block.tensor
        def _(eng):
            em.emit("pe", eng)

        @block.vector
        def _(eng):
            em.emit("dve", eng)

        @block.scalar
        def _(eng):
            em.emit("act", eng)

        @block.gpsimd
        def _(eng):
            em.emit("pool", eng)

    ctx.close()
    return nc


def _consts(TT):
    bf = ml_dtypes.bfloat16
    ident = np.eye(P, dtype=np.float32).astype(bf)
    col = np.arange(256) % CH
    row = np.arange(P) % CH
    mask = (row[:, None] <= col[None, :]).astype(np.float32).astype(bf)
    gate = np.ascontiguousarray(np.ones((P, 1), np.float32) * ((np.arange(TT) % CH) != 0))
    gate = gate.astype(bf)
    ones1 = np.ones((P, 1), np.float32).astype(bf)
    onesb = np.ones((1, P), np.float32).astype(bf)
    gnacc = np.zeros((P, P), np.float32)
    for g in range(NS):
        for p in range(P):
            gnacc[p, g * 16 + 2 * g + p // D] = 1
    gnsel = np.zeros((16, NS * P), np.float32)
    for g in range(NS):
        for p in range(P):
            gnsel[2 * g + p // D, g * P + p] = 1
    return dict(ident=ident, mask=mask, gate=gate, ones1=ones1, onesb=onesb,
                gnacc=gnacc.astype(bf), gnsel=gnsel.astype(bf))


def _wmaps(inputs):
    def c(x):
        return np.ascontiguousarray(np.asarray(x, dtype=np.float32))
    return {
        "wr": c(np.asarray(inputs["L0_ffn_W_r"]).T),
        "wk": c(np.asarray(inputs["L0_ffn_W_k"]).T),
        "wv": c(np.asarray(inputs["L0_ffn_W_v"]).T),
        "wq": c(inputs["L0_attn_Wq"]),
        "wka": c(inputs["L0_attn_Wk"]),
        "wva": c(inputs["L0_attn_Wv"]),
        "wo": c(inputs["L0_attn_Wo"]),
        "wf1": c(inputs["L0_attn_Wf1"]),
        "wf2": c(inputs["L0_attn_Wf2"]),
        "wg1": c(inputs["L0_attn_Wg1"]),
        "wg2": c(inputs["L0_attn_Wg2"]),
        "mu_r": c(np.asarray(inputs["L0_ffn_mu_r"]).reshape(NS, P).T),
        "mu_k": c(np.asarray(inputs["L0_ffn_mu_k"]).reshape(NS, P).T),
        "nw": c(np.asarray(inputs["L0_attn_norm_weight"]).reshape(NS, P).T),
    }


def kernel(**inputs):
    x = np.asarray(inputs["L0_attn_x"], dtype=np.float32)
    B, T, _ = x.shape
    TT = 256
    HALF = T // 2
    nc = build_program(T, TT)
    wm = _wmaps(inputs)
    wm.update(_consts(TT))
    in_maps = []
    for cid in range(8):
        b, h = cid // 2, cid % 2
        mp = dict(wm)
        xt = np.asarray(x[b]).T
        if h == 1:
            mp["xT"] = np.ascontiguousarray(xt)
        else:
            mp["xT"] = np.ascontiguousarray(
                np.concatenate([np.zeros((E, HALF), np.float32), xt[:, :HALF]], axis=1))
        in_maps.append(mp)
    res = run_bass_kernel_spmd(nc, in_maps, core_ids=list(range(8)))
    out = np.empty((B, T, E), dtype=np.float32)
    for b in range(B):
        out[b, :HALF] = res.results[2 * b]["outT"].T
        out[b, HALF:] = res.results[2 * b + 1]["outT"].T
    return out


# version 33
# speedup vs baseline: 1.0739x; 1.0739x over previous
import math
import numpy as np
import ml_dtypes
import concourse.bass as bass
import concourse.mybir as mybir
from concourse.bass_utils import run_bass_kernel_spmd

F32 = mybir.dt.float32
BF16 = mybir.dt.bfloat16
AF = mybir.ActivationFunctionType
OP = mybir.AluOpType
MULT = OP.mult
ADD = OP.add

E = 1024
P = 128
NS = 8
D = 64
H = 16
CH = 64
EPS = 1e-5
SCALE = D ** -0.5

ENGINES = ("sp", "pe", "act", "dve", "pool")


class _Op:
    __slots__ = ("idx", "eng", "fn", "deps", "dma", "need_inc", "inc_val", "waits", "tag")

    def __init__(self, idx, eng, fn, dma, tag):
        self.idx = idx
        self.eng = eng
        self.fn = fn
        self.deps = set()
        self.dma = dma
        self.need_inc = False
        self.inc_val = None
        self.waits = []
        self.tag = tag


class Emitter:
    def __init__(self):
        self.ops = []
        self.writer = {}
        self.readers = {}

    def op(self, eng, fn, rd=(), wr=(), dma=None, tag=""):
        o = _Op(len(self.ops), eng, fn, dma, tag)
        for r in rd:
            w = self.writer.get(r)
            if w is not None:
                o.deps.add(w)
            self.readers.setdefault(r, []).append(o.idx)
        for r in wr:
            w = self.writer.get(r)
            if w is not None:
                o.deps.add(w)
            for ri in self.readers.get(r, ()):
                o.deps.add(ri)
            self.writer[r] = o.idx
            self.readers[r] = []
        o.deps.discard(o.idx)
        self.ops.append(o)
        return o

    def finalize(self, sems, dsems):
        for o in self.ops:
            for d in o.deps:
                p = self.ops[d]
                if p.dma is not None or p.eng != o.eng:
                    p.need_inc = True
        counts = {e: 0 for e in ENGINES}
        dcounts = {q: 0 for q in dsems}
        for o in self.ops:
            if o.dma is not None:
                dcounts[o.dma] += 16
                o.inc_val = dcounts[o.dma]
            elif o.need_inc:
                counts[o.eng] += 1
                o.inc_val = counts[o.eng]
        for o in self.ops:
            need = {}
            if o.dma is not None and o.inc_val > 16:
                need[("d", o.dma)] = o.inc_val - 16
            for d in sorted(o.deps):
                p = self.ops[d]
                if p.dma is not None:
                    k = ("d", p.dma)
                    need[k] = max(need.get(k, 0), p.inc_val)
                elif p.eng != o.eng:
                    k = ("e", p.eng)
                    need[k] = max(need.get(k, 0), p.inc_val)
            o.waits = sorted(need.items())
        self._sems = sems
        self._dsems = dsems
        self.dcounts = dcounts

    def emit(self, eng_name, eng):
        water = {}
        for o in self.ops:
            if o.eng != eng_name:
                continue
            for (kind, key), val in o.waits:
                sem = self._dsems[key] if kind == "d" else self._sems[key]
                if val > water.get((kind, key), 0):
                    eng.wait_ge(sem, val)
                    water[(kind, key)] = val
            inst = o.fn(eng)
            if o.dma is not None:
                inst.then_inc(self._dsems[o.dma], 16)
            elif o.need_inc:
                inst.then_inc(self._sems[eng_name], 1)


def build_program(T, TT, limit=None):
    assert T % TT == 0 and TT % CH == 0 and TT % P == 0
    NTILE = T // TT
    C = TT // CH
    TT1 = TT + 1
    NSUB = TT // P
    nc = bass.Bass("TRN2", target_bir_lowering=False, debug=False,
                   detect_race_conditions=False)

    dt = {}

    def din(name, shape, dtype=F32):
        dt[name] = nc.dram_tensor(name, list(shape), dtype, kind="ExternalInput").ap()
        return dt[name]

    din("xT", (E, T))
    outT = nc.dram_tensor("outT", [E, T // 2], F32, kind="ExternalOutput").ap()
    for w in ("wr", "wk", "wv", "wq", "wka", "wva", "wo"):
        din(w, (E, E))
    din("wf1", (E, D)); din("wg1", (E, D))
    din("wf2", (D, E)); din("wg2", (D, E))
    din("mu_r", (P, NS)); din("mu_k", (P, NS)); din("nw", (P, NS))
    din("ident", (P, P), BF16)
    din("mask", (P, 256), BF16)
    din("gate", (P, TT), BF16)
    din("ones1", (P, 1), BF16)
    din("onesb", (1, P), BF16)
    din("gnacc", (P, P), BF16)
    din("gnsel", (16, NS * P), BF16)

    em = Emitter()
    import contextlib
    ctx = contextlib.ExitStack()
    sb = {}

    def sbuf(name, shape, dtype):
        sb[name] = ctx.enter_context(nc.sbuf_tensor(name, list(shape), dtype))

    for w in ("wr", "wk", "wv", "wq", "wka", "wva", "wo"):
        sbuf(w + "_s", (P, NS * E), BF16)
    sbuf("wf1_s", (P, NS * D), BF16); sbuf("wg1_s", (P, NS * D), BF16)
    sbuf("wf2_s", (D, E), BF16); sbuf("wg2_s", (D, E), BF16)
    for v in ("mu_r", "mu_k", "nw"):
        sbuf(v + "_s", (P, NS), F32)
    sbuf("ident_s", (P, P), BF16)
    sbuf("mask_s", (P, 256), BF16)
    sbuf("gate_s", (P, TT), BF16)
    sbuf("ones1_s", (P, 1), BF16)
    sbuf("onesb_s", (1, P), BF16)
    sbuf("gnacc_s", (P, P), BF16)
    sbuf("gnsel_s", (16, NS * P), BF16)

    sbuf("xio0", (P, NS * TT), F32)
    sbuf("xio1", (P, NS * TT), F32)
    sbuf("lnx", (P, NS * TT1), BF16)
    sbuf("lnx2", (P, NS * TT1), BF16)
    for s in ("sA", "sB", "sC", "sD", "sF", "sG", "sH"):
        sbuf(s, (P, NS * TT), BF16)
    sbuf("sEx", (P, 2048), BF16)
    sbuf("kt", (P, 2048), BF16)
    sbuf("st1", (P, 512), F32)
    sbuf("st2", (P, 512), F32)
    sbuf("Ssb", (P, NS * 5 * D), BF16)
    sbuf("lam", (P, NS * C), F32)
    sbuf("mid2", (D, 2 * TT), BF16)
    sbuf("lnm", (1, TT), F32); sbuf("lnv", (1, TT), F32); sbuf("lnt", (1, TT), F32)
    for s_ in (0, 1):
        sbuf("lnAb%d" % s_, (1, TT), BF16); sbuf("lnBb%d" % s_, (1, TT), BF16)
    sbuf("gns", (16, TT), F32); sbuf("gnsb", (16, TT), BF16)
    sbuf("cEps", (P, 1), F32); sbuf("cOne", (P, 1), F32); sbuf("cLnS", (P, 1), F32)
    sbuf("kp", (P, 1024), BF16)
    sbuf("lnab", (P, 4 * TT), BF16)

    ps = {}
    for i in range(6):
        ps[i] = ctx.enter_context(nc.psum_tensor("pb%d" % i, [P, 512], F32))
    ps[6] = ctx.enter_context(nc.psum_tensor("pt6", [P, 1024], BF16))
    ps[7] = ctx.enter_context(nc.psum_tensor("pb7", [P, 512], F32))

    sems = {e: ctx.enter_context(nc.semaphore("sem_" + e)) for e in ENGINES}
    dsems = {"sp": ctx.enter_context(nc.semaphore("dma_sp")),
             "spc": ctx.enter_context(nc.semaphore("dma_spc")),
             "pool": ctx.enter_context(nc.semaphore("dma_pool"))}
    for _w in ("wr", "wk", "wv", "wq", "wka", "wva", "wo", "wsm"):
        dsems["w_" + _w] = ctx.enter_context(nc.semaphore("dma_" + _w))

    A = {k: v.ap() for k, v in sb.items()}
    rotc = [0]

    def nextbank():
        b = (4, 5, 7)[rotc[0] % 3]
        rotc[0] += 1
        return b

    rotc6 = [0]

    def nextbank6():
        b = (4, 5, 7, 0, 1, 3)[rotc6[0] % 6]
        rotc6[0] += 1
        return b
    STRIDE1 = ("lnx", "lnx2")

    def fm(name, g, c0, n):
        stride = TT1 if name in STRIDE1 else TT
        return A[name][:, g * stride + c0: g * stride + c0 + n]

    def fmh(name, j, g, c0, n):
        stride = TT1 if name in STRIDE1 else TT
        return A[name][j * D:(j + 1) * D, g * stride + c0: g * stride + c0 + n]

    def vtm_keys(sub, half):
        return [("sA", 4 * sub + 2 * half), ("sA", 4 * sub + 2 * half + 1), ("vtm", sub, half)]

    def pm_keys(p_):
        return [("sEx", 2 * p_), ("sEx", 2 * p_ + 1)]

    def wload(name, src_key, rearr, q):
        src = dt[src_key]
        dst = A[name]
        if rearr:
            src = src.rearrange("(c p) d -> p c d", p=P)
            dst = dst.rearrange("p (c d) -> p c d", c=NS)
        eng_name = "sp" if q == "spc" else "pool"
        em.op(eng_name, lambda eng, s=src, d=dst: eng.dma_start(out=d, in_=s),
              wr=[(name,)], dma=q, tag="wload:" + name)

    _prev_w = [None]

    def wload_chain(name, src_key, rearr, q):
        wload(name, src_key, rearr, q)
        if _prev_w[0] is not None:
            em.ops[-1].deps.add(_prev_w[0].idx)
        _prev_w[0] = em.ops[-1]

    for w in ("wr", "wk", "wv"):
        wload_chain(w + "_s", w, True, "w_" + w)
    wload_chain("wf1_s", "wf1", True, "w_wsm")
    wload_chain("wf2_s", "wf2", False, "w_wsm")
    for w in ("wka", "wva", "wq", "wo"):
        wload_chain(w + "_s", w, True, "w_" + w)
    wload_chain("wg1_s", "wg1", True, "w_wsm")
    wload_chain("wg2_s", "wg2", False, "w_wsm")
    for v in ("mu_r", "mu_k", "nw"):
        em.op("sp", lambda eng, s=dt[v], d=A[v + "_s"]: eng.dma_start(out=d, in_=s),
              wr=[(v,)], dma="spc", tag="vload:" + v)
    for cname in ("ones1", "onesb", "gate", "mask", "ident", "gnacc", "gnsel"):
        wload(cname + "_s", cname, False, "spc")

    em.op("dve", lambda eng: eng.memset(A["Ssb"][:, :], 0.0),
          wr=[("S", g) for g in range(NS)], tag="S0")
    em.op("dve", lambda eng: eng.memset(A["st1"][:, :], 0.0), wr=[("st1", 0), ("st1", 1)], tag="z0")
    em.op("dve", lambda eng: eng.memset(A["cEps"][:, :], EPS), wr=[("cEps",)], tag="zE")
    em.op("dve", lambda eng: eng.memset(A["cOne"][:, :], 1.0), wr=[("cOne",)], tag="z1")
    em.op("dve", lambda eng: eng.memset(A["cLnS"][:, :], math.log(SCALE)), wr=[("cLnS",)], tag="zS")
    for i in (0, 1, 2, 3):
        em.op("dve", lambda eng, i=i: eng.tensor_copy(out=ps[i].ap()[:, :], in_=A["st1"][:, :]),
              rd=[("st1", 0), ("st1", 1)],
              wr=[("pb%d" % i,), ("pb%d" % i, "d0"), ("pb%d" % i, "d1")], tag="psz%d" % i)
    em.op("dve", lambda eng: eng.memset(A["sEx"][:, 0:512], 0.0),
          wr=[("sEx", 0), ("sEx", 1)], tag="zb")
    for k in range(8):
        em.op("pe", lambda eng, k=k: eng.transpose(
            ps[6].ap()[:, k * P:(k + 1) * P], A["sEx"][:, 0:P], A["ident_s"][:, :]),
            rd=[("sEx", 0), ("sEx", 1), ("ident_s",)], wr=[("pt6", k // 4)], tag="ptz%d" % k)
    ln1_3 = A["lnx"].rearrange("p (s n) -> p s n", n=TT1)
    ln2_3 = A["lnx2"].rearrange("p (s n) -> p s n", n=TT1)
    em.op("dve", lambda eng: eng.memset(ln1_3[:, :, 0:1], 0.0), wr=[("lnxc0",)], tag="c0a")
    em.op("dve", lambda eng: eng.memset(ln2_3[:, :, 0:1], 0.0), wr=[("lnx2c0",)], tag="c0b")

    xT3 = dt["xT"].rearrange("(s p) n -> p s n", p=P)
    outT3 = outT.rearrange("(s p) n -> p s n", p=P)

    lncnt = [0]

    ln_ctx = {}

    def layer_norm(src, out_name, out_c0, tag, after_slab=None, part=0,
                   cast="sD", sqb="sG"):
        if part in (0, 1):
            si_ = lncnt[0] % 2
            lncnt[0] += 1
            ln_ctx[tag] = (si_, cast)
            LM, LV, LT = "lnm", "lnv", "lnt"
            LA, LB = "lnAb%d" % si_, "lnBb%d" % si_
            for g in range(NS):
                em.op("act", lambda eng, g=g: eng.activation(
                    out=fm(cast, g, 0, TT), in_=fm(src, g, 0, TT), func=AF.Copy),
                    rd=[(src, g)], wr=[(cast, g)], tag=tag + "cast%d" % g)
            for g in range(NS):
                em.op("act", lambda eng, g=g: eng.activation(
                    out=fm(sqb, g, 0, TT), in_=fm(cast, g, 0, TT), func=AF.Square),
                    rd=[(cast, g)], wr=[(sqb, g)], tag=tag + "sq%d" % g)
            for g in range(NS):
                em.op("pe", lambda eng, g=g: eng.matmul(
                    ps[4].ap()[0:1, 0:TT], A["ones1_s"][:, :], fm(cast, g, 0, TT),
                    start=(g == 0), stop=(g == NS - 1)),
                    rd=[(cast, g), ("ones1_s",)],
                    wr=[("pb4",)] if g in (0, NS - 1) else [], tag=tag + "sum%d" % g)
            for g in range(NS):
                em.op("pe", lambda eng, g=g: eng.matmul(
                    ps[5].ap()[0:1, 0:TT], A["ones1_s"][:, :], fm(sqb, g, 0, TT),
                    start=(g == 0), stop=(g == NS - 1)),
                    rd=[(sqb, g), ("ones1_s",)],
                    wr=[("pb5",)] if g in (0, NS - 1) else [], tag=tag + "ssq%d" % g)
            em.op("dve", lambda eng: eng.tensor_scalar_mul(
                out=A[LM][:, :], in0=ps[4].ap()[0:1, 0:TT], scalar1=1.0 / E),
                rd=[("pb4",)], wr=[(LM,)], tag=tag + "mean")
            em.op("dve", lambda eng: eng.tensor_mul(
                out=A[LT][:, :], in0=A[LM][:, :], in1=A[LM][:, :]),
                rd=[(LM,)], wr=[(LT,)], tag=tag + "msq")
            em.op("dve", lambda eng: eng.scalar_tensor_tensor(
                out=A[LV][:, :], in0=ps[5].ap()[0:1, 0:TT], scalar=1.0 / E,
                in1=A[LT][:, :], op0=MULT, op1=OP.subtract),
                rd=[("pb5",), (LT,)], wr=[(LV,)], tag=tag + "var")
            em.op("act", lambda eng: eng.activation(
                out=A[LT][:, :], in_=A[LV][:, :], func=AF.Ln, bias=A["cEps"][0:1, 0:1]),
                rd=[(LV,), ("cEps",)], wr=[(LT,)], tag=tag + "log")
            em.op("act", lambda eng: eng.activation(
                out=A[LA][:, :], in_=A[LT][:, :], func=AF.Exp, scale=-0.5),
                rd=[(LT,)], wr=[(LA,)], tag=tag + "rstd")
            em.op("dve", lambda eng: eng.tensor_mul(
                out=A[LB][:, :], in0=A[LM][:, :], in1=A[LA][:, :]),
                rd=[(LM,), (LA,)], wr=[(LB,)], tag=tag + "Bb")
        if part in (0, 2):
            si_, cast = ln_ctx[tag]
            LA, LB = "lnAb%d" % si_, "lnBb%d" % si_
            abo = si_ * 512
            em.op("pe", lambda eng: eng.matmul(
                ps[7].ap()[0:P, 0:TT], A["onesb_s"][:, :], A[LA][:, :], start=True, stop=True),
                rd=[(LA,), ("onesb_s",)], wr=[("pb7",)], tag=tag + "bcA")
            em.op("pe", lambda eng: eng.matmul(
                ps[2].ap()[0:P, 0:TT], A["onesb_s"][:, :], A[LB][:, :], start=True, stop=True),
                rd=[(LB,), ("onesb_s",)],
                wr=[("pb2",), ("pb2", "d0"), ("pb2", "d1")], tag=tag + "bcB")
            em.op("act", lambda eng: eng.activation(
                out=A["lnab"][:, abo:abo + TT], in_=ps[7].ap()[0:P, 0:TT], func=AF.Copy),
                rd=[("pb7",)], wr=[("lnabA", si_)], tag=tag + "evA")
            em.op("act", lambda eng: eng.activation(
                out=A["lnab"][:, abo + TT:abo + 2 * TT], in_=ps[2].ap()[0:P, 0:TT], func=AF.Copy),
                rd=[("pb2",)], wr=[("lnabB", si_)], tag=tag + "evB")
            for g in range(NS):
                em.op("dve", lambda eng, g=g: eng.tensor_mul(
                    out=fm(cast, g, 0, TT), in0=fm(cast, g, 0, TT), in1=A["lnab"][:, abo:abo + TT]),
                    rd=[(cast, g), ("lnabA", si_)], wr=[(cast, g)], tag=tag + "ap1_%d" % g)
                em.op("dve", lambda eng, g=g: eng.tensor_sub(
                    out=fm(out_name, g, out_c0, TT), in0=fm(cast, g, 0, TT),
                    in1=A["lnab"][:, abo + TT:abo + 2 * TT]),
                    rd=[(cast, g), ("lnabB", si_)], wr=[(out_name, g)], tag=tag + "ap2_%d" % g)
                if after_slab is not None:
                    after_slab(g)

    def ffn_mixes(src_ln, c0key, tag, bufR, bufK):
        xkeys = [("vtm", 0, 0), ("vtm", 0, 1), ("vtm", 1, 0), ("vtm", 1, 1)] if bufR == "sA" else []

        def mix(g):
            em.op("dve", lambda eng, g=g: eng.tensor_sub(
                out=fm("sG", g, 0, TT), in0=fm(src_ln, g, 1, TT), in1=fm(src_ln, g, 0, TT)),
                rd=[(src_ln, g), (c0key,)], wr=[("sG", g)], tag=tag + "d%d" % g)
            em.op("dve", lambda eng, g=g: eng.scalar_tensor_tensor(
                out=fm(bufR, g, 0, TT), in0=fm("sG", g, 0, TT),
                scalar=A["mu_r_s"][:, g:g + 1], in1=fm(src_ln, g, 0, TT), op0=MULT, op1=ADD),
                rd=[("sG", g), (src_ln, g), (c0key,), ("mu_r",)],
                wr=[(bufR, g)] + ([("vtm", g // 4, (g % 4) // 2)] if bufR == "sA" else []),
                tag=tag + "xr%d" % g)
            em.op("dve", lambda eng, g=g: eng.scalar_tensor_tensor(
                out=fm(bufK, g, 0, TT), in0=fm("sG", g, 0, TT),
                scalar=A["mu_k_s"][:, g:g + 1], in1=fm(src_ln, g, 0, TT), op0=MULT, op1=ADD),
                rd=[("sG", g), (src_ln, g), (c0key,), ("mu_k",)],
                wr=[(bufK, g)], tag=tag + "xk%d" % g)
        return mix

    def _proj(wname, rhs_name, out_cb, tag2):
        for bp in range(4):
            bank = nextbank6()
            for j in range(2):
                do = 2 * bp + j
                for c in range(NS):
                    em.op("pe", lambda eng, bank=bank, j=j, do=do, c=c: eng.matmul(
                        ps[bank].ap()[:, j * TT:j * TT + TT],
                        A[wname][:, c * E + do * P: c * E + do * P + P],
                        fm(rhs_name, c, 0, TT),
                        start=(c == 0), stop=(c == NS - 1)),
                        rd=[(rhs_name, c), (wname,)],
                        wr=[("pb%d" % bank,)] if ((c == 0 and j == 0) or (c == NS - 1 and j == 1)) else [],
                        tag=tag2 + "%d_%d_%d" % (bp, j, c))
            out_cb(bp, bank)

    def ffn_a(tag, bufR="sA", bufK="sB", rbuf="sH", kbuf="sC", k_first=False):
        def r_out(bp, bank):
            em.op("act", lambda eng, bp=bp, bank=bank: eng.activation(
                out=fm(rbuf, 2 * bp, 0, 2 * TT), in_=ps[bank].ap()[:, 0:2 * TT], func=AF.Sigmoid),
                rd=[("pb%d" % bank,)], wr=[(rbuf, 2 * bp), (rbuf, 2 * bp + 1)], tag=tag + "rs%d" % bp)

        def k_out(bp, bank):
            em.op("act", lambda eng, bp=bp, bank=bank: eng.activation(
                out=fm(kbuf, 2 * bp, 0, 2 * TT), in_=ps[bank].ap()[:, 0:2 * TT], func=AF.Relu),
                rd=[("pb%d" % bank,)], wr=[(kbuf, 2 * bp), (kbuf, 2 * bp + 1)], tag=tag + "kr%d" % bp)
        if k_first:
            _proj("wk_s", bufK, k_out, tag + "k")
            _proj("wr_s", bufR, r_out, tag + "r")
        else:
            _proj("wr_s", bufR, r_out, tag + "r")
            _proj("wk_s", bufK, k_out, tag + "k")

    def ffn_b(dst_name, resid_name, tag, rbuf="sH", kbuf="sC", ts_="st1"):
        for q4 in range(4):
            em.op("dve", lambda eng, q4=q4: eng.tensor_mul(
                out=A[kbuf][:, q4 * 2 * TT:(q4 + 1) * 2 * TT],
                in0=A[kbuf][:, q4 * 2 * TT:(q4 + 1) * 2 * TT],
                in1=A[kbuf][:, q4 * 2 * TT:(q4 + 1) * 2 * TT]),
                rd=[(kbuf, 2 * q4), (kbuf, 2 * q4 + 1)],
                wr=[(kbuf, 2 * q4), (kbuf, 2 * q4 + 1)], tag=tag + "k2_%d" % q4)

        def v_out(bp, bank):
            for j in range(2):
                do = 2 * bp + j
                em.op("dve", lambda eng, j=j, bank=bank, do=do: eng.tensor_mul(
                    out=A[ts_][:, 0:TT], in0=ps[bank].ap()[:, j * TT:j * TT + TT],
                    in1=fm(rbuf, do, 0, TT)),
                    rd=[("pb%d" % bank,), (rbuf, do)], wr=[(ts_, 0)], tag=tag + "vm%d_%d" % (bp, j))
                em.op("dve", lambda eng, do=do: eng.scalar_tensor_tensor(
                    out=fm(dst_name, do, 0, TT), in0=A[ts_][:, 0:TT], scalar=0.5,
                    in1=fm(resid_name, do, 0, TT), op0=MULT, op1=ADD),
                    rd=[(ts_, 0), (resid_name, do)], wr=[(dst_name, do)], tag=tag + "vo%d_%d" % (bp, j))
        _proj("wv_s", kbuf, v_out, tag + "v")

    def tile_ctx(t):
        xio = "xio0" if t % 2 == 0 else "xio1"
        return xio, t * TT, A[xio].rearrange("p (s n) -> p s n", n=TT)

    def emit_tile_front_a(t, mode):
        xio, base, xio3 = tile_ctx(t)
        em.op("sp", lambda eng, b=base, d=xio3: eng.dma_start(
            out=d[:, :, :], in_=xT3[:, :, b:b + TT]),
            wr=[(xio, g) for g in range(NS)], dma="sp", tag="t%dxin" % t)
        if t > 0:
            em.op("dve", lambda eng: eng.tensor_copy(
                out=ln1_3[:, :, 0:1], in_=ln1_3[:, :, TT:TT1]),
                rd=[("lnx", g) for g in range(NS)], wr=[("lnxc0",)], tag="t%dc0a" % t)
        layer_norm(xio, "lnx", 1, "t%dln1_" % t, part=1)

    def emit_tile_front_b(t, mode):
        layer_norm(None, "lnx", 1, "t%dln1_" % t, part=2,
                   after_slab=ffn_mixes("lnx", "lnxc0", "t%df1" % t, "sA", "sB"))

    def emit_tile_front2a(t, mode):
        ffn_a("t%df1" % t, bufR="sA", bufK="sB", rbuf="sB", kbuf="sD", k_first=True)

    def emit_tile_front2b(t, mode):
        xio, base, xio3 = tile_ctx(t)
        ffn_b(xio, xio, "t%df1" % t, rbuf="sB", kbuf="sD", ts_="st1")
        layer_norm(xio, "lnx", 0, "t%dln2_" % t, part=1)

    def emit_tile_front2c(t, mode):
        layer_norm(None, "lnx", 0, "t%dln2_" % t, part=2)

    def emit_tile_mid(t, mode):
        full = mode != "pre"
        xio, base, xio3 = tile_ctx(t)
        LN1 = "lnx"
        for g in range(NS):
            em.op("pe", lambda eng, g=g: eng.matmul(
                ps[7].ap()[0:D, 0:TT], A["wf1_s"][:, g * D:(g + 1) * D], fm(LN1, g, 0, TT),
                start=(g == 0), stop=(g == NS - 1)),
                rd=[(LN1, g), ("wf1_s",)],
                wr=[("pb7",)] if g in (0, NS - 1) else [], tag="t%dfm%d" % (t, g))
        em.op("act", lambda eng: eng.activation(
            out=A["mid2"][0:D, 0:TT], in_=ps[7].ap()[0:D, 0:TT], func=AF.Copy),
            rd=[("pb7",)], wr=[("mid2f",)], tag="t%dfmc" % t)
        if full:
            for g in range(NS):
                em.op("pe", lambda eng, g=g: eng.matmul(
                    ps[7].ap()[0:D, 256:256 + TT], A["wg1_s"][:, g * D:(g + 1) * D], fm(LN1, g, 0, TT),
                    start=(g == 0), stop=(g == NS - 1)),
                    rd=[(LN1, g), ("wg1_s",)],
                    wr=[("pb7",)] if g in (0, NS - 1) else [], tag="t%dgm%d" % (t, g))
            em.op("act", lambda eng: eng.activation(
                out=A["mid2"][0:D, TT:2 * TT], in_=ps[7].ap()[0:D, 256:256 + TT], func=AF.Copy),
                rd=[("pb7",)], wr=[("mid2g",)], tag="t%dgmc" % t)

        def aproj(wname, rhs_name, out_cb, tag2):
            for bp in range(4):
                bank = nextbank()
                for j in range(2):
                    do = 2 * bp + j
                    for c in range(NS):
                        em.op("pe", lambda eng, bank=bank, j=j, do=do, c=c: eng.matmul(
                            ps[bank].ap()[:, j * TT:j * TT + TT],
                            A[wname][:, c * E + do * P: c * E + do * P + P],
                            fm(rhs_name, c, 0, TT),
                            start=(c == 0), stop=(c == NS - 1)),
                            rd=[(rhs_name, c), (wname,)],
                            wr=[("pb%d" % bank,)] if ((c == 0 and j == 0) or (c == NS - 1 and j == 1)) else [],
                            tag=tag2 + "%d_%d_%d" % (bp, j, c))
                out_cb(bp, bank)

        def q_out(bp, bank):
            em.op("act", lambda eng, bp=bp, bank=bank: eng.activation(
                out=fm("sC", 2 * bp, 0, 2 * TT), in_=ps[bank].ap()[:, 0:2 * TT], func=AF.Silu),
                rd=[("pb%d" % bank,)], wr=[("sC", 2 * bp), ("sC", 2 * bp + 1)],
                tag="t%dqs%d" % (t, bp))
        if full:
            aproj("wq_s", LN1, q_out, "t%dq" % t)

        def ka_out(bp, bank):
            em.op("act", lambda eng, bp=bp, bank=bank: eng.activation(
                out=fm("sD", 2 * bp, 0, 2 * TT), in_=ps[bank].ap()[:, 0:2 * TT], func=AF.Silu),
                rd=[("pb%d" % bank,)], wr=[("sD", 2 * bp), ("sD", 2 * bp + 1)],
                tag="t%dks%d" % (t, bp))
        aproj("wka_s", LN1, ka_out, "t%dk" % t)

        for bp in range(4):
            bank = nextbank()
            for j in range(2):
                do = 2 * bp + j
                em.op("pe", lambda eng, bank=bank, j=j, do=do: eng.matmul(
                    ps[bank].ap()[:, j * TT:j * TT + TT],
                    A["wf2_s"][0:D, do * P:do * P + P], A["mid2"][0:D, 0:TT],
                    start=True, stop=True),
                    rd=[("mid2f",), ("wf2_s",)], wr=[("pb%d" % bank,)],
                    tag="t%dfz%d_%d" % (t, bp, j))
            em.op("act", lambda eng, bank=bank, bp=bp: eng.activation(
                out=fm("sH", 2 * bp, 0, 2 * TT), in_=ps[bank].ap()[:, 0:512], func=AF.Copy),
                rd=[("pb%d" % bank,)], wr=[("sH", 2 * bp), ("sH", 2 * bp + 1)], tag="t%dfze%d" % (t, bp))
        for bp in (range(4) if full else ()):
            bank = nextbank()
            for j in range(2):
                do = 2 * bp + j
                em.op("pe", lambda eng, bank=bank, j=j, do=do: eng.matmul(
                    ps[bank].ap()[:, j * TT:j * TT + TT],
                    A["wg2_s"][0:D, do * P:do * P + P], A["mid2"][0:D, TT:2 * TT],
                    start=True, stop=True),
                    rd=[("mid2g",), ("wg2_s",)], wr=[("pb%d" % bank,)],
                    tag="t%dgz%d_%d" % (t, bp, j))
            em.op("act", lambda eng, bank=bank, bp=bp: eng.activation(
                out=fm("sB", 2 * bp, 0, 2 * TT), in_=ps[bank].ap()[:, 0:512], func=AF.Sigmoid),
                rd=[("pb%d" % bank,)], wr=[("sB", 2 * bp), ("sB", 2 * bp + 1)],
                tag="t%dgs%d" % (t, bp))

        for bp in (range(4) if full else ()):
            bank = nextbank()
            for j in range(2):
                do = 2 * bp + j
                em.op("pe", lambda eng, bank=bank, j=j, do=do: eng.matmul(
                    ps[bank].ap()[:, j * TT:j * TT + TT],
                    A["wg2_s"][0:D, do * P:do * P + P], A["mid2"][0:D, TT:2 * TT],
                    start=True, stop=True),
                    rd=[("mid2g",), ("wg2_s",)], wr=[("pb%d" % bank,)],
                    tag="t%dgz%d_%d" % (t, bp, j))
            em.op("act", lambda eng, bank=bank, bp=bp: eng.activation(
                out=fm("sB", 2 * bp, 0, 2 * TT), in_=ps[bank].ap()[:, 0:512], func=AF.Sigmoid),
                rd=[("pb%d" % bank,)], wr=[("sB", 2 * bp), ("sB", 2 * bp + 1)],
                tag="t%dgs%d" % (t, bp))

        def emit_kp(bp):
            for g in (2 * bp, 2 * bp + 1):
                p_ = g % 2
                sidx = (bp % 2) * 2 + p_
                for c in range(C):
                    em.op("dve", lambda eng, g=g, c=c, sidx=sidx: eng.tensor_scalar_mul(
                        out=A["kp"][:, sidx * 256 + c * CH: sidx * 256 + c * CH + CH],
                        in0=fm("sD", g, c * CH, CH),
                        scalar1=A["lam"][:, g * C + c: g * C + c + 1]),
                        rd=[("sD", g), ("lam",)], wr=[("kp", sidx)], tag="t%dKp%d_%d" % (t, g, c))


        def emit_chain_pair(bp):
                for j in range(2):
                    g = 2 * bp + j
                    em.op("act", lambda eng, g=g, j=j: eng.activation(
                        out=A["st1"][:, j * TT:j * TT + TT], in_=fm("sH", g, 0, TT),
                        func=AF.Exp, scale=-1.0),
                        rd=[("sH", g)], wr=[("st1", j)], tag="t%dfe%d_%d" % (t, bp, j))
                    em.op("act", lambda eng, j=j: eng.activation(
                        out=A["st2"][:, j * TT:j * TT + TT], in_=A["st1"][:, j * TT:j * TT + TT],
                        func=AF.Ln, bias=A["cOne"][:, 0:1]),
                        rd=[("st1", j), ("cOne",)], wr=[("st2", j)], tag="t%dfl%d_%d" % (t, bp, j))
                    em.op("dve", lambda eng, g=g, j=j: eng.tensor_tensor_scan(
                        out=A["st1"][:, j * TT:j * TT + TT], data0=A["gate_s"][:, 0:TT],
                        data1=A["st2"][:, j * TT:j * TT + TT], initial=0.0, op0=MULT, op1=ADD),
                        rd=[("st2", j), ("gate_s",)], wr=[("st1", j)], tag="t%dsc%d" % (t, g))
                pr = [("sEx", 0, 2), ("sEx", 0, 3)]
                if full:
                    for j in range(2):
                        em.op("act", lambda eng, j=j: eng.activation(
                            out=A["sEx"][:, 512 + j * TT:512 + j * TT + TT],
                            in_=A["st1"][:, j * TT:j * TT + TT],
                            func=AF.Exp, scale=-1.0, bias=A["cLnS"][:, 0:1]),
                            rd=[("st1", j), ("cLnS",)], wr=[("sEx", 0, 2 + j)],
                            tag="t%dEp%d_%d" % (t, bp, j))
                    em.op("dve", lambda eng, bp=bp: eng.tensor_mul(
                        out=fm("sC", 2 * bp, 0, 2 * TT), in0=fm("sC", 2 * bp, 0, 2 * TT),
                        in1=A["sEx"][:, 512:1024]),
                        rd=[("sC", 2 * bp), ("sC", 2 * bp + 1)] + pr,
                        wr=[("sC", 2 * bp), ("sC", 2 * bp + 1)], tag="t%dQh%d" % (t, bp))
                em.op("act", lambda eng, bp=bp: eng.activation(
                    out=A["sEx"][:, 512:1024], in_=A["st1"][:, 0:512], func=AF.Exp),
                    rd=[("st1", 0), ("st1", 1)], wr=pr, tag="t%dEm%d" % (t, bp))
                em.op("dve", lambda eng, bp=bp: eng.tensor_mul(
                    out=fm("sD", 2 * bp, 0, 2 * TT), in0=fm("sD", 2 * bp, 0, 2 * TT),
                    in1=A["sEx"][:, 512:1024]),
                    rd=[("sD", 2 * bp), ("sD", 2 * bp + 1)] + pr,
                    wr=[("sD", 2 * bp), ("sD", 2 * bp + 1)], tag="t%dKh%d" % (t, bp))
                st1v = A["st1"].rearrange("p (s c l) -> p s c l", s=2, c=C)
                lam3 = A["lam"].rearrange("p (g c) -> p g c", g=NS)
                em.op("act", lambda eng, bp=bp, sv=st1v, l3=lam3: eng.activation(
                    out=l3[:, 2 * bp:2 * bp + 2, :], in_=sv[:, :, :, CH - 1],
                    func=AF.Exp, scale=-1.0),
                    rd=[("st1", 0), ("st1", 1)], wr=[("lam",)], tag="t%dlam%d" % (t, bp))
                if bp < 2:
                    emit_kp(bp)

        def emit_vtm_fill(sub, half):
            bank = nextbank()
            for c in range(NS):
                em.op("pe", lambda eng, bank=bank, sub=sub, half=half, c=c: eng.matmul(
                    ps[bank].ap()[:, 0:512], fm(LN1, c, sub * P, P),
                    A["wva_s"][:, c * E + half * 512: c * E + half * 512 + 512],
                    start=(c == 0), stop=(c == NS - 1)),
                    rd=[(LN1, c), ("wva_s",)],
                    wr=[("pb%d" % bank,)] if c in (0, NS - 1) else [],
                    tag="t%dvt%d_%d_%d" % (t, sub, half, c))
            em.op("dve", lambda eng, bank=bank, sub=sub, half=half: eng.tensor_copy(
                out=A["sA"][:, sub * E + half * 512: sub * E + half * 512 + 512],
                in_=ps[bank].ap()[:, 0:512]),
                rd=[("pb%d" % bank,)], wr=vtm_keys(sub, half),
                tag="t%dve%d_%d" % (t, sub, half))

        for p4 in range(4):
            emit_chain_pair(p4)
            emit_vtm_fill(p4 // 2, p4 % 2)

        slot0 = (4 * t) % 5

        late = []

        def scan_A(gp):
            pair = (2 * gp, 2 * gp + 1)
            kb = (gp % 2) * 1024
            for g in pair:
                b0 = 4 * (g % 2)
                for c in (range(C) if full else ()):
                    for j in range(2):
                        em.op("pe", lambda eng, g=g, c=c, j=j, b0=b0: eng.matmul(
                            ps[b0 + j].ap()[D * (c % 2):D * (c % 2) + D, c * CH: c * CH + CH],
                            fmh("sD", j, g, c * CH, CH), fmh("sC", j, g, c * CH, CH),
                            start=True, stop=True, tile_position=(D * j, D * (c % 2))),
                            rd=[("sD", g), ("sC", g)], wr=[("pb%d" % (b0 + j),)],
                            tag="t%dP%d_%d_%d" % (t, g, c, j))
            eb = (gp % 2) * 1024
            for g in (pair if full else ()):
                p_ = g % 2
                b0 = 4 * p_
                for j in range(2):
                    em.op("dve", lambda eng, j=j, b0=b0, p_=p_, eb=eb: eng.tensor_mul(
                        out=A["sEx"][:, eb + p_ * 512 + j * 256: eb + p_ * 512 + j * 256 + 256],
                        in0=ps[b0 + j].ap()[:, 0:256], in1=A["mask_s"][:, 0:256]),
                        rd=[("pb%d" % (b0 + j),), ("mask_s",)], wr=[("sEx", gp % 2, 2 * p_ + j)],
                        tag="t%dPM%d_%d" % (t, g, j))
            for g in pair:
                p_ = g % 2
                sidx = (gp % 2) * 2 + p_
                for c in range(C):
                    em.op("pe", lambda eng, g=g, c=c, p_=p_, sidx=sidx: eng.transpose(
                        ps[6].ap()[D * (c % 2):D * (c % 2) + D, p_ * 512 + c * P: p_ * 512 + c * P + P],
                        A["kp"][:, sidx * 256 + c * CH: sidx * 256 + c * CH + CH], A["ident_s"][:, :],
                        tile_position=(0, D * (c % 2))),
                        rd=[("kp", sidx), ("ident_s",)], wr=[("pt6", p_)], tag="t%dT%d_%d" % (t, g, c))
            for p_ in range(2):
                em.op("dve", lambda eng, p_=p_, kb=kb: eng.tensor_copy(
                    out=A["kt"][:, kb + p_ * 512: kb + p_ * 512 + 512],
                    in_=ps[6].ap()[:, p_ * 512: p_ * 512 + 512]),
                    rd=[("pt6", 0), ("pt6", 1)], wr=[("kt", gp % 2, p_)],
                    tag="t%dTe%d_%d" % (t, gp, p_))

        def scan_B(gp):
            pair = (2 * gp, 2 * gp + 1)
            kb = (gp % 2) * 1024
            eb = (gp % 2) * 1024
            for g in pair:
                p_ = g % 2
                for c in range(C):
                    for j in range(2):
                        hh = 2 * g + j
                        em.op("pe", lambda eng, g=g, c=c, j=j, p_=p_, hh=hh, kb=kb: eng.matmul(
                            ps[2 + (c % 2)].ap()[D * j:D * j + D,
                                                 p_ * 128 + (c // 2) * CH: p_ * 128 + (c // 2) * CH + CH],
                            A["kt"][D * (c % 2):D * (c % 2) + D,
                                    kb + p_ * 512 + c * P + j * D: kb + p_ * 512 + c * P + j * D + D],
                            A["sA"][D * (c % 2):D * (c % 2) + D,
                                    (c // 2) * E + hh * D: (c // 2) * E + hh * D + D],
                            start=True, stop=True, tile_position=(D * (c % 2), D * j)),
                            rd=[("kt", gp % 2, p_), ("vtm", c // 2, hh // 8)],
                            wr=[("pb%d" % (2 + c % 2),), ("pb%d" % (2 + c % 2), "d0"), ("pb%d" % (2 + c % 2), "d1")],
                            tag="t%dD%d_%d_%d" % (t, g, c, j))
            for g in pair:
                p_ = g % 2
                for c in range(C):
                    si = (slot0 + c) % 5
                    so = (slot0 + c + 1) % 5
                    em.op("dve", lambda eng, g=g, c=c, si=si, so=so, p_=p_: eng.scalar_tensor_tensor(
                        out=A["Ssb"][:, (g * 5 + so) * D: (g * 5 + so) * D + D],
                        in0=A["Ssb"][:, (g * 5 + si) * D: (g * 5 + si) * D + D],
                        scalar=A["lam"][:, g * C + c: g * C + c + 1],
                        in1=ps[2 + (c % 2)].ap()[:, p_ * 128 + (c // 2) * CH: p_ * 128 + (c // 2) * CH + CH],
                        op0=MULT, op1=ADD),
                        rd=[("S", g), ("lam",), ("pb%d" % (2 + c % 2), "d%d" % p_)], wr=[("S", g)],
                        tag="t%dSu%d_%d" % (t, g, c))
            if gp < 2:
                late.append(lambda gp=gp: emit_kp(gp + 2))
            for g in (pair if full else ()):
                p_ = g % 2
                for c in range(C):
                    for j in range(2):
                        hh = 2 * g + j
                        em.op("pe", lambda eng, g=g, c=c, j=j, p_=p_, hh=hh: eng.matmul(
                            ps[c % 2].ap()[D * j:D * j + D,
                                           256 + p_ * 128 + (c // 2) * CH: 256 + p_ * 128 + (c // 2) * CH + CH],
                            A["sA"][D * (c % 2):D * (c % 2) + D,
                                    (c // 2) * E + hh * D: (c // 2) * E + hh * D + D],
                            A["sEx"][D * (c % 2):D * (c % 2) + D,
                                     eb + p_ * 512 + j * 256 + c * CH: eb + p_ * 512 + j * 256 + c * CH + CH],
                            start=True, stop=True, tile_position=(D * (c % 2), D * j)),
                            rd=[("vtm", c // 2, hh // 8), ("sEx", gp % 2, 2 * p_ + j)],
                            wr=[("pb%d" % (c % 2),)], tag="t%dOI%d_%d_%d" % (t, g, c, j))
            for g in (pair if full else ()):
                p_ = g % 2
                ob = 2 + 2 * p_
                for c in range(C):
                    for j in range(2):
                        si = (slot0 + c) % 5
                        em.op("pe", lambda eng, g=g, c=c, j=j, si=si, ob=ob: eng.matmul(
                            ps[ob + j].ap()[D * j:D * j + D, 256 + c * CH: 256 + c * CH + CH],
                            A["Ssb"][D * j:D * j + D, (g * 5 + si) * D: (g * 5 + si) * D + D],
                            fmh("sC", j, g, c * CH, CH),
                            start=True, stop=True, tile_position=(D * j, D * j)),
                            rd=[("S", g), ("sC", g)],
                            wr=[("pb%d" % (ob + j),), ("pb%d" % (ob + j), "d0"),
                                ("pb%d" % (ob + j), "d1")],
                            tag="t%dOC%d_%d_%d" % (t, g, c, j))
            if full:
                def oie(g):
                    p_ = g % 2
                    sf3 = fm("sF", g, 0, TT).rearrange("p (a l) -> p a l", l=CH)
                    for p2 in range(2):
                        em.op("act", lambda eng, p2=p2, p_=p_, o3=sf3: eng.activation(
                            out=o3[:, p2::2, :],
                            in_=ps[p2].ap()[:, 256 + p_ * 128: 256 + p_ * 128 + 128], func=AF.Copy),
                            rd=[("pb%d" % p2,)], wr=[("sF", g)], tag="t%doie%d_%d" % (t, g, p2))

                def ocv(g):
                    ob = 2 + 2 * (g % 2)
                    for j in range(2):
                        em.op("act", lambda eng, g=g, j=j, ob=ob: eng.activation(
                            out=fmh("sG", j, g, 0, TT), in_=ps[ob + j].ap()[D * j:D * j + D, 256:512],
                            func=AF.Copy),
                            rd=[("pb%d" % (ob + j),)], wr=[("sG", g)], tag="t%doc%d_%d" % (t, g, j))

                def omg(g):
                    em.op("dve", lambda eng, g=g: eng.tensor_add(
                        out=fm("sF", g, 0, TT), in0=fm("sF", g, 0, TT), in1=fm("sG", g, 0, TT)),
                        rd=[("sF", g), ("sG", g)], wr=[("sF", g)], tag="t%dom%d" % (t, g))
                    em.op("dve", lambda eng, g=g: eng.tensor_mul(
                        out=fm("sF", g, 0, TT), in0=fm("sF", g, 0, TT), in1=fm("sB", g, 0, TT)),
                        rd=[("sF", g), ("sB", g)], wr=[("sF", g)], tag="t%dog%d" % (t, g))
                g0, g1 = pair
                oie(g0)
                oie(g1)
                ocv(g0)
                ocv(g1)
                late.append(lambda g0=g0: omg(g0))
                late.append(lambda g1=g1: omg(g1))

        scan_A(0)
        scan_A(1)
        scan_B(0)
        for th in late:
            th()
        late = []
        scan_A(2)
        scan_B(1)
        for th in late:
            th()
        late = []
        scan_A(3)
        scan_B(2)
        for th in late:
            th()
        late = []
        scan_B(3)

        for th in late:
            th()
        late = []

    def emit_tile_back_a1(t, mode):
        full = mode != "pre"
        xio, base, xio3 = tile_ctx(t)
        if not full:
            return
        for g in range(NS):
            em.op("act", lambda eng, g=g: eng.activation(
                out=fm("sG", g, 0, TT), in_=fm("sF", g, 0, TT), func=AF.Square),
                rd=[("sF", g)], wr=[("sG", g)], tag="t%dq2_%d" % (t, g))
        for g in range(NS):
            em.op("pe", lambda eng, g=g: eng.matmul(
                ps[4].ap()[0:16, 0:TT], A["gnacc_s"][:, g * 16:(g + 1) * 16],
                fm("sG", g, 0, TT), start=(g == 0), stop=(g == NS - 1)),
                rd=[("sG", g), ("gnacc_s",)],
                wr=[("pb4",)] if g in (0, NS - 1) else [], tag="t%dgn%d" % (t, g))
        em.op("act", lambda eng: eng.activation(
            out=A["gns"][:, :], in_=ps[4].ap()[0:16, 0:TT], func=AF.Ln,
            scale=1.0 / D, bias=A["cEps"][0:16, 0:1]),
            rd=[("pb4",), ("cEps",)], wr=[("gns",)], tag="t%dgl" % t)
        em.op("act", lambda eng: eng.activation(
            out=A["gnsb"][:, :], in_=A["gns"][:, :], func=AF.Exp, scale=-0.5),
            rd=[("gns",)], wr=[("gnsb",)], tag="t%dgr" % t)
        for g in range(NS):
            bank = 7 if g % 2 == 0 else 2
            em.op("pe", lambda eng, g=g, bank=bank: eng.matmul(
                ps[bank].ap()[0:P, 0:TT], A["gnsel_s"][0:16, g * P:(g + 1) * P],
                A["gnsb"][0:16, 0:TT], start=True, stop=True),
                rd=[("gnsb",), ("gnsel_s",)],
                wr=[("pb%d" % bank,), ("pb%d" % bank, "d0"), ("pb%d" % bank, "d1")],
                tag="t%dgb%d" % (t, g))
            em.op("dve", lambda eng, g=g, bank=bank: eng.scalar_tensor_tensor(
                out=fm("sF", g, 0, TT), in0=fm("sF", g, 0, TT),
                scalar=A["nw_s"][:, g:g + 1], in1=ps[bank].ap()[0:P, 0:TT],
                op0=MULT, op1=MULT),
                rd=[("sF", g), ("nw",), ("pb%d" % bank,)], wr=[("sF", g)],
                tag="t%dga%d" % (t, g))
    def emit_tile_back_a2(t, mode):
        full = mode != "pre"
        xio, base, xio3 = tile_ctx(t)
        if not full:
            return
        for bp in range(4):
            bank = (4, 5, 0, 1)[bp]
            for j in range(2):
                do = 2 * bp + j
                for c in range(NS):
                    em.op("pe", lambda eng, bank=bank, j=j, do=do, c=c: eng.matmul(
                        ps[bank].ap()[:, j * TT:j * TT + TT],
                        A["wo_s"][:, c * E + do * P: c * E + do * P + P],
                        fm("sF", c, 0, TT), start=(c == 0), stop=(c == NS - 1)),
                        rd=[("sF", c), ("wo_s",)],
                        wr=[("pb%d" % bank,)] if ((c == 0 and j == 0) or (c == NS - 1 and j == 1)) else [],
                        tag="t%dwo%d_%d_%d" % (t, bp, j, c))
            for j in range(2):
                do = 2 * bp + j
                em.op("dve", lambda eng, bank=bank, j=j, do=do, xio=xio: eng.tensor_add(
                    out=fm(xio, do, 0, TT), in0=fm(xio, do, 0, TT),
                    in1=ps[bank].ap()[:, j * TT:j * TT + TT]),
                    rd=[("pb%d" % bank,), (xio, do)], wr=[(xio, do)],
                    tag="t%dx2_%d_%d" % (t, bp, j))
        if t > 0:
            em.op("dve", lambda eng: eng.tensor_copy(
                out=ln2_3[:, :, 0:1], in_=ln2_3[:, :, TT:TT1]),
                rd=[("lnx2", g) for g in range(NS)], wr=[("lnx2c0",)], tag="t%dc0b" % t)
        layer_norm(xio, "lnx2", 1, "t%dln3_" % t, part=1, cast="sH", sqb="sG")

    def emit_tile_back_b(t, mode):
        if mode == "pre":
            return
        layer_norm(None, "lnx2", 1, "t%dln3_" % t, part=2,
                   after_slab=ffn_mixes("lnx2", "lnx2c0", "t%df3" % t, "sC", "sF"))

    def emit_tile_back_c(t, mode):
        full = mode != "pre"
        xio, base, xio3 = tile_ctx(t)
        if full:
            ffn_a("t%df3" % t, bufR="sC", bufK="sF", rbuf="sH", kbuf="sC", k_first=False)
            ffn_b(xio, xio, "t%df3" % t, rbuf="sH", kbuf="sC", ts_="st2")
        if mode == "own":
            ob = base - (NTILE // 2) * TT
            em.op("sp", lambda eng, b=ob, s=xio3: eng.dma_start(
                out=outT3[:, :, b:b + TT], in_=s[:, :, :]),
                rd=[(xio, g) for g in range(NS)], dma="sp", tag="t%dout" % t)

    NPRE = NTILE // 2

    def mode_of(t):
        return "pre" if t < NPRE - 1 else ("pre_last" if t == NPRE - 1 else "own")

    emit_tile_front_a(0, mode_of(0))
    emit_tile_front_b(0, mode_of(0))
    emit_tile_front2a(0, mode_of(0))
    emit_tile_front2b(0, mode_of(0))
    emit_tile_front2c(0, mode_of(0))
    for t in range(NTILE):
        m1 = mode_of(t + 1) if t + 1 < NTILE else None
        emit_tile_mid(t, mode_of(t))
        if m1:
            emit_tile_front_a(t + 1, m1)
        emit_tile_back_a1(t, mode_of(t))
        if m1:
            emit_tile_front_b(t + 1, m1)
        emit_tile_back_a2(t, mode_of(t))
        if m1:
            emit_tile_front2a(t + 1, m1)
        if m1:
            emit_tile_front2b(t + 1, m1)
        emit_tile_back_b(t, mode_of(t))
        emit_tile_back_c(t, mode_of(t))
        if m1:
            emit_tile_front2c(t + 1, m1)

    if limit is not None:
        import sys
        for i, o in enumerate(em.ops):
            if i % 50 == 0:
                print("op[%d] %s %s" % (i, o.eng, o.tag), file=sys.stderr)
        em.ops = em.ops[:limit]
    em.finalize(sems, dsems)
    with nc.Block() as block:
        @block.sync
        def _(eng):
            em.emit("sp", eng)
            for q in sorted(dsems):
                if em.dcounts.get(q):
                    eng.wait_ge(dsems[q], em.dcounts[q])

        @block.tensor
        def _(eng):
            em.emit("pe", eng)

        @block.vector
        def _(eng):
            em.emit("dve", eng)

        @block.scalar
        def _(eng):
            em.emit("act", eng)

        @block.gpsimd
        def _(eng):
            em.emit("pool", eng)

    ctx.close()
    return nc


def _consts(TT):
    bf = ml_dtypes.bfloat16
    ident = np.eye(P, dtype=np.float32).astype(bf)
    col = np.arange(256) % CH
    row = np.arange(P) % CH
    mask = (row[:, None] <= col[None, :]).astype(np.float32).astype(bf)
    gate = np.ascontiguousarray(np.ones((P, 1), np.float32) * ((np.arange(TT) % CH) != 0))
    gate = gate.astype(bf)
    ones1 = np.ones((P, 1), np.float32).astype(bf)
    onesb = np.ones((1, P), np.float32).astype(bf)
    gnacc = np.zeros((P, P), np.float32)
    for g in range(NS):
        for p in range(P):
            gnacc[p, g * 16 + 2 * g + p // D] = 1
    gnsel = np.zeros((16, NS * P), np.float32)
    for g in range(NS):
        for p in range(P):
            gnsel[2 * g + p // D, g * P + p] = 1
    return dict(ident=ident, mask=mask, gate=gate, ones1=ones1, onesb=onesb,
                gnacc=gnacc.astype(bf), gnsel=gnsel.astype(bf))


def _wmaps(inputs):
    def c(x):
        return np.ascontiguousarray(np.asarray(x, dtype=np.float32))
    return {
        "wr": c(np.asarray(inputs["L0_ffn_W_r"]).T),
        "wk": c(np.asarray(inputs["L0_ffn_W_k"]).T),
        "wv": c(np.asarray(inputs["L0_ffn_W_v"]).T),
        "wq": c(inputs["L0_attn_Wq"]),
        "wka": c(inputs["L0_attn_Wk"]),
        "wva": c(inputs["L0_attn_Wv"]),
        "wo": c(inputs["L0_attn_Wo"]),
        "wf1": c(inputs["L0_attn_Wf1"]),
        "wf2": c(inputs["L0_attn_Wf2"]),
        "wg1": c(inputs["L0_attn_Wg1"]),
        "wg2": c(inputs["L0_attn_Wg2"]),
        "mu_r": c(np.asarray(inputs["L0_ffn_mu_r"]).reshape(NS, P).T),
        "mu_k": c(np.asarray(inputs["L0_ffn_mu_k"]).reshape(NS, P).T),
        "nw": c(np.asarray(inputs["L0_attn_norm_weight"]).reshape(NS, P).T),
    }


def kernel(**inputs):
    x = np.asarray(inputs["L0_attn_x"], dtype=np.float32)
    B, T, _ = x.shape
    TT = 256
    HALF = T // 2
    nc = build_program(T, TT)
    wm = _wmaps(inputs)
    wm.update(_consts(TT))
    in_maps = []
    for cid in range(8):
        b, h = cid // 2, cid % 2
        mp = dict(wm)
        xt = np.asarray(x[b]).T
        if h == 1:
            mp["xT"] = np.ascontiguousarray(xt)
        else:
            mp["xT"] = np.ascontiguousarray(
                np.concatenate([np.zeros((E, HALF), np.float32), xt[:, :HALF]], axis=1))
        in_maps.append(mp)
    res = run_bass_kernel_spmd(nc, in_maps, core_ids=list(range(8)))
    out = np.empty((B, T, E), dtype=np.float32)
    for b in range(B):
        out[b, :HALF] = res.results[2 * b]["outT"].T
        out[b, HALF:] = res.results[2 * b + 1]["outT"].T
    return out


# version 34
# speedup vs baseline: 1.0744x; 1.0005x over previous
import math
import numpy as np
import ml_dtypes
import concourse.bass as bass
import concourse.mybir as mybir
from concourse.bass_utils import run_bass_kernel_spmd

F32 = mybir.dt.float32
BF16 = mybir.dt.bfloat16
AF = mybir.ActivationFunctionType
OP = mybir.AluOpType
MULT = OP.mult
ADD = OP.add

E = 1024
P = 128
NS = 8
D = 64
H = 16
CH = 64
EPS = 1e-5
SCALE = D ** -0.5

ENGINES = ("sp", "pe", "act", "dve", "pool")


class _Op:
    __slots__ = ("idx", "eng", "fn", "deps", "dma", "need_inc", "inc_val", "waits", "tag")

    def __init__(self, idx, eng, fn, dma, tag):
        self.idx = idx
        self.eng = eng
        self.fn = fn
        self.deps = set()
        self.dma = dma
        self.need_inc = False
        self.inc_val = None
        self.waits = []
        self.tag = tag


class Emitter:
    def __init__(self):
        self.ops = []
        self.writer = {}
        self.readers = {}

    def op(self, eng, fn, rd=(), wr=(), dma=None, tag=""):
        o = _Op(len(self.ops), eng, fn, dma, tag)
        for r in rd:
            w = self.writer.get(r)
            if w is not None:
                o.deps.add(w)
            self.readers.setdefault(r, []).append(o.idx)
        for r in wr:
            w = self.writer.get(r)
            if w is not None:
                o.deps.add(w)
            for ri in self.readers.get(r, ()):
                o.deps.add(ri)
            self.writer[r] = o.idx
            self.readers[r] = []
        o.deps.discard(o.idx)
        self.ops.append(o)
        return o

    def finalize(self, sems, dsems):
        for o in self.ops:
            for d in o.deps:
                p = self.ops[d]
                if p.dma is not None or p.eng != o.eng:
                    p.need_inc = True
        counts = {e: 0 for e in ENGINES}
        dcounts = {q: 0 for q in dsems}
        for o in self.ops:
            if o.dma is not None:
                dcounts[o.dma] += 16
                o.inc_val = dcounts[o.dma]
            elif o.need_inc:
                counts[o.eng] += 1
                o.inc_val = counts[o.eng]
        for o in self.ops:
            need = {}
            if o.dma is not None and o.inc_val > 16:
                need[("d", o.dma)] = o.inc_val - 16
            for d in sorted(o.deps):
                p = self.ops[d]
                if p.dma is not None:
                    k = ("d", p.dma)
                    need[k] = max(need.get(k, 0), p.inc_val)
                elif p.eng != o.eng:
                    k = ("e", p.eng)
                    need[k] = max(need.get(k, 0), p.inc_val)
            o.waits = sorted(need.items())
        self._sems = sems
        self._dsems = dsems
        self.dcounts = dcounts

    def emit(self, eng_name, eng):
        water = {}
        for o in self.ops:
            if o.eng != eng_name:
                continue
            for (kind, key), val in o.waits:
                sem = self._dsems[key] if kind == "d" else self._sems[key]
                if val > water.get((kind, key), 0):
                    eng.wait_ge(sem, val)
                    water[(kind, key)] = val
            inst = o.fn(eng)
            if o.dma is not None:
                inst.then_inc(self._dsems[o.dma], 16)
            elif o.need_inc:
                inst.then_inc(self._sems[eng_name], 1)


def build_program(T, TT, limit=None):
    assert T % TT == 0 and TT % CH == 0 and TT % P == 0
    NTILE = T // TT
    C = TT // CH
    TT1 = TT + 1
    NSUB = TT // P
    nc = bass.Bass("TRN2", target_bir_lowering=False, debug=False,
                   detect_race_conditions=False)

    dt = {}

    def din(name, shape, dtype=F32):
        dt[name] = nc.dram_tensor(name, list(shape), dtype, kind="ExternalInput").ap()
        return dt[name]

    din("xT", (E, T))
    outT = nc.dram_tensor("outT", [E, T // 2], F32, kind="ExternalOutput").ap()
    for w in ("wr", "wk", "wv", "wq", "wka", "wva", "wo"):
        din(w, (E, E))
    din("wf1", (E, D)); din("wg1", (E, D))
    din("wf2", (D, E)); din("wg2", (D, E))
    din("mu_r", (P, NS)); din("mu_k", (P, NS)); din("nw", (P, NS))
    din("ident", (P, P), BF16)
    din("mask", (P, 256), BF16)
    din("gate", (P, TT), BF16)
    din("ones1", (P, 1), BF16)
    din("onesb", (1, P), BF16)
    din("gnacc", (P, P), BF16)
    din("gnsel", (16, NS * P), BF16)

    em = Emitter()
    import contextlib
    ctx = contextlib.ExitStack()
    sb = {}

    def sbuf(name, shape, dtype):
        sb[name] = ctx.enter_context(nc.sbuf_tensor(name, list(shape), dtype))

    for w in ("wr", "wk", "wv", "wq", "wka", "wva", "wo"):
        sbuf(w + "_s", (P, NS * E), BF16)
    sbuf("wf1_s", (P, NS * D), BF16); sbuf("wg1_s", (P, NS * D), BF16)
    sbuf("wf2_s", (D, E), BF16); sbuf("wg2_s", (D, E), BF16)
    for v in ("mu_r", "mu_k", "nw"):
        sbuf(v + "_s", (P, NS), F32)
    sbuf("ident_s", (P, P), BF16)
    sbuf("mask_s", (P, 256), BF16)
    sbuf("gate_s", (P, TT), BF16)
    sbuf("ones1_s", (P, 1), BF16)
    sbuf("onesb_s", (1, P), BF16)
    sbuf("gnacc_s", (P, P), BF16)
    sbuf("gnsel_s", (16, NS * P), BF16)

    sbuf("xio0", (P, NS * TT), F32)
    sbuf("xio1", (P, NS * TT), F32)
    sbuf("lnx", (P, NS * TT1), BF16)
    sbuf("lnx2", (P, NS * TT1), BF16)
    for s in ("sA", "sB", "sC", "sD", "sF", "sG", "sH"):
        sbuf(s, (P, NS * TT), BF16)
    sbuf("sEx", (P, 2048), BF16)
    sbuf("kt", (P, 2048), BF16)
    sbuf("st1", (P, 512), F32)
    sbuf("st2", (P, 512), F32)
    sbuf("Ssb", (P, NS * 5 * D), BF16)
    sbuf("lam", (P, NS * C), F32)
    sbuf("mid2", (D, 2 * TT), BF16)
    sbuf("lnm", (1, TT), F32); sbuf("lnv", (1, TT), F32); sbuf("lnt", (1, TT), F32)
    for s_ in (0, 1):
        sbuf("lnAb%d" % s_, (1, TT), BF16); sbuf("lnBb%d" % s_, (1, TT), BF16)
    sbuf("gns", (16, TT), F32); sbuf("gnsb", (16, TT), BF16)
    sbuf("cEps", (P, 1), F32); sbuf("cOne", (P, 1), F32); sbuf("cLnS", (P, 1), F32)
    sbuf("kp", (P, 1024), BF16)
    sbuf("lnab", (P, 4 * TT), BF16)

    ps = {}
    for i in range(6):
        ps[i] = ctx.enter_context(nc.psum_tensor("pb%d" % i, [P, 512], F32))
    ps[6] = ctx.enter_context(nc.psum_tensor("pt6", [P, 1024], BF16))
    ps[7] = ctx.enter_context(nc.psum_tensor("pb7", [P, 512], F32))

    sems = {e: ctx.enter_context(nc.semaphore("sem_" + e)) for e in ENGINES}
    dsems = {"sp": ctx.enter_context(nc.semaphore("dma_sp")),
             "spc": ctx.enter_context(nc.semaphore("dma_spc")),
             "pool": ctx.enter_context(nc.semaphore("dma_pool"))}
    for _w in ("wr", "wk", "wv", "wq", "wka", "wva", "wo", "wsm"):
        dsems["w_" + _w] = ctx.enter_context(nc.semaphore("dma_" + _w))

    A = {k: v.ap() for k, v in sb.items()}
    rotc = [0]

    def nextbank():
        b = (4, 5, 7)[rotc[0] % 3]
        rotc[0] += 1
        return b

    rotc6 = [0]

    def nextbank6():
        b = (4, 5, 7, 0, 1, 3)[rotc6[0] % 6]
        rotc6[0] += 1
        return b
    STRIDE1 = ("lnx", "lnx2")

    def fm(name, g, c0, n):
        stride = TT1 if name in STRIDE1 else TT
        return A[name][:, g * stride + c0: g * stride + c0 + n]

    def fmh(name, j, g, c0, n):
        stride = TT1 if name in STRIDE1 else TT
        return A[name][j * D:(j + 1) * D, g * stride + c0: g * stride + c0 + n]

    def vtm_keys(sub, half):
        return [("sA", 4 * sub + 2 * half), ("sA", 4 * sub + 2 * half + 1), ("vtm", sub, half)]

    def pm_keys(p_):
        return [("sEx", 2 * p_), ("sEx", 2 * p_ + 1)]

    def wload(name, src_key, rearr, q):
        src = dt[src_key]
        dst = A[name]
        if rearr:
            src = src.rearrange("(c p) d -> p c d", p=P)
            dst = dst.rearrange("p (c d) -> p c d", c=NS)
        eng_name = "sp" if q == "spc" else "pool"
        em.op(eng_name, lambda eng, s=src, d=dst: eng.dma_start(out=d, in_=s),
              wr=[(name,)], dma=q, tag="wload:" + name)

    _prev_w = [None]

    def wload_chain(name, src_key, rearr, q):
        wload(name, src_key, rearr, q)
        if _prev_w[0] is not None:
            em.ops[-1].deps.add(_prev_w[0].idx)
        _prev_w[0] = em.ops[-1]

    for w in ("wr", "wk", "wv"):
        wload_chain(w + "_s", w, True, "w_" + w)
    wload_chain("wf1_s", "wf1", True, "w_wsm")
    wload_chain("wf2_s", "wf2", False, "w_wsm")
    for w in ("wka", "wva", "wq", "wo"):
        wload_chain(w + "_s", w, True, "w_" + w)
    wload_chain("wg1_s", "wg1", True, "w_wsm")
    wload_chain("wg2_s", "wg2", False, "w_wsm")
    for v in ("mu_r", "mu_k", "nw"):
        em.op("sp", lambda eng, s=dt[v], d=A[v + "_s"]: eng.dma_start(out=d, in_=s),
              wr=[(v,)], dma="spc", tag="vload:" + v)
    for cname in ("ones1", "onesb", "gate", "mask", "ident", "gnacc", "gnsel"):
        wload(cname + "_s", cname, False, "spc")

    em.op("dve", lambda eng: eng.memset(A["Ssb"][:, :], 0.0),
          wr=[("S", g) for g in range(NS)], tag="S0")
    em.op("dve", lambda eng: eng.memset(A["st1"][:, :], 0.0), wr=[("st1", 0), ("st1", 1)], tag="z0")
    em.op("dve", lambda eng: eng.memset(A["cEps"][:, :], EPS), wr=[("cEps",)], tag="zE")
    em.op("dve", lambda eng: eng.memset(A["cOne"][:, :], 1.0), wr=[("cOne",)], tag="z1")
    em.op("dve", lambda eng: eng.memset(A["cLnS"][:, :], math.log(SCALE)), wr=[("cLnS",)], tag="zS")
    for i in (0, 1, 2, 3):
        em.op("dve", lambda eng, i=i: eng.tensor_copy(out=ps[i].ap()[:, :], in_=A["st1"][:, :]),
              rd=[("st1", 0), ("st1", 1)],
              wr=[("pb%d" % i,), ("pb%d" % i, "d0"), ("pb%d" % i, "d1")], tag="psz%d" % i)
    em.op("dve", lambda eng: eng.memset(A["sEx"][:, 0:512], 0.0),
          wr=[("sEx", 0), ("sEx", 1)], tag="zb")
    for k in range(8):
        em.op("pe", lambda eng, k=k: eng.transpose(
            ps[6].ap()[:, k * P:(k + 1) * P], A["sEx"][:, 0:P], A["ident_s"][:, :]),
            rd=[("sEx", 0), ("sEx", 1), ("ident_s",)], wr=[("pt6", k // 4)], tag="ptz%d" % k)
    ln1_3 = A["lnx"].rearrange("p (s n) -> p s n", n=TT1)
    ln2_3 = A["lnx2"].rearrange("p (s n) -> p s n", n=TT1)
    em.op("dve", lambda eng: eng.memset(ln1_3[:, :, 0:1], 0.0), wr=[("lnxc0",)], tag="c0a")
    em.op("dve", lambda eng: eng.memset(ln2_3[:, :, 0:1], 0.0), wr=[("lnx2c0",)], tag="c0b")

    xT3 = dt["xT"].rearrange("(s p) n -> p s n", p=P)
    outT3 = outT.rearrange("(s p) n -> p s n", p=P)

    lncnt = [0]

    ln_ctx = {}

    def layer_norm(src, out_name, out_c0, tag, after_slab=None, part=0,
                   cast="sD", sqb="sG"):
        if part in (0, 1):
            si_ = lncnt[0] % 2
            lncnt[0] += 1
            ln_ctx[tag] = (si_, cast)
            LM, LV, LT = "lnm", "lnv", "lnt"
            LA, LB = "lnAb%d" % si_, "lnBb%d" % si_
            for g in range(NS):
                em.op("act", lambda eng, g=g: eng.activation(
                    out=fm(cast, g, 0, TT), in_=fm(src, g, 0, TT), func=AF.Copy),
                    rd=[(src, g)], wr=[(cast, g)], tag=tag + "cast%d" % g)
            for g in range(NS):
                em.op("act", lambda eng, g=g: eng.activation(
                    out=fm(sqb, g, 0, TT), in_=fm(cast, g, 0, TT), func=AF.Square),
                    rd=[(cast, g)], wr=[(sqb, g)], tag=tag + "sq%d" % g)
            for g in range(NS):
                em.op("pe", lambda eng, g=g: eng.matmul(
                    ps[4].ap()[0:1, 0:TT], A["ones1_s"][:, :], fm(cast, g, 0, TT),
                    start=(g == 0), stop=(g == NS - 1)),
                    rd=[(cast, g), ("ones1_s",)],
                    wr=[("pb4",)] if g in (0, NS - 1) else [], tag=tag + "sum%d" % g)
            for g in range(NS):
                em.op("pe", lambda eng, g=g: eng.matmul(
                    ps[5].ap()[0:1, 0:TT], A["ones1_s"][:, :], fm(sqb, g, 0, TT),
                    start=(g == 0), stop=(g == NS - 1)),
                    rd=[(sqb, g), ("ones1_s",)],
                    wr=[("pb5",)] if g in (0, NS - 1) else [], tag=tag + "ssq%d" % g)
            em.op("dve", lambda eng: eng.tensor_scalar_mul(
                out=A[LM][:, :], in0=ps[4].ap()[0:1, 0:TT], scalar1=1.0 / E),
                rd=[("pb4",)], wr=[(LM,)], tag=tag + "mean")
            em.op("dve", lambda eng: eng.tensor_mul(
                out=A[LT][:, :], in0=A[LM][:, :], in1=A[LM][:, :]),
                rd=[(LM,)], wr=[(LT,)], tag=tag + "msq")
            em.op("dve", lambda eng: eng.scalar_tensor_tensor(
                out=A[LV][:, :], in0=ps[5].ap()[0:1, 0:TT], scalar=1.0 / E,
                in1=A[LT][:, :], op0=MULT, op1=OP.subtract),
                rd=[("pb5",), (LT,)], wr=[(LV,)], tag=tag + "var")
            em.op("act", lambda eng: eng.activation(
                out=A[LT][:, :], in_=A[LV][:, :], func=AF.Ln, bias=A["cEps"][0:1, 0:1]),
                rd=[(LV,), ("cEps",)], wr=[(LT,)], tag=tag + "log")
            em.op("act", lambda eng: eng.activation(
                out=A[LA][:, :], in_=A[LT][:, :], func=AF.Exp, scale=-0.5),
                rd=[(LT,)], wr=[(LA,)], tag=tag + "rstd")
            em.op("dve", lambda eng: eng.tensor_mul(
                out=A[LB][:, :], in0=A[LM][:, :], in1=A[LA][:, :]),
                rd=[(LM,), (LA,)], wr=[(LB,)], tag=tag + "Bb")
        if part in (0, 2):
            si_, cast = ln_ctx[tag]
            LA, LB = "lnAb%d" % si_, "lnBb%d" % si_
            abo = si_ * 512
            em.op("pe", lambda eng: eng.matmul(
                ps[7].ap()[0:P, 0:TT], A["onesb_s"][:, :], A[LA][:, :], start=True, stop=True),
                rd=[(LA,), ("onesb_s",)], wr=[("pb7",)], tag=tag + "bcA")
            em.op("pe", lambda eng: eng.matmul(
                ps[2].ap()[0:P, 0:TT], A["onesb_s"][:, :], A[LB][:, :], start=True, stop=True),
                rd=[(LB,), ("onesb_s",)],
                wr=[("pb2",), ("pb2", "d0"), ("pb2", "d1")], tag=tag + "bcB")
            em.op("act", lambda eng: eng.activation(
                out=A["lnab"][:, abo:abo + TT], in_=ps[7].ap()[0:P, 0:TT], func=AF.Copy),
                rd=[("pb7",)], wr=[("lnabA", si_)], tag=tag + "evA")
            em.op("act", lambda eng: eng.activation(
                out=A["lnab"][:, abo + TT:abo + 2 * TT], in_=ps[2].ap()[0:P, 0:TT], func=AF.Copy),
                rd=[("pb2",)], wr=[("lnabB", si_)], tag=tag + "evB")
            for g in range(NS):
                em.op("dve", lambda eng, g=g: eng.tensor_mul(
                    out=fm(cast, g, 0, TT), in0=fm(cast, g, 0, TT), in1=A["lnab"][:, abo:abo + TT]),
                    rd=[(cast, g), ("lnabA", si_)], wr=[(cast, g)], tag=tag + "ap1_%d" % g)
                em.op("dve", lambda eng, g=g: eng.tensor_sub(
                    out=fm(out_name, g, out_c0, TT), in0=fm(cast, g, 0, TT),
                    in1=A["lnab"][:, abo + TT:abo + 2 * TT]),
                    rd=[(cast, g), ("lnabB", si_)], wr=[(out_name, g)], tag=tag + "ap2_%d" % g)
                if after_slab is not None:
                    after_slab(g)

    def ffn_mixes(src_ln, c0key, tag, bufR, bufK):
        xkeys = [("vtm", 0, 0), ("vtm", 0, 1), ("vtm", 1, 0), ("vtm", 1, 1)] if bufR == "sA" else []

        def mix(g):
            em.op("dve", lambda eng, g=g: eng.tensor_sub(
                out=fm("sG", g, 0, TT), in0=fm(src_ln, g, 1, TT), in1=fm(src_ln, g, 0, TT)),
                rd=[(src_ln, g), (c0key,)], wr=[("sG", g)], tag=tag + "d%d" % g)
            em.op("dve", lambda eng, g=g: eng.scalar_tensor_tensor(
                out=fm(bufR, g, 0, TT), in0=fm("sG", g, 0, TT),
                scalar=A["mu_r_s"][:, g:g + 1], in1=fm(src_ln, g, 0, TT), op0=MULT, op1=ADD),
                rd=[("sG", g), (src_ln, g), (c0key,), ("mu_r",)],
                wr=[(bufR, g)] + ([("vtm", g // 4, (g % 4) // 2)] if bufR == "sA" else []),
                tag=tag + "xr%d" % g)
            em.op("dve", lambda eng, g=g: eng.scalar_tensor_tensor(
                out=fm(bufK, g, 0, TT), in0=fm("sG", g, 0, TT),
                scalar=A["mu_k_s"][:, g:g + 1], in1=fm(src_ln, g, 0, TT), op0=MULT, op1=ADD),
                rd=[("sG", g), (src_ln, g), (c0key,), ("mu_k",)],
                wr=[(bufK, g)], tag=tag + "xk%d" % g)
        return mix

    def _proj(wname, rhs_name, out_cb, tag2):
        for bp in range(4):
            bank = nextbank6()
            for j in range(2):
                do = 2 * bp + j
                for c in range(NS):
                    em.op("pe", lambda eng, bank=bank, j=j, do=do, c=c: eng.matmul(
                        ps[bank].ap()[:, j * TT:j * TT + TT],
                        A[wname][:, c * E + do * P: c * E + do * P + P],
                        fm(rhs_name, c, 0, TT),
                        start=(c == 0), stop=(c == NS - 1)),
                        rd=[(rhs_name, c), (wname,)],
                        wr=[("pb%d" % bank,)] if ((c == 0 and j == 0) or (c == NS - 1 and j == 1)) else [],
                        tag=tag2 + "%d_%d_%d" % (bp, j, c))
            out_cb(bp, bank)

    def ffn_a(tag, bufR="sA", bufK="sB", rbuf="sH", kbuf="sC", k_first=False):
        def r_out(bp, bank):
            em.op("act", lambda eng, bp=bp, bank=bank: eng.activation(
                out=fm(rbuf, 2 * bp, 0, 2 * TT), in_=ps[bank].ap()[:, 0:2 * TT], func=AF.Sigmoid),
                rd=[("pb%d" % bank,)], wr=[(rbuf, 2 * bp), (rbuf, 2 * bp + 1)], tag=tag + "rs%d" % bp)

        def k_out(bp, bank):
            em.op("act", lambda eng, bp=bp, bank=bank: eng.activation(
                out=fm(kbuf, 2 * bp, 0, 2 * TT), in_=ps[bank].ap()[:, 0:2 * TT], func=AF.Relu),
                rd=[("pb%d" % bank,)], wr=[(kbuf, 2 * bp), (kbuf, 2 * bp + 1)], tag=tag + "kr%d" % bp)
        if k_first:
            _proj("wk_s", bufK, k_out, tag + "k")
            _proj("wr_s", bufR, r_out, tag + "r")
        else:
            _proj("wr_s", bufR, r_out, tag + "r")
            _proj("wk_s", bufK, k_out, tag + "k")

    def ffn_b(dst_name, resid_name, tag, rbuf="sH", kbuf="sC", ts_="st1"):
        for q4 in range(4):
            em.op("dve", lambda eng, q4=q4: eng.tensor_mul(
                out=A[kbuf][:, q4 * 2 * TT:(q4 + 1) * 2 * TT],
                in0=A[kbuf][:, q4 * 2 * TT:(q4 + 1) * 2 * TT],
                in1=A[kbuf][:, q4 * 2 * TT:(q4 + 1) * 2 * TT]),
                rd=[(kbuf, 2 * q4), (kbuf, 2 * q4 + 1)],
                wr=[(kbuf, 2 * q4), (kbuf, 2 * q4 + 1)], tag=tag + "k2_%d" % q4)

        def v_out(bp, bank):
            for j in range(2):
                do = 2 * bp + j
                em.op("dve", lambda eng, j=j, bank=bank, do=do: eng.tensor_mul(
                    out=A[ts_][:, 0:TT], in0=ps[bank].ap()[:, j * TT:j * TT + TT],
                    in1=fm(rbuf, do, 0, TT)),
                    rd=[("pb%d" % bank,), (rbuf, do)], wr=[(ts_, 0)], tag=tag + "vm%d_%d" % (bp, j))
                em.op("dve", lambda eng, do=do: eng.scalar_tensor_tensor(
                    out=fm(dst_name, do, 0, TT), in0=A[ts_][:, 0:TT], scalar=0.5,
                    in1=fm(resid_name, do, 0, TT), op0=MULT, op1=ADD),
                    rd=[(ts_, 0), (resid_name, do)], wr=[(dst_name, do)], tag=tag + "vo%d_%d" % (bp, j))
        _proj("wv_s", kbuf, v_out, tag + "v")

    def tile_ctx(t):
        xio = "xio0" if t % 2 == 0 else "xio1"
        return xio, t * TT, A[xio].rearrange("p (s n) -> p s n", n=TT)

    def emit_tile_front_a(t, mode):
        xio, base, xio3 = tile_ctx(t)
        em.op("sp", lambda eng, b=base, d=xio3: eng.dma_start(
            out=d[:, :, :], in_=xT3[:, :, b:b + TT]),
            wr=[(xio, g) for g in range(NS)], dma="sp", tag="t%dxin" % t)
        if t > 0:
            em.op("dve", lambda eng: eng.tensor_copy(
                out=ln1_3[:, :, 0:1], in_=ln1_3[:, :, TT:TT1]),
                rd=[("lnx", g) for g in range(NS)], wr=[("lnxc0",)], tag="t%dc0a" % t)
        layer_norm(xio, "lnx", 1, "t%dln1_" % t, part=1)

    def emit_tile_front_b(t, mode):
        layer_norm(None, "lnx", 1, "t%dln1_" % t, part=2,
                   after_slab=ffn_mixes("lnx", "lnxc0", "t%df1" % t, "sA", "sB"))

    def emit_tile_front2a(t, mode):
        ffn_a("t%df1" % t, bufR="sA", bufK="sB", rbuf="sB", kbuf="sD", k_first=True)

    def emit_tile_front2b(t, mode):
        xio, base, xio3 = tile_ctx(t)
        ffn_b(xio, xio, "t%df1" % t, rbuf="sB", kbuf="sD", ts_="st1")
        layer_norm(xio, "lnx", 0, "t%dln2_" % t, part=1)

    def emit_tile_front2c(t, mode):
        layer_norm(None, "lnx", 0, "t%dln2_" % t, part=2)

    def emit_tile_mid(t, mode):
        full = mode != "pre"
        xio, base, xio3 = tile_ctx(t)
        LN1 = "lnx"
        for g in range(NS):
            em.op("pe", lambda eng, g=g: eng.matmul(
                ps[7].ap()[0:D, 0:TT], A["wf1_s"][:, g * D:(g + 1) * D], fm(LN1, g, 0, TT),
                start=(g == 0), stop=(g == NS - 1)),
                rd=[(LN1, g), ("wf1_s",)],
                wr=[("pb7",)] if g in (0, NS - 1) else [], tag="t%dfm%d" % (t, g))
        em.op("act", lambda eng: eng.activation(
            out=A["mid2"][0:D, 0:TT], in_=ps[7].ap()[0:D, 0:TT], func=AF.Copy),
            rd=[("pb7",)], wr=[("mid2f",)], tag="t%dfmc" % t)
        if full:
            for g in range(NS):
                em.op("pe", lambda eng, g=g: eng.matmul(
                    ps[7].ap()[0:D, 256:256 + TT], A["wg1_s"][:, g * D:(g + 1) * D], fm(LN1, g, 0, TT),
                    start=(g == 0), stop=(g == NS - 1)),
                    rd=[(LN1, g), ("wg1_s",)],
                    wr=[("pb7",)] if g in (0, NS - 1) else [], tag="t%dgm%d" % (t, g))
            em.op("act", lambda eng: eng.activation(
                out=A["mid2"][0:D, TT:2 * TT], in_=ps[7].ap()[0:D, 256:256 + TT], func=AF.Copy),
                rd=[("pb7",)], wr=[("mid2g",)], tag="t%dgmc" % t)

        def aproj(wname, rhs_name, out_cb, tag2):
            for bp in range(4):
                bank = nextbank()
                for j in range(2):
                    do = 2 * bp + j
                    for c in range(NS):
                        em.op("pe", lambda eng, bank=bank, j=j, do=do, c=c: eng.matmul(
                            ps[bank].ap()[:, j * TT:j * TT + TT],
                            A[wname][:, c * E + do * P: c * E + do * P + P],
                            fm(rhs_name, c, 0, TT),
                            start=(c == 0), stop=(c == NS - 1)),
                            rd=[(rhs_name, c), (wname,)],
                            wr=[("pb%d" % bank,)] if ((c == 0 and j == 0) or (c == NS - 1 and j == 1)) else [],
                            tag=tag2 + "%d_%d_%d" % (bp, j, c))
                out_cb(bp, bank)

        def q_out(bp, bank):
            em.op("act", lambda eng, bp=bp, bank=bank: eng.activation(
                out=fm("sC", 2 * bp, 0, 2 * TT), in_=ps[bank].ap()[:, 0:2 * TT], func=AF.Silu),
                rd=[("pb%d" % bank,)], wr=[("sC", 2 * bp), ("sC", 2 * bp + 1)],
                tag="t%dqs%d" % (t, bp))
        if full:
            aproj("wq_s", LN1, q_out, "t%dq" % t)

        def ka_out(bp, bank):
            em.op("act", lambda eng, bp=bp, bank=bank: eng.activation(
                out=fm("sD", 2 * bp, 0, 2 * TT), in_=ps[bank].ap()[:, 0:2 * TT], func=AF.Silu),
                rd=[("pb%d" % bank,)], wr=[("sD", 2 * bp), ("sD", 2 * bp + 1)],
                tag="t%dks%d" % (t, bp))
        aproj("wka_s", LN1, ka_out, "t%dk" % t)

        for bp in range(4):
            bank = nextbank()
            for j in range(2):
                do = 2 * bp + j
                em.op("pe", lambda eng, bank=bank, j=j, do=do: eng.matmul(
                    ps[bank].ap()[:, j * TT:j * TT + TT],
                    A["wf2_s"][0:D, do * P:do * P + P], A["mid2"][0:D, 0:TT],
                    start=True, stop=True),
                    rd=[("mid2f",), ("wf2_s",)], wr=[("pb%d" % bank,)],
                    tag="t%dfz%d_%d" % (t, bp, j))
            em.op("act", lambda eng, bank=bank, bp=bp: eng.activation(
                out=fm("sH", 2 * bp, 0, 2 * TT), in_=ps[bank].ap()[:, 0:512], func=AF.Copy),
                rd=[("pb%d" % bank,)], wr=[("sH", 2 * bp), ("sH", 2 * bp + 1)], tag="t%dfze%d" % (t, bp))
        for bp in (range(4) if full else ()):
            bank = nextbank()
            for j in range(2):
                do = 2 * bp + j
                em.op("pe", lambda eng, bank=bank, j=j, do=do: eng.matmul(
                    ps[bank].ap()[:, j * TT:j * TT + TT],
                    A["wg2_s"][0:D, do * P:do * P + P], A["mid2"][0:D, TT:2 * TT],
                    start=True, stop=True),
                    rd=[("mid2g",), ("wg2_s",)], wr=[("pb%d" % bank,)],
                    tag="t%dgz%d_%d" % (t, bp, j))
            em.op("act", lambda eng, bank=bank, bp=bp: eng.activation(
                out=fm("sB", 2 * bp, 0, 2 * TT), in_=ps[bank].ap()[:, 0:512], func=AF.Sigmoid),
                rd=[("pb%d" % bank,)], wr=[("sB", 2 * bp), ("sB", 2 * bp + 1)],
                tag="t%dgs%d" % (t, bp))

        for bp in (range(4) if full else ()):
            bank = nextbank()
            for j in range(2):
                do = 2 * bp + j
                em.op("pe", lambda eng, bank=bank, j=j, do=do: eng.matmul(
                    ps[bank].ap()[:, j * TT:j * TT + TT],
                    A["wg2_s"][0:D, do * P:do * P + P], A["mid2"][0:D, TT:2 * TT],
                    start=True, stop=True),
                    rd=[("mid2g",), ("wg2_s",)], wr=[("pb%d" % bank,)],
                    tag="t%dgz%d_%d" % (t, bp, j))
            em.op("act", lambda eng, bank=bank, bp=bp: eng.activation(
                out=fm("sB", 2 * bp, 0, 2 * TT), in_=ps[bank].ap()[:, 0:512], func=AF.Sigmoid),
                rd=[("pb%d" % bank,)], wr=[("sB", 2 * bp), ("sB", 2 * bp + 1)],
                tag="t%dgs%d" % (t, bp))

        def emit_kp(bp):
            for g in (2 * bp, 2 * bp + 1):
                p_ = g % 2
                sidx = (bp % 2) * 2 + p_
                for c in range(C):
                    em.op("dve", lambda eng, g=g, c=c, sidx=sidx: eng.tensor_scalar_mul(
                        out=A["kp"][:, sidx * 256 + c * CH: sidx * 256 + c * CH + CH],
                        in0=fm("sD", g, c * CH, CH),
                        scalar1=A["lam"][:, g * C + c: g * C + c + 1]),
                        rd=[("sD", g), ("lam",)], wr=[("kp", sidx)], tag="t%dKp%d_%d" % (t, g, c))


        def emit_chain_pair(bp):
                for j in range(2):
                    g = 2 * bp + j
                    em.op("act", lambda eng, g=g, j=j: eng.activation(
                        out=A["st1"][:, j * TT:j * TT + TT], in_=fm("sH", g, 0, TT),
                        func=AF.Exp, scale=-1.0),
                        rd=[("sH", g)], wr=[("st1", j)], tag="t%dfe%d_%d" % (t, bp, j))
                    em.op("act", lambda eng, j=j: eng.activation(
                        out=A["st2"][:, j * TT:j * TT + TT], in_=A["st1"][:, j * TT:j * TT + TT],
                        func=AF.Ln, bias=A["cOne"][:, 0:1]),
                        rd=[("st1", j), ("cOne",)], wr=[("st2", j)], tag="t%dfl%d_%d" % (t, bp, j))
                    em.op("dve", lambda eng, g=g, j=j: eng.tensor_tensor_scan(
                        out=A["st1"][:, j * TT:j * TT + TT], data0=A["gate_s"][:, 0:TT],
                        data1=A["st2"][:, j * TT:j * TT + TT], initial=0.0, op0=MULT, op1=ADD),
                        rd=[("st2", j), ("gate_s",)], wr=[("st1", j)], tag="t%dsc%d" % (t, g))
                pr = [("sEx", 0, 2), ("sEx", 0, 3)]
                if full:
                    for j in range(2):
                        em.op("act", lambda eng, j=j: eng.activation(
                            out=A["sEx"][:, 512 + j * TT:512 + j * TT + TT],
                            in_=A["st1"][:, j * TT:j * TT + TT],
                            func=AF.Exp, scale=-1.0, bias=A["cLnS"][:, 0:1]),
                            rd=[("st1", j), ("cLnS",)], wr=[("sEx", 0, 2 + j)],
                            tag="t%dEp%d_%d" % (t, bp, j))
                    em.op("dve", lambda eng, bp=bp: eng.tensor_mul(
                        out=fm("sC", 2 * bp, 0, 2 * TT), in0=fm("sC", 2 * bp, 0, 2 * TT),
                        in1=A["sEx"][:, 512:1024]),
                        rd=[("sC", 2 * bp), ("sC", 2 * bp + 1)] + pr,
                        wr=[("sC", 2 * bp), ("sC", 2 * bp + 1)], tag="t%dQh%d" % (t, bp))
                em.op("act", lambda eng, bp=bp: eng.activation(
                    out=A["sEx"][:, 512:1024], in_=A["st1"][:, 0:512], func=AF.Exp),
                    rd=[("st1", 0), ("st1", 1)], wr=pr, tag="t%dEm%d" % (t, bp))
                em.op("dve", lambda eng, bp=bp: eng.tensor_mul(
                    out=fm("sD", 2 * bp, 0, 2 * TT), in0=fm("sD", 2 * bp, 0, 2 * TT),
                    in1=A["sEx"][:, 512:1024]),
                    rd=[("sD", 2 * bp), ("sD", 2 * bp + 1)] + pr,
                    wr=[("sD", 2 * bp), ("sD", 2 * bp + 1)], tag="t%dKh%d" % (t, bp))
                st1v = A["st1"].rearrange("p (s c l) -> p s c l", s=2, c=C)
                lam3 = A["lam"].rearrange("p (g c) -> p g c", g=NS)
                em.op("act", lambda eng, bp=bp, sv=st1v, l3=lam3: eng.activation(
                    out=l3[:, 2 * bp:2 * bp + 2, :], in_=sv[:, :, :, CH - 1],
                    func=AF.Exp, scale=-1.0),
                    rd=[("st1", 0), ("st1", 1)], wr=[("lam",)], tag="t%dlam%d" % (t, bp))
                if bp < 2:
                    emit_kp(bp)

        def emit_vtm_fill(sub, half):
            bank = nextbank()
            for c in range(NS):
                em.op("pe", lambda eng, bank=bank, sub=sub, half=half, c=c: eng.matmul(
                    ps[bank].ap()[:, 0:512], fm(LN1, c, sub * P, P),
                    A["wva_s"][:, c * E + half * 512: c * E + half * 512 + 512],
                    start=(c == 0), stop=(c == NS - 1)),
                    rd=[(LN1, c), ("wva_s",)],
                    wr=[("pb%d" % bank,)] if c in (0, NS - 1) else [],
                    tag="t%dvt%d_%d_%d" % (t, sub, half, c))
            em.op("dve", lambda eng, bank=bank, sub=sub, half=half: eng.tensor_copy(
                out=A["sA"][:, sub * E + half * 512: sub * E + half * 512 + 512],
                in_=ps[bank].ap()[:, 0:512]),
                rd=[("pb%d" % bank,)], wr=vtm_keys(sub, half),
                tag="t%dve%d_%d" % (t, sub, half))

        for p4 in range(4):
            emit_chain_pair(p4)
            emit_vtm_fill(p4 // 2, p4 % 2)

        slot0 = (4 * t) % 5

        late = []

        def scan_A(gp):
            pair = (2 * gp, 2 * gp + 1)
            kb = (gp % 2) * 1024
            for g in pair:
                b0 = 4 * (g % 2)
                for c in (range(C) if full else ()):
                    for j in range(2):
                        em.op("pe", lambda eng, g=g, c=c, j=j, b0=b0: eng.matmul(
                            ps[b0 + j].ap()[D * (c % 2):D * (c % 2) + D, c * CH: c * CH + CH],
                            fmh("sD", j, g, c * CH, CH), fmh("sC", j, g, c * CH, CH),
                            start=True, stop=True, tile_position=(D * j, D * (c % 2))),
                            rd=[("sD", g), ("sC", g)], wr=[("pb%d" % (b0 + j),)],
                            tag="t%dP%d_%d_%d" % (t, g, c, j))
            eb = (gp % 2) * 1024
            for g in (pair if full else ()):
                p_ = g % 2
                b0 = 4 * p_
                for j in range(2):
                    em.op("dve", lambda eng, j=j, b0=b0, p_=p_, eb=eb: eng.tensor_mul(
                        out=A["sEx"][:, eb + p_ * 512 + j * 256: eb + p_ * 512 + j * 256 + 256],
                        in0=ps[b0 + j].ap()[:, 0:256], in1=A["mask_s"][:, 0:256]),
                        rd=[("pb%d" % (b0 + j),), ("mask_s",)], wr=[("sEx", gp % 2, 2 * p_ + j)],
                        tag="t%dPM%d_%d" % (t, g, j))
            for g in pair:
                p_ = g % 2
                sidx = (gp % 2) * 2 + p_
                for c in range(C):
                    em.op("pe", lambda eng, g=g, c=c, p_=p_, sidx=sidx: eng.transpose(
                        ps[6].ap()[D * (c % 2):D * (c % 2) + D, p_ * 512 + c * P: p_ * 512 + c * P + P],
                        A["kp"][:, sidx * 256 + c * CH: sidx * 256 + c * CH + CH], A["ident_s"][:, :],
                        tile_position=(0, D * (c % 2))),
                        rd=[("kp", sidx), ("ident_s",)], wr=[("pt6", p_)], tag="t%dT%d_%d" % (t, g, c))
            for p_ in range(2):
                em.op("dve", lambda eng, p_=p_, kb=kb: eng.tensor_copy(
                    out=A["kt"][:, kb + p_ * 512: kb + p_ * 512 + 512],
                    in_=ps[6].ap()[:, p_ * 512: p_ * 512 + 512]),
                    rd=[("pt6", 0), ("pt6", 1)], wr=[("kt", gp % 2, p_)],
                    tag="t%dTe%d_%d" % (t, gp, p_))

        def scan_B(gp):
            pair = (2 * gp, 2 * gp + 1)
            kb = (gp % 2) * 1024
            eb = (gp % 2) * 1024
            for g in pair:
                p_ = g % 2
                for c in range(C):
                    for j in range(2):
                        hh = 2 * g + j
                        em.op("pe", lambda eng, g=g, c=c, j=j, p_=p_, hh=hh, kb=kb: eng.matmul(
                            ps[2 + (c % 2)].ap()[D * j:D * j + D,
                                                 p_ * 128 + (c // 2) * CH: p_ * 128 + (c // 2) * CH + CH],
                            A["kt"][D * (c % 2):D * (c % 2) + D,
                                    kb + p_ * 512 + c * P + j * D: kb + p_ * 512 + c * P + j * D + D],
                            A["sA"][D * (c % 2):D * (c % 2) + D,
                                    (c // 2) * E + hh * D: (c // 2) * E + hh * D + D],
                            start=True, stop=True, tile_position=(D * (c % 2), D * j)),
                            rd=[("kt", gp % 2, p_), ("vtm", c // 2, hh // 8)],
                            wr=[("pb%d" % (2 + c % 2),), ("pb%d" % (2 + c % 2), "d0"), ("pb%d" % (2 + c % 2), "d1")],
                            tag="t%dD%d_%d_%d" % (t, g, c, j))
            for g in pair:
                p_ = g % 2
                for c in range(C):
                    si = (slot0 + c) % 5
                    so = (slot0 + c + 1) % 5
                    em.op("dve", lambda eng, g=g, c=c, si=si, so=so, p_=p_: eng.scalar_tensor_tensor(
                        out=A["Ssb"][:, (g * 5 + so) * D: (g * 5 + so) * D + D],
                        in0=A["Ssb"][:, (g * 5 + si) * D: (g * 5 + si) * D + D],
                        scalar=A["lam"][:, g * C + c: g * C + c + 1],
                        in1=ps[2 + (c % 2)].ap()[:, p_ * 128 + (c // 2) * CH: p_ * 128 + (c // 2) * CH + CH],
                        op0=MULT, op1=ADD),
                        rd=[("S", g), ("lam",), ("pb%d" % (2 + c % 2), "d%d" % p_)], wr=[("S", g)],
                        tag="t%dSu%d_%d" % (t, g, c))
            if gp < 2:
                late.append(lambda gp=gp: emit_kp(gp + 2))
            for g in (pair if full else ()):
                p_ = g % 2
                for c in range(C):
                    for j in range(2):
                        hh = 2 * g + j
                        em.op("pe", lambda eng, g=g, c=c, j=j, p_=p_, hh=hh: eng.matmul(
                            ps[c % 2].ap()[D * j:D * j + D,
                                           256 + p_ * 128 + (c // 2) * CH: 256 + p_ * 128 + (c // 2) * CH + CH],
                            A["sA"][D * (c % 2):D * (c % 2) + D,
                                    (c // 2) * E + hh * D: (c // 2) * E + hh * D + D],
                            A["sEx"][D * (c % 2):D * (c % 2) + D,
                                     eb + p_ * 512 + j * 256 + c * CH: eb + p_ * 512 + j * 256 + c * CH + CH],
                            start=True, stop=True, tile_position=(D * (c % 2), D * j)),
                            rd=[("vtm", c // 2, hh // 8), ("sEx", gp % 2, 2 * p_ + j)],
                            wr=[("pb%d" % (c % 2),)], tag="t%dOI%d_%d_%d" % (t, g, c, j))
            for g in (pair if full else ()):
                p_ = g % 2
                for c in range(C):
                    for j in range(2):
                        si = (slot0 + c) % 5
                        em.op("pe", lambda eng, g=g, c=c, j=j, si=si, p_=p_: eng.matmul(
                            ps[4 + j].ap()[D * j:D * j + D, p_ * 256 + c * CH: p_ * 256 + c * CH + CH],
                            A["Ssb"][D * j:D * j + D, (g * 5 + si) * D: (g * 5 + si) * D + D],
                            fmh("sC", j, g, c * CH, CH),
                            start=True, stop=True, tile_position=(D * j, D * j)),
                            rd=[("S", g), ("sC", g)],
                            wr=[("pb%d" % (4 + j),)],
                            tag="t%dOC%d_%d_%d" % (t, g, c, j))
            if full:
                def oie(g):
                    p_ = g % 2
                    sf3 = fm("sF", g, 0, TT).rearrange("p (a l) -> p a l", l=CH)
                    for p2 in range(2):
                        em.op("act", lambda eng, p2=p2, p_=p_, o3=sf3: eng.activation(
                            out=o3[:, p2::2, :],
                            in_=ps[p2].ap()[:, 256 + p_ * 128: 256 + p_ * 128 + 128], func=AF.Copy),
                            rd=[("pb%d" % p2,)], wr=[("sF", g)], tag="t%doie%d_%d" % (t, g, p2))

                def ocv(g):
                    p_ = g % 2
                    for j in range(2):
                        em.op("act", lambda eng, g=g, j=j, p_=p_: eng.activation(
                            out=fmh("sG", j, g, 0, TT),
                            in_=ps[4 + j].ap()[D * j:D * j + D, p_ * 256: p_ * 256 + 256],
                            func=AF.Copy),
                            rd=[("pb%d" % (4 + j),)], wr=[("sG", g)], tag="t%doc%d_%d" % (t, g, j))

                def omg(g):
                    em.op("dve", lambda eng, g=g: eng.tensor_add(
                        out=fm("sF", g, 0, TT), in0=fm("sF", g, 0, TT), in1=fm("sG", g, 0, TT)),
                        rd=[("sF", g), ("sG", g)], wr=[("sF", g)], tag="t%dom%d" % (t, g))
                    em.op("dve", lambda eng, g=g: eng.tensor_mul(
                        out=fm("sF", g, 0, TT), in0=fm("sF", g, 0, TT), in1=fm("sB", g, 0, TT)),
                        rd=[("sF", g), ("sB", g)], wr=[("sF", g)], tag="t%dog%d" % (t, g))
                g0, g1 = pair
                oie(g0)
                oie(g1)
                ocv(g0)
                ocv(g1)
                late.append(lambda g0=g0: omg(g0))
                late.append(lambda g1=g1: omg(g1))

        scan_A(0)
        scan_A(1)
        scan_B(0)
        for th in late:
            th()
        late = []
        scan_A(2)
        scan_B(1)
        for th in late:
            th()
        late = []
        scan_A(3)
        scan_B(2)
        for th in late:
            th()
        late = []
        scan_B(3)

        for th in late:
            th()
        late = []

    def emit_tile_back_a1(t, mode):
        full = mode != "pre"
        xio, base, xio3 = tile_ctx(t)
        if not full:
            return
        for g in range(NS):
            em.op("act", lambda eng, g=g: eng.activation(
                out=fm("sG", g, 0, TT), in_=fm("sF", g, 0, TT), func=AF.Square),
                rd=[("sF", g)], wr=[("sG", g)], tag="t%dq2_%d" % (t, g))
        for g in range(NS):
            em.op("pe", lambda eng, g=g: eng.matmul(
                ps[7].ap()[0:16, 0:TT], A["gnacc_s"][:, g * 16:(g + 1) * 16],
                fm("sG", g, 0, TT), start=(g == 0), stop=(g == NS - 1)),
                rd=[("sG", g), ("gnacc_s",)],
                wr=[("pb7",)] if g in (0, NS - 1) else [], tag="t%dgn%d" % (t, g))
        em.op("act", lambda eng: eng.activation(
            out=A["gns"][:, :], in_=ps[7].ap()[0:16, 0:TT], func=AF.Ln,
            scale=1.0 / D, bias=A["cEps"][0:16, 0:1]),
            rd=[("pb7",), ("cEps",)], wr=[("gns",)], tag="t%dgl" % t)
        em.op("act", lambda eng: eng.activation(
            out=A["gnsb"][:, :], in_=A["gns"][:, :], func=AF.Exp, scale=-0.5),
            rd=[("gns",)], wr=[("gnsb",)], tag="t%dgr" % t)
        for g in range(NS):
            bank = 7 if g % 2 == 0 else 2
            em.op("pe", lambda eng, g=g, bank=bank: eng.matmul(
                ps[bank].ap()[0:P, 0:TT], A["gnsel_s"][0:16, g * P:(g + 1) * P],
                A["gnsb"][0:16, 0:TT], start=True, stop=True),
                rd=[("gnsb",), ("gnsel_s",)],
                wr=[("pb%d" % bank,), ("pb%d" % bank, "d0"), ("pb%d" % bank, "d1")],
                tag="t%dgb%d" % (t, g))
            em.op("dve", lambda eng, g=g, bank=bank: eng.scalar_tensor_tensor(
                out=fm("sF", g, 0, TT), in0=fm("sF", g, 0, TT),
                scalar=A["nw_s"][:, g:g + 1], in1=ps[bank].ap()[0:P, 0:TT],
                op0=MULT, op1=MULT),
                rd=[("sF", g), ("nw",), ("pb%d" % bank,)], wr=[("sF", g)],
                tag="t%dga%d" % (t, g))
    def emit_tile_back_a2(t, mode):
        full = mode != "pre"
        xio, base, xio3 = tile_ctx(t)
        if not full:
            return
        for bp in range(4):
            bank = (4, 5, 0, 1)[bp]
            for j in range(2):
                do = 2 * bp + j
                for c in range(NS):
                    em.op("pe", lambda eng, bank=bank, j=j, do=do, c=c: eng.matmul(
                        ps[bank].ap()[:, j * TT:j * TT + TT],
                        A["wo_s"][:, c * E + do * P: c * E + do * P + P],
                        fm("sF", c, 0, TT), start=(c == 0), stop=(c == NS - 1)),
                        rd=[("sF", c), ("wo_s",)],
                        wr=[("pb%d" % bank,)] if ((c == 0 and j == 0) or (c == NS - 1 and j == 1)) else [],
                        tag="t%dwo%d_%d_%d" % (t, bp, j, c))
            for j in range(2):
                do = 2 * bp + j
                em.op("dve", lambda eng, bank=bank, j=j, do=do, xio=xio: eng.tensor_add(
                    out=fm(xio, do, 0, TT), in0=fm(xio, do, 0, TT),
                    in1=ps[bank].ap()[:, j * TT:j * TT + TT]),
                    rd=[("pb%d" % bank,), (xio, do)], wr=[(xio, do)],
                    tag="t%dx2_%d_%d" % (t, bp, j))
        if t > 0:
            em.op("dve", lambda eng: eng.tensor_copy(
                out=ln2_3[:, :, 0:1], in_=ln2_3[:, :, TT:TT1]),
                rd=[("lnx2", g) for g in range(NS)], wr=[("lnx2c0",)], tag="t%dc0b" % t)
        layer_norm(xio, "lnx2", 1, "t%dln3_" % t, part=1, cast="sH", sqb="sG")

    def emit_tile_back_b(t, mode):
        if mode == "pre":
            return
        layer_norm(None, "lnx2", 1, "t%dln3_" % t, part=2,
                   after_slab=ffn_mixes("lnx2", "lnx2c0", "t%df3" % t, "sC", "sF"))

    def emit_tile_back_c(t, mode):
        full = mode != "pre"
        xio, base, xio3 = tile_ctx(t)
        if full:
            ffn_a("t%df3" % t, bufR="sC", bufK="sF", rbuf="sH", kbuf="sC", k_first=False)
            ffn_b(xio, xio, "t%df3" % t, rbuf="sH", kbuf="sC", ts_="st2")
        if mode == "own":
            ob = base - (NTILE // 2) * TT
            em.op("sp", lambda eng, b=ob, s=xio3: eng.dma_start(
                out=outT3[:, :, b:b + TT], in_=s[:, :, :]),
                rd=[(xio, g) for g in range(NS)], dma="sp", tag="t%dout" % t)

    NPRE = NTILE // 2

    def mode_of(t):
        return "pre" if t < NPRE - 1 else ("pre_last" if t == NPRE - 1 else "own")

    emit_tile_front_a(0, mode_of(0))
    emit_tile_front_b(0, mode_of(0))
    emit_tile_front2a(0, mode_of(0))
    emit_tile_front2b(0, mode_of(0))
    emit_tile_front2c(0, mode_of(0))
    for t in range(NTILE):
        m1 = mode_of(t + 1) if t + 1 < NTILE else None
        emit_tile_mid(t, mode_of(t))
        if m1:
            emit_tile_front_a(t + 1, m1)
        emit_tile_back_a1(t, mode_of(t))
        if m1:
            emit_tile_front_b(t + 1, m1)
        emit_tile_back_a2(t, mode_of(t))
        if m1:
            emit_tile_front2a(t + 1, m1)
        if m1:
            emit_tile_front2b(t + 1, m1)
        emit_tile_back_b(t, mode_of(t))
        emit_tile_back_c(t, mode_of(t))
        if m1:
            emit_tile_front2c(t + 1, m1)

    if limit is not None:
        import sys
        for i, o in enumerate(em.ops):
            if i % 50 == 0:
                print("op[%d] %s %s" % (i, o.eng, o.tag), file=sys.stderr)
        em.ops = em.ops[:limit]
    em.finalize(sems, dsems)
    with nc.Block() as block:
        @block.sync
        def _(eng):
            em.emit("sp", eng)
            for q in sorted(dsems):
                if em.dcounts.get(q):
                    eng.wait_ge(dsems[q], em.dcounts[q])

        @block.tensor
        def _(eng):
            em.emit("pe", eng)

        @block.vector
        def _(eng):
            em.emit("dve", eng)

        @block.scalar
        def _(eng):
            em.emit("act", eng)

        @block.gpsimd
        def _(eng):
            em.emit("pool", eng)

    ctx.close()
    return nc


def _consts(TT):
    bf = ml_dtypes.bfloat16
    ident = np.eye(P, dtype=np.float32).astype(bf)
    col = np.arange(256) % CH
    row = np.arange(P) % CH
    mask = (row[:, None] <= col[None, :]).astype(np.float32).astype(bf)
    gate = np.ascontiguousarray(np.ones((P, 1), np.float32) * ((np.arange(TT) % CH) != 0))
    gate = gate.astype(bf)
    ones1 = np.ones((P, 1), np.float32).astype(bf)
    onesb = np.ones((1, P), np.float32).astype(bf)
    gnacc = np.zeros((P, P), np.float32)
    for g in range(NS):
        for p in range(P):
            gnacc[p, g * 16 + 2 * g + p // D] = 1
    gnsel = np.zeros((16, NS * P), np.float32)
    for g in range(NS):
        for p in range(P):
            gnsel[2 * g + p // D, g * P + p] = 1
    return dict(ident=ident, mask=mask, gate=gate, ones1=ones1, onesb=onesb,
                gnacc=gnacc.astype(bf), gnsel=gnsel.astype(bf))


def _wmaps(inputs):
    def c(x):
        return np.ascontiguousarray(np.asarray(x, dtype=np.float32))
    return {
        "wr": c(np.asarray(inputs["L0_ffn_W_r"]).T),
        "wk": c(np.asarray(inputs["L0_ffn_W_k"]).T),
        "wv": c(np.asarray(inputs["L0_ffn_W_v"]).T),
        "wq": c(inputs["L0_attn_Wq"]),
        "wka": c(inputs["L0_attn_Wk"]),
        "wva": c(inputs["L0_attn_Wv"]),
        "wo": c(inputs["L0_attn_Wo"]),
        "wf1": c(inputs["L0_attn_Wf1"]),
        "wf2": c(inputs["L0_attn_Wf2"]),
        "wg1": c(inputs["L0_attn_Wg1"]),
        "wg2": c(inputs["L0_attn_Wg2"]),
        "mu_r": c(np.asarray(inputs["L0_ffn_mu_r"]).reshape(NS, P).T),
        "mu_k": c(np.asarray(inputs["L0_ffn_mu_k"]).reshape(NS, P).T),
        "nw": c(np.asarray(inputs["L0_attn_norm_weight"]).reshape(NS, P).T),
    }


def kernel(**inputs):
    x = np.asarray(inputs["L0_attn_x"], dtype=np.float32)
    B, T, _ = x.shape
    TT = 256
    HALF = T // 2
    nc = build_program(T, TT)
    wm = _wmaps(inputs)
    wm.update(_consts(TT))
    in_maps = []
    for cid in range(8):
        b, h = cid // 2, cid % 2
        mp = dict(wm)
        xt = np.asarray(x[b]).T
        if h == 1:
            mp["xT"] = np.ascontiguousarray(xt)
        else:
            mp["xT"] = np.ascontiguousarray(
                np.concatenate([np.zeros((E, HALF), np.float32), xt[:, :HALF]], axis=1))
        in_maps.append(mp)
    res = run_bass_kernel_spmd(nc, in_maps, core_ids=list(range(8)))
    out = np.empty((B, T, E), dtype=np.float32)
    for b in range(B):
        out[b, :HALF] = res.results[2 * b]["outT"].T
        out[b, HALF:] = res.results[2 * b + 1]["outT"].T
    return out


# version 35
# speedup vs baseline: 1.0879x; 1.0126x over previous
import math
import numpy as np
import ml_dtypes
import concourse.bass as bass
import concourse.mybir as mybir
from concourse.bass_utils import run_bass_kernel_spmd

F32 = mybir.dt.float32
BF16 = mybir.dt.bfloat16
AF = mybir.ActivationFunctionType
OP = mybir.AluOpType
MULT = OP.mult
ADD = OP.add

E = 1024
P = 128
NS = 8
D = 64
H = 16
CH = 64
EPS = 1e-5
SCALE = D ** -0.5

ENGINES = ("sp", "pe", "act", "dve", "pool")


class _Op:
    __slots__ = ("idx", "eng", "fn", "deps", "dma", "need_inc", "inc_val", "waits", "tag")

    def __init__(self, idx, eng, fn, dma, tag):
        self.idx = idx
        self.eng = eng
        self.fn = fn
        self.deps = set()
        self.dma = dma
        self.need_inc = False
        self.inc_val = None
        self.waits = []
        self.tag = tag


class Emitter:
    def __init__(self):
        self.ops = []
        self.writer = {}
        self.readers = {}

    def op(self, eng, fn, rd=(), wr=(), dma=None, tag=""):
        o = _Op(len(self.ops), eng, fn, dma, tag)
        for r in rd:
            w = self.writer.get(r)
            if w is not None:
                o.deps.add(w)
            self.readers.setdefault(r, []).append(o.idx)
        for r in wr:
            w = self.writer.get(r)
            if w is not None:
                o.deps.add(w)
            for ri in self.readers.get(r, ()):
                o.deps.add(ri)
            self.writer[r] = o.idx
            self.readers[r] = []
        o.deps.discard(o.idx)
        self.ops.append(o)
        return o

    def finalize(self, sems, dsems):
        for o in self.ops:
            for d in o.deps:
                p = self.ops[d]
                if p.dma is not None or p.eng != o.eng:
                    p.need_inc = True
        counts = {e: 0 for e in ENGINES}
        dcounts = {q: 0 for q in dsems}
        for o in self.ops:
            if o.dma is not None:
                dcounts[o.dma] += 16
                o.inc_val = dcounts[o.dma]
            elif o.need_inc:
                counts[o.eng] += 1
                o.inc_val = counts[o.eng]
        for o in self.ops:
            need = {}
            if o.dma is not None and o.inc_val > 16:
                need[("d", o.dma)] = o.inc_val - 16
            for d in sorted(o.deps):
                p = self.ops[d]
                if p.dma is not None:
                    k = ("d", p.dma)
                    need[k] = max(need.get(k, 0), p.inc_val)
                elif p.eng != o.eng:
                    k = ("e", p.eng)
                    need[k] = max(need.get(k, 0), p.inc_val)
            o.waits = sorted(need.items())
        self._sems = sems
        self._dsems = dsems
        self.dcounts = dcounts

    def emit(self, eng_name, eng):
        water = {}
        for o in self.ops:
            if o.eng != eng_name:
                continue
            for (kind, key), val in o.waits:
                sem = self._dsems[key] if kind == "d" else self._sems[key]
                if val > water.get((kind, key), 0):
                    eng.wait_ge(sem, val)
                    water[(kind, key)] = val
            inst = o.fn(eng)
            if o.dma is not None:
                inst.then_inc(self._dsems[o.dma], 16)
            elif o.need_inc:
                inst.then_inc(self._sems[eng_name], 1)


def build_program(T, TT, limit=None):
    assert T % TT == 0 and TT % CH == 0 and TT % P == 0
    NTILE = T // TT
    C = TT // CH
    TT1 = TT + 1
    NSUB = TT // P
    nc = bass.Bass("TRN2", target_bir_lowering=False, debug=False,
                   detect_race_conditions=False)

    dt = {}

    def din(name, shape, dtype=F32):
        dt[name] = nc.dram_tensor(name, list(shape), dtype, kind="ExternalInput").ap()
        return dt[name]

    din("xT", (E, T))
    outT = nc.dram_tensor("outT", [E, T // 2], F32, kind="ExternalOutput").ap()
    for w in ("wr", "wk", "wv", "wq", "wka", "wva", "wo"):
        din(w, (E, E))
    din("wf1", (E, D)); din("wg1", (E, D))
    din("wf2", (D, E)); din("wg2", (D, E))
    din("mu_r", (P, NS)); din("mu_k", (P, NS)); din("nw", (P, NS))
    din("ident", (P, P), BF16)
    din("mask", (P, 256), BF16)
    din("gate", (P, TT), BF16)
    din("ones1", (P, 1), BF16)
    din("onesb", (1, P), BF16)
    din("gnacc", (P, P), BF16)
    din("gnsel", (16, NS * P), BF16)

    em = Emitter()
    import contextlib
    ctx = contextlib.ExitStack()
    sb = {}

    def sbuf(name, shape, dtype):
        sb[name] = ctx.enter_context(nc.sbuf_tensor(name, list(shape), dtype))

    for w in ("wr", "wk", "wv", "wq", "wka", "wva", "wo"):
        sbuf(w + "_s", (P, NS * E), BF16)
    sbuf("wf1_s", (P, NS * D), BF16); sbuf("wg1_s", (P, NS * D), BF16)
    sbuf("wf2_s", (D, E), BF16); sbuf("wg2_s", (D, E), BF16)
    for v in ("mu_r", "mu_k", "nw"):
        sbuf(v + "_s", (P, NS), F32)
    sbuf("ident_s", (P, P), BF16)
    sbuf("mask_s", (P, 256), BF16)
    sbuf("gate_s", (P, TT), BF16)
    sbuf("ones1_s", (P, 1), BF16)
    sbuf("onesb_s", (1, P), BF16)
    sbuf("gnacc_s", (P, P), BF16)
    sbuf("gnsel_s", (16, NS * P), BF16)

    sbuf("xio0", (P, NS * TT), F32)
    sbuf("xio1", (P, NS * TT), F32)
    sbuf("lnx", (P, NS * TT1), BF16)
    sbuf("lnx2", (P, NS * TT1), BF16)
    for s in ("sA", "sB", "sC", "sD", "sF", "sG", "sH"):
        sbuf(s, (P, NS * TT), BF16)
    sbuf("sEx", (P, 2048), BF16)
    sbuf("kt", (P, 2048), BF16)
    sbuf("st1", (P, 512), F32)
    sbuf("st2", (P, 512), F32)
    sbuf("Ssb", (P, NS * 5 * D), BF16)
    sbuf("lam", (P, NS * C), F32)
    sbuf("mid2", (D, 2 * TT), BF16)
    sbuf("lnm", (1, TT), F32); sbuf("lnv", (1, TT), F32); sbuf("lnt", (1, TT), F32)
    for s_ in (0, 1):
        sbuf("lnAb%d" % s_, (1, TT), BF16); sbuf("lnBb%d" % s_, (1, TT), BF16)
    sbuf("gns", (16, TT), F32); sbuf("gnsb", (16, TT), BF16)
    sbuf("cEps", (P, 1), F32); sbuf("cOne", (P, 1), F32); sbuf("cLnS", (P, 1), F32)
    sbuf("kp", (P, 1024), BF16)
    sbuf("lnab", (P, 4 * TT), BF16)

    ps = {}
    for i in range(6):
        ps[i] = ctx.enter_context(nc.psum_tensor("pb%d" % i, [P, 512], F32))
    ps[6] = ctx.enter_context(nc.psum_tensor("pt6", [P, 1024], BF16))
    ps[7] = ctx.enter_context(nc.psum_tensor("pb7", [P, 512], F32))

    sems = {e: ctx.enter_context(nc.semaphore("sem_" + e)) for e in ENGINES}
    dsems = {"sp": ctx.enter_context(nc.semaphore("dma_sp")),
             "spc": ctx.enter_context(nc.semaphore("dma_spc")),
             "pool": ctx.enter_context(nc.semaphore("dma_pool"))}
    for _w in ("wr", "wk", "wv", "wq", "wka", "wva", "wo", "wsm"):
        dsems["w_" + _w] = ctx.enter_context(nc.semaphore("dma_" + _w))

    A = {k: v.ap() for k, v in sb.items()}
    rotc = [0]

    def nextbank():
        b = (4, 5, 7)[rotc[0] % 3]
        rotc[0] += 1
        return b

    rotc6 = [0]

    def nextbank6():
        b = (4, 5, 7, 0, 1, 3)[rotc6[0] % 6]
        rotc6[0] += 1
        return b
    STRIDE1 = ("lnx", "lnx2")

    def fm(name, g, c0, n):
        stride = TT1 if name in STRIDE1 else TT
        return A[name][:, g * stride + c0: g * stride + c0 + n]

    def fmh(name, j, g, c0, n):
        stride = TT1 if name in STRIDE1 else TT
        return A[name][j * D:(j + 1) * D, g * stride + c0: g * stride + c0 + n]

    def vtm_keys(sub, half):
        return [("sA", 4 * sub + 2 * half), ("sA", 4 * sub + 2 * half + 1), ("vtm", sub, half)]

    def pm_keys(p_):
        return [("sEx", 2 * p_), ("sEx", 2 * p_ + 1)]

    def wload(name, src_key, rearr, q):
        src = dt[src_key]
        dst = A[name]
        if rearr:
            src = src.rearrange("(c p) d -> p c d", p=P)
            dst = dst.rearrange("p (c d) -> p c d", c=NS)
        eng_name = "sp" if q == "spc" else "pool"
        em.op(eng_name, lambda eng, s=src, d=dst: eng.dma_start(out=d, in_=s),
              wr=[(name,)], dma=q, tag="wload:" + name)

    _prev_w = [None]

    def wload_chain(name, src_key, rearr, q):
        wload(name, src_key, rearr, q)
        if _prev_w[0] is not None:
            em.ops[-1].deps.add(_prev_w[0].idx)
        _prev_w[0] = em.ops[-1]

    for w in ("wr", "wk", "wv"):
        wload_chain(w + "_s", w, True, "w_" + w)
    wload_chain("wf1_s", "wf1", True, "w_wsm")
    wload_chain("wf2_s", "wf2", False, "w_wsm")
    for w in ("wka", "wva", "wq", "wo"):
        wload_chain(w + "_s", w, True, "w_" + w)
    wload_chain("wg1_s", "wg1", True, "w_wsm")
    wload_chain("wg2_s", "wg2", False, "w_wsm")
    for v in ("mu_r", "mu_k", "nw"):
        em.op("sp", lambda eng, s=dt[v], d=A[v + "_s"]: eng.dma_start(out=d, in_=s),
              wr=[(v,)], dma="spc", tag="vload:" + v)
    for cname in ("ones1", "onesb", "gate", "mask", "ident", "gnacc", "gnsel"):
        wload(cname + "_s", cname, False, "spc")

    em.op("dve", lambda eng: eng.memset(A["Ssb"][:, :], 0.0),
          wr=[("S", g, s5) for g in range(NS) for s5 in range(5)], tag="S0")
    em.op("dve", lambda eng: eng.memset(A["st1"][:, :], 0.0), wr=[("st1", 0), ("st1", 1)], tag="z0")
    em.op("dve", lambda eng: eng.memset(A["cEps"][:, :], EPS), wr=[("cEps",)], tag="zE")
    em.op("dve", lambda eng: eng.memset(A["cOne"][:, :], 1.0), wr=[("cOne",)], tag="z1")
    em.op("dve", lambda eng: eng.memset(A["cLnS"][:, :], math.log(SCALE)), wr=[("cLnS",)], tag="zS")
    for i in (0, 1, 2, 3):
        em.op("dve", lambda eng, i=i: eng.tensor_copy(out=ps[i].ap()[:, :], in_=A["st1"][:, :]),
              rd=[("st1", 0), ("st1", 1)],
              wr=[("pb%d" % i,), ("pb%d" % i, "d0"), ("pb%d" % i, "d1")], tag="psz%d" % i)
    em.op("dve", lambda eng: eng.memset(A["sEx"][:, 0:512], 0.0),
          wr=[("sEx", 0), ("sEx", 1)], tag="zb")
    for k in range(8):
        em.op("pe", lambda eng, k=k: eng.transpose(
            ps[6].ap()[:, k * P:(k + 1) * P], A["sEx"][:, 0:P], A["ident_s"][:, :]),
            rd=[("sEx", 0), ("sEx", 1), ("ident_s",)], wr=[("pt6", k // 4)], tag="ptz%d" % k)
    ln1_3 = A["lnx"].rearrange("p (s n) -> p s n", n=TT1)
    ln2_3 = A["lnx2"].rearrange("p (s n) -> p s n", n=TT1)
    em.op("dve", lambda eng: eng.memset(ln1_3[:, :, 0:1], 0.0), wr=[("lnxc0",)], tag="c0a")
    em.op("dve", lambda eng: eng.memset(ln2_3[:, :, 0:1], 0.0), wr=[("lnx2c0",)], tag="c0b")

    xT3 = dt["xT"].rearrange("(s p) n -> p s n", p=P)
    outT3 = outT.rearrange("(s p) n -> p s n", p=P)

    lncnt = [0]

    ln_ctx = {}

    def layer_norm(src, out_name, out_c0, tag, after_slab=None, part=0,
                   cast="sD", sqb="sG"):
        if part in (0, 1):
            si_ = lncnt[0] % 2
            lncnt[0] += 1
            ln_ctx[tag] = (si_, cast)
            LM, LV, LT = "lnm", "lnv", "lnt"
            LA, LB = "lnAb%d" % si_, "lnBb%d" % si_
            for g in range(NS):
                em.op("act", lambda eng, g=g: eng.activation(
                    out=fm(cast, g, 0, TT), in_=fm(src, g, 0, TT), func=AF.Copy),
                    rd=[(src, g)], wr=[(cast, g)], tag=tag + "cast%d" % g)
            for g in range(NS):
                em.op("act", lambda eng, g=g: eng.activation(
                    out=fm(sqb, g, 0, TT), in_=fm(cast, g, 0, TT), func=AF.Square),
                    rd=[(cast, g)], wr=[(sqb, g)], tag=tag + "sq%d" % g)
            for g in range(NS):
                em.op("pe", lambda eng, g=g: eng.matmul(
                    ps[4].ap()[0:1, 0:TT], A["ones1_s"][:, :], fm(cast, g, 0, TT),
                    start=(g == 0), stop=(g == NS - 1)),
                    rd=[(cast, g), ("ones1_s",)],
                    wr=[("pb4",)] if g in (0, NS - 1) else [], tag=tag + "sum%d" % g)
            for g in range(NS):
                em.op("pe", lambda eng, g=g: eng.matmul(
                    ps[5].ap()[0:1, 0:TT], A["ones1_s"][:, :], fm(sqb, g, 0, TT),
                    start=(g == 0), stop=(g == NS - 1)),
                    rd=[(sqb, g), ("ones1_s",)],
                    wr=[("pb5",)] if g in (0, NS - 1) else [], tag=tag + "ssq%d" % g)
            em.op("dve", lambda eng: eng.tensor_scalar_mul(
                out=A[LM][:, :], in0=ps[4].ap()[0:1, 0:TT], scalar1=1.0 / E),
                rd=[("pb4",)], wr=[(LM,)], tag=tag + "mean")
            em.op("dve", lambda eng: eng.tensor_mul(
                out=A[LT][:, :], in0=A[LM][:, :], in1=A[LM][:, :]),
                rd=[(LM,)], wr=[(LT,)], tag=tag + "msq")
            em.op("dve", lambda eng: eng.scalar_tensor_tensor(
                out=A[LV][:, :], in0=ps[5].ap()[0:1, 0:TT], scalar=1.0 / E,
                in1=A[LT][:, :], op0=MULT, op1=OP.subtract),
                rd=[("pb5",), (LT,)], wr=[(LV,)], tag=tag + "var")
            em.op("act", lambda eng: eng.activation(
                out=A[LT][:, :], in_=A[LV][:, :], func=AF.Ln, bias=A["cEps"][0:1, 0:1]),
                rd=[(LV,), ("cEps",)], wr=[(LT,)], tag=tag + "log")
            em.op("act", lambda eng: eng.activation(
                out=A[LA][:, :], in_=A[LT][:, :], func=AF.Exp, scale=-0.5),
                rd=[(LT,)], wr=[(LA,)], tag=tag + "rstd")
            em.op("dve", lambda eng: eng.tensor_mul(
                out=A[LB][:, :], in0=A[LM][:, :], in1=A[LA][:, :]),
                rd=[(LM,), (LA,)], wr=[(LB,)], tag=tag + "Bb")
        if part in (0, 2):
            si_, cast = ln_ctx[tag]
            LA, LB = "lnAb%d" % si_, "lnBb%d" % si_
            abo = si_ * 512
            em.op("pe", lambda eng: eng.matmul(
                ps[7].ap()[0:P, 0:TT], A["onesb_s"][:, :], A[LA][:, :], start=True, stop=True),
                rd=[(LA,), ("onesb_s",)], wr=[("pb7",)], tag=tag + "bcA")
            em.op("pe", lambda eng: eng.matmul(
                ps[2].ap()[0:P, 0:TT], A["onesb_s"][:, :], A[LB][:, :], start=True, stop=True),
                rd=[(LB,), ("onesb_s",)],
                wr=[("pb2",), ("pb2", "d0"), ("pb2", "d1")], tag=tag + "bcB")
            em.op("act", lambda eng: eng.activation(
                out=A["lnab"][:, abo:abo + TT], in_=ps[7].ap()[0:P, 0:TT], func=AF.Copy),
                rd=[("pb7",)], wr=[("lnabA", si_)], tag=tag + "evA")
            em.op("act", lambda eng: eng.activation(
                out=A["lnab"][:, abo + TT:abo + 2 * TT], in_=ps[2].ap()[0:P, 0:TT], func=AF.Copy),
                rd=[("pb2",)], wr=[("lnabB", si_)], tag=tag + "evB")
            for g in range(NS):
                em.op("dve", lambda eng, g=g: eng.tensor_mul(
                    out=fm(cast, g, 0, TT), in0=fm(cast, g, 0, TT), in1=A["lnab"][:, abo:abo + TT]),
                    rd=[(cast, g), ("lnabA", si_)], wr=[(cast, g)], tag=tag + "ap1_%d" % g)
                em.op("dve", lambda eng, g=g: eng.tensor_sub(
                    out=fm(out_name, g, out_c0, TT), in0=fm(cast, g, 0, TT),
                    in1=A["lnab"][:, abo + TT:abo + 2 * TT]),
                    rd=[(cast, g), ("lnabB", si_)], wr=[(out_name, g)], tag=tag + "ap2_%d" % g)
                if after_slab is not None:
                    after_slab(g)

    def ffn_mixes(src_ln, c0key, tag, bufR, bufK):
        xkeys = [("vtm", 0, 0), ("vtm", 0, 1), ("vtm", 1, 0), ("vtm", 1, 1)] if bufR == "sA" else []

        def mix(g):
            em.op("dve", lambda eng, g=g: eng.tensor_sub(
                out=fm("sG", g, 0, TT), in0=fm(src_ln, g, 1, TT), in1=fm(src_ln, g, 0, TT)),
                rd=[(src_ln, g), (c0key,)], wr=[("sG", g)], tag=tag + "d%d" % g)
            em.op("dve", lambda eng, g=g: eng.scalar_tensor_tensor(
                out=fm(bufR, g, 0, TT), in0=fm("sG", g, 0, TT),
                scalar=A["mu_r_s"][:, g:g + 1], in1=fm(src_ln, g, 0, TT), op0=MULT, op1=ADD),
                rd=[("sG", g), (src_ln, g), (c0key,), ("mu_r",)],
                wr=[(bufR, g)] + ([("vtm", g // 4, (g % 4) // 2)] if bufR == "sA" else []),
                tag=tag + "xr%d" % g)
            em.op("dve", lambda eng, g=g: eng.scalar_tensor_tensor(
                out=fm(bufK, g, 0, TT), in0=fm("sG", g, 0, TT),
                scalar=A["mu_k_s"][:, g:g + 1], in1=fm(src_ln, g, 0, TT), op0=MULT, op1=ADD),
                rd=[("sG", g), (src_ln, g), (c0key,), ("mu_k",)],
                wr=[(bufK, g)], tag=tag + "xk%d" % g)
        return mix

    def _proj(wname, rhs_name, out_cb, tag2):
        for bp in range(4):
            bank = nextbank6()
            for j in range(2):
                do = 2 * bp + j
                for c in range(NS):
                    em.op("pe", lambda eng, bank=bank, j=j, do=do, c=c: eng.matmul(
                        ps[bank].ap()[:, j * TT:j * TT + TT],
                        A[wname][:, c * E + do * P: c * E + do * P + P],
                        fm(rhs_name, c, 0, TT),
                        start=(c == 0), stop=(c == NS - 1)),
                        rd=[(rhs_name, c), (wname,)],
                        wr=[("pb%d" % bank,)] if ((c == 0 and j == 0) or (c == NS - 1 and j == 1)) else [],
                        tag=tag2 + "%d_%d_%d" % (bp, j, c))
            out_cb(bp, bank)

    def ffn_a(tag, bufR="sA", bufK="sB", rbuf="sH", kbuf="sC", k_first=False):
        def r_out(bp, bank):
            em.op("act", lambda eng, bp=bp, bank=bank: eng.activation(
                out=fm(rbuf, 2 * bp, 0, 2 * TT), in_=ps[bank].ap()[:, 0:2 * TT], func=AF.Sigmoid),
                rd=[("pb%d" % bank,)], wr=[(rbuf, 2 * bp), (rbuf, 2 * bp + 1)], tag=tag + "rs%d" % bp)

        def k_out(bp, bank):
            em.op("act", lambda eng, bp=bp, bank=bank: eng.activation(
                out=fm(kbuf, 2 * bp, 0, 2 * TT), in_=ps[bank].ap()[:, 0:2 * TT], func=AF.Relu),
                rd=[("pb%d" % bank,)], wr=[(kbuf, 2 * bp), (kbuf, 2 * bp + 1)], tag=tag + "kr%d" % bp)
        if k_first:
            _proj("wk_s", bufK, k_out, tag + "k")
            _proj("wr_s", bufR, r_out, tag + "r")
        else:
            _proj("wr_s", bufR, r_out, tag + "r")
            _proj("wk_s", bufK, k_out, tag + "k")

    def ffn_b(dst_name, resid_name, tag, rbuf="sH", kbuf="sC", ts_="st1"):
        for q4 in range(4):
            em.op("dve", lambda eng, q4=q4: eng.tensor_mul(
                out=A[kbuf][:, q4 * 2 * TT:(q4 + 1) * 2 * TT],
                in0=A[kbuf][:, q4 * 2 * TT:(q4 + 1) * 2 * TT],
                in1=A[kbuf][:, q4 * 2 * TT:(q4 + 1) * 2 * TT]),
                rd=[(kbuf, 2 * q4), (kbuf, 2 * q4 + 1)],
                wr=[(kbuf, 2 * q4), (kbuf, 2 * q4 + 1)], tag=tag + "k2_%d" % q4)

        def v_out(bp, bank):
            for j in range(2):
                do = 2 * bp + j
                em.op("dve", lambda eng, j=j, bank=bank, do=do: eng.tensor_mul(
                    out=A[ts_][:, 0:TT], in0=ps[bank].ap()[:, j * TT:j * TT + TT],
                    in1=fm(rbuf, do, 0, TT)),
                    rd=[("pb%d" % bank,), (rbuf, do)], wr=[(ts_, 0)], tag=tag + "vm%d_%d" % (bp, j))
                em.op("dve", lambda eng, do=do: eng.scalar_tensor_tensor(
                    out=fm(dst_name, do, 0, TT), in0=A[ts_][:, 0:TT], scalar=0.5,
                    in1=fm(resid_name, do, 0, TT), op0=MULT, op1=ADD),
                    rd=[(ts_, 0), (resid_name, do)], wr=[(dst_name, do)], tag=tag + "vo%d_%d" % (bp, j))
        _proj("wv_s", kbuf, v_out, tag + "v")

    def tile_ctx(t):
        xio = "xio0" if t % 2 == 0 else "xio1"
        return xio, t * TT, A[xio].rearrange("p (s n) -> p s n", n=TT)

    def emit_tile_front_a(t, mode):
        xio, base, xio3 = tile_ctx(t)
        em.op("sp", lambda eng, b=base, d=xio3: eng.dma_start(
            out=d[:, :, :], in_=xT3[:, :, b:b + TT]),
            wr=[(xio, g) for g in range(NS)], dma="sp", tag="t%dxin" % t)
        if t > 0:
            em.op("dve", lambda eng: eng.tensor_copy(
                out=ln1_3[:, :, 0:1], in_=ln1_3[:, :, TT:TT1]),
                rd=[("lnx", g) for g in range(NS)], wr=[("lnxc0",)], tag="t%dc0a" % t)
        layer_norm(xio, "lnx", 1, "t%dln1_" % t, part=1)

    def emit_tile_front_b(t, mode):
        layer_norm(None, "lnx", 1, "t%dln1_" % t, part=2,
                   after_slab=ffn_mixes("lnx", "lnxc0", "t%df1" % t, "sA", "sB"))

    def emit_tile_front2a(t, mode):
        ffn_a("t%df1" % t, bufR="sA", bufK="sB", rbuf="sB", kbuf="sD", k_first=True)

    def emit_tile_front2b(t, mode):
        xio, base, xio3 = tile_ctx(t)
        ffn_b(xio, xio, "t%df1" % t, rbuf="sB", kbuf="sD", ts_="st1")
        layer_norm(xio, "lnx", 0, "t%dln2_" % t, part=1)

    def emit_tile_front2c(t, mode):
        layer_norm(None, "lnx", 0, "t%dln2_" % t, part=2)

    def emit_tile_mid(t, mode):
        full = mode != "pre"
        xio, base, xio3 = tile_ctx(t)
        LN1 = "lnx"
        for g in range(NS):
            em.op("pe", lambda eng, g=g: eng.matmul(
                ps[7].ap()[0:D, 0:TT], A["wf1_s"][:, g * D:(g + 1) * D], fm(LN1, g, 0, TT),
                start=(g == 0), stop=(g == NS - 1)),
                rd=[(LN1, g), ("wf1_s",)],
                wr=[("pb7",)] if g in (0, NS - 1) else [], tag="t%dfm%d" % (t, g))
        em.op("act", lambda eng: eng.activation(
            out=A["mid2"][0:D, 0:TT], in_=ps[7].ap()[0:D, 0:TT], func=AF.Copy),
            rd=[("pb7",)], wr=[("mid2f",)], tag="t%dfmc" % t)
        if full:
            for g in range(NS):
                em.op("pe", lambda eng, g=g: eng.matmul(
                    ps[7].ap()[0:D, 256:256 + TT], A["wg1_s"][:, g * D:(g + 1) * D], fm(LN1, g, 0, TT),
                    start=(g == 0), stop=(g == NS - 1)),
                    rd=[(LN1, g), ("wg1_s",)],
                    wr=[("pb7",)] if g in (0, NS - 1) else [], tag="t%dgm%d" % (t, g))
            em.op("act", lambda eng: eng.activation(
                out=A["mid2"][0:D, TT:2 * TT], in_=ps[7].ap()[0:D, 256:256 + TT], func=AF.Copy),
                rd=[("pb7",)], wr=[("mid2g",)], tag="t%dgmc" % t)

        def aproj(wname, rhs_name, out_cb, tag2):
            for bp in range(4):
                bank = nextbank()
                for j in range(2):
                    do = 2 * bp + j
                    for c in range(NS):
                        em.op("pe", lambda eng, bank=bank, j=j, do=do, c=c: eng.matmul(
                            ps[bank].ap()[:, j * TT:j * TT + TT],
                            A[wname][:, c * E + do * P: c * E + do * P + P],
                            fm(rhs_name, c, 0, TT),
                            start=(c == 0), stop=(c == NS - 1)),
                            rd=[(rhs_name, c), (wname,)],
                            wr=[("pb%d" % bank,)] if ((c == 0 and j == 0) or (c == NS - 1 and j == 1)) else [],
                            tag=tag2 + "%d_%d_%d" % (bp, j, c))
                out_cb(bp, bank)

        def q_out(bp, bank):
            em.op("act", lambda eng, bp=bp, bank=bank: eng.activation(
                out=fm("sC", 2 * bp, 0, 2 * TT), in_=ps[bank].ap()[:, 0:2 * TT], func=AF.Silu),
                rd=[("pb%d" % bank,)], wr=[("sC", 2 * bp), ("sC", 2 * bp + 1)],
                tag="t%dqs%d" % (t, bp))
        if full:
            aproj("wq_s", LN1, q_out, "t%dq" % t)

        def ka_out(bp, bank):
            em.op("act", lambda eng, bp=bp, bank=bank: eng.activation(
                out=fm("sD", 2 * bp, 0, 2 * TT), in_=ps[bank].ap()[:, 0:2 * TT], func=AF.Silu),
                rd=[("pb%d" % bank,)], wr=[("sD", 2 * bp), ("sD", 2 * bp + 1)],
                tag="t%dks%d" % (t, bp))
        aproj("wka_s", LN1, ka_out, "t%dk" % t)

        for bp in range(4):
            bank = nextbank()
            for j in range(2):
                do = 2 * bp + j
                em.op("pe", lambda eng, bank=bank, j=j, do=do: eng.matmul(
                    ps[bank].ap()[:, j * TT:j * TT + TT],
                    A["wf2_s"][0:D, do * P:do * P + P], A["mid2"][0:D, 0:TT],
                    start=True, stop=True),
                    rd=[("mid2f",), ("wf2_s",)], wr=[("pb%d" % bank,)],
                    tag="t%dfz%d_%d" % (t, bp, j))
            em.op("act", lambda eng, bank=bank, bp=bp: eng.activation(
                out=fm("sH", 2 * bp, 0, 2 * TT), in_=ps[bank].ap()[:, 0:512], func=AF.Copy),
                rd=[("pb%d" % bank,)], wr=[("sH", 2 * bp), ("sH", 2 * bp + 1)], tag="t%dfze%d" % (t, bp))
        for bp in (range(4) if full else ()):
            bank = nextbank()
            for j in range(2):
                do = 2 * bp + j
                em.op("pe", lambda eng, bank=bank, j=j, do=do: eng.matmul(
                    ps[bank].ap()[:, j * TT:j * TT + TT],
                    A["wg2_s"][0:D, do * P:do * P + P], A["mid2"][0:D, TT:2 * TT],
                    start=True, stop=True),
                    rd=[("mid2g",), ("wg2_s",)], wr=[("pb%d" % bank,)],
                    tag="t%dgz%d_%d" % (t, bp, j))
            em.op("act", lambda eng, bank=bank, bp=bp: eng.activation(
                out=fm("sB", 2 * bp, 0, 2 * TT), in_=ps[bank].ap()[:, 0:512], func=AF.Sigmoid),
                rd=[("pb%d" % bank,)], wr=[("sB", 2 * bp), ("sB", 2 * bp + 1)],
                tag="t%dgs%d" % (t, bp))

        for bp in (range(4) if full else ()):
            bank = nextbank()
            for j in range(2):
                do = 2 * bp + j
                em.op("pe", lambda eng, bank=bank, j=j, do=do: eng.matmul(
                    ps[bank].ap()[:, j * TT:j * TT + TT],
                    A["wg2_s"][0:D, do * P:do * P + P], A["mid2"][0:D, TT:2 * TT],
                    start=True, stop=True),
                    rd=[("mid2g",), ("wg2_s",)], wr=[("pb%d" % bank,)],
                    tag="t%dgz%d_%d" % (t, bp, j))
            em.op("act", lambda eng, bank=bank, bp=bp: eng.activation(
                out=fm("sB", 2 * bp, 0, 2 * TT), in_=ps[bank].ap()[:, 0:512], func=AF.Sigmoid),
                rd=[("pb%d" % bank,)], wr=[("sB", 2 * bp), ("sB", 2 * bp + 1)],
                tag="t%dgs%d" % (t, bp))

        def emit_kp(bp):
            for g in (2 * bp, 2 * bp + 1):
                p_ = g % 2
                sidx = (bp % 2) * 2 + p_
                for c in range(C):
                    em.op("dve", lambda eng, g=g, c=c, sidx=sidx: eng.tensor_scalar_mul(
                        out=A["kp"][:, sidx * 256 + c * CH: sidx * 256 + c * CH + CH],
                        in0=fm("sD", g, c * CH, CH),
                        scalar1=A["lam"][:, g * C + c: g * C + c + 1]),
                        rd=[("sD", g), ("lam",)], wr=[("kp", sidx)], tag="t%dKp%d_%d" % (t, g, c))


        def emit_chain_pair(bp):
                for j in range(2):
                    g = 2 * bp + j
                    em.op("act", lambda eng, g=g, j=j: eng.activation(
                        out=A["st1"][:, j * TT:j * TT + TT], in_=fm("sH", g, 0, TT),
                        func=AF.Exp, scale=-1.0),
                        rd=[("sH", g)], wr=[("st1", j)], tag="t%dfe%d_%d" % (t, bp, j))
                    em.op("act", lambda eng, j=j: eng.activation(
                        out=A["st2"][:, j * TT:j * TT + TT], in_=A["st1"][:, j * TT:j * TT + TT],
                        func=AF.Ln, bias=A["cOne"][:, 0:1]),
                        rd=[("st1", j), ("cOne",)], wr=[("st2", j)], tag="t%dfl%d_%d" % (t, bp, j))
                    em.op("dve", lambda eng, g=g, j=j: eng.tensor_tensor_scan(
                        out=A["st1"][:, j * TT:j * TT + TT], data0=A["gate_s"][:, 0:TT],
                        data1=A["st2"][:, j * TT:j * TT + TT], initial=0.0, op0=MULT, op1=ADD),
                        rd=[("st2", j), ("gate_s",)], wr=[("st1", j)], tag="t%dsc%d" % (t, g))
                pr = [("sEx", 0, 2), ("sEx", 0, 3)]
                if full:
                    for j in range(2):
                        em.op("act", lambda eng, j=j: eng.activation(
                            out=A["sEx"][:, 512 + j * TT:512 + j * TT + TT],
                            in_=A["st1"][:, j * TT:j * TT + TT],
                            func=AF.Exp, scale=-1.0, bias=A["cLnS"][:, 0:1]),
                            rd=[("st1", j), ("cLnS",)], wr=[("sEx", 0, 2 + j)],
                            tag="t%dEp%d_%d" % (t, bp, j))
                    em.op("dve", lambda eng, bp=bp: eng.tensor_mul(
                        out=fm("sC", 2 * bp, 0, 2 * TT), in0=fm("sC", 2 * bp, 0, 2 * TT),
                        in1=A["sEx"][:, 512:1024]),
                        rd=[("sC", 2 * bp), ("sC", 2 * bp + 1)] + pr,
                        wr=[("sC", 2 * bp), ("sC", 2 * bp + 1)], tag="t%dQh%d" % (t, bp))
                em.op("act", lambda eng, bp=bp: eng.activation(
                    out=A["sEx"][:, 512:1024], in_=A["st1"][:, 0:512], func=AF.Exp),
                    rd=[("st1", 0), ("st1", 1)], wr=pr, tag="t%dEm%d" % (t, bp))
                em.op("dve", lambda eng, bp=bp: eng.tensor_mul(
                    out=fm("sD", 2 * bp, 0, 2 * TT), in0=fm("sD", 2 * bp, 0, 2 * TT),
                    in1=A["sEx"][:, 512:1024]),
                    rd=[("sD", 2 * bp), ("sD", 2 * bp + 1)] + pr,
                    wr=[("sD", 2 * bp), ("sD", 2 * bp + 1)], tag="t%dKh%d" % (t, bp))
                st1v = A["st1"].rearrange("p (s c l) -> p s c l", s=2, c=C)
                lam3 = A["lam"].rearrange("p (g c) -> p g c", g=NS)
                em.op("act", lambda eng, bp=bp, sv=st1v, l3=lam3: eng.activation(
                    out=l3[:, 2 * bp:2 * bp + 2, :], in_=sv[:, :, :, CH - 1],
                    func=AF.Exp, scale=-1.0),
                    rd=[("st1", 0), ("st1", 1)], wr=[("lam",)], tag="t%dlam%d" % (t, bp))
                if bp < 2:
                    emit_kp(bp)

        def emit_vtm_fill(sub, half):
            bank = nextbank()
            for c in range(NS):
                em.op("pe", lambda eng, bank=bank, sub=sub, half=half, c=c: eng.matmul(
                    ps[bank].ap()[:, 0:512], fm(LN1, c, sub * P, P),
                    A["wva_s"][:, c * E + half * 512: c * E + half * 512 + 512],
                    start=(c == 0), stop=(c == NS - 1)),
                    rd=[(LN1, c), ("wva_s",)],
                    wr=[("pb%d" % bank,)] if c in (0, NS - 1) else [],
                    tag="t%dvt%d_%d_%d" % (t, sub, half, c))
            em.op("dve", lambda eng, bank=bank, sub=sub, half=half: eng.tensor_copy(
                out=A["sA"][:, sub * E + half * 512: sub * E + half * 512 + 512],
                in_=ps[bank].ap()[:, 0:512]),
                rd=[("pb%d" % bank,)], wr=vtm_keys(sub, half),
                tag="t%dve%d_%d" % (t, sub, half))

        for p4 in range(4):
            emit_chain_pair(p4)
            emit_vtm_fill(p4 // 2, p4 % 2)

        slot0 = (4 * t) % 5

        late = []

        def scan_A(gp):
            pair = (2 * gp, 2 * gp + 1)
            kb = (gp % 2) * 1024
            for g in pair:
                b0 = 4 * (g % 2)
                for c in (range(C) if full else ()):
                    for j in range(2):
                        em.op("pe", lambda eng, g=g, c=c, j=j, b0=b0: eng.matmul(
                            ps[b0 + j].ap()[D * (c % 2):D * (c % 2) + D, c * CH: c * CH + CH],
                            fmh("sD", j, g, c * CH, CH), fmh("sC", j, g, c * CH, CH),
                            start=True, stop=True, tile_position=(D * j, D * (c % 2))),
                            rd=[("sD", g), ("sC", g)], wr=[("pb%d" % (b0 + j),)],
                            tag="t%dP%d_%d_%d" % (t, g, c, j))
            eb = (gp % 2) * 1024
            for g in (pair if full else ()):
                p_ = g % 2
                b0 = 4 * p_
                for j in range(2):
                    em.op("dve", lambda eng, j=j, b0=b0, p_=p_, eb=eb: eng.tensor_mul(
                        out=A["sEx"][:, eb + p_ * 512 + j * 256: eb + p_ * 512 + j * 256 + 256],
                        in0=ps[b0 + j].ap()[:, 0:256], in1=A["mask_s"][:, 0:256]),
                        rd=[("pb%d" % (b0 + j),), ("mask_s",)], wr=[("sEx", gp % 2, 2 * p_ + j)],
                        tag="t%dPM%d_%d" % (t, g, j))
            for g in pair:
                p_ = g % 2
                sidx = (gp % 2) * 2 + p_
                for c in range(C):
                    em.op("pe", lambda eng, g=g, c=c, p_=p_, sidx=sidx: eng.transpose(
                        ps[6].ap()[D * (c % 2):D * (c % 2) + D, p_ * 512 + c * P: p_ * 512 + c * P + P],
                        A["kp"][:, sidx * 256 + c * CH: sidx * 256 + c * CH + CH], A["ident_s"][:, :],
                        tile_position=(0, D * (c % 2))),
                        rd=[("kp", sidx), ("ident_s",)], wr=[("pt6", p_)], tag="t%dT%d_%d" % (t, g, c))
            for p_ in range(2):
                em.op("dve", lambda eng, p_=p_, kb=kb: eng.tensor_copy(
                    out=A["kt"][:, kb + p_ * 512: kb + p_ * 512 + 512],
                    in_=ps[6].ap()[:, p_ * 512: p_ * 512 + 512]),
                    rd=[("pt6", 0), ("pt6", 1)], wr=[("kt", gp % 2, p_)],
                    tag="t%dTe%d_%d" % (t, gp, p_))

        def scan_B(gp):
            pair = (2 * gp, 2 * gp + 1)
            kb = (gp % 2) * 1024
            eb = (gp % 2) * 1024
            for g in pair:
                p_ = g % 2
                for c in range(C):
                    for j in range(2):
                        hh = 2 * g + j
                        em.op("pe", lambda eng, g=g, c=c, j=j, p_=p_, hh=hh, kb=kb: eng.matmul(
                            ps[2 + (c % 2)].ap()[D * j:D * j + D,
                                                 p_ * 128 + (c // 2) * CH: p_ * 128 + (c // 2) * CH + CH],
                            A["kt"][D * (c % 2):D * (c % 2) + D,
                                    kb + p_ * 512 + c * P + j * D: kb + p_ * 512 + c * P + j * D + D],
                            A["sA"][D * (c % 2):D * (c % 2) + D,
                                    (c // 2) * E + hh * D: (c // 2) * E + hh * D + D],
                            start=True, stop=True, tile_position=(D * (c % 2), D * j)),
                            rd=[("kt", gp % 2, p_), ("vtm", c // 2, hh // 8)],
                            wr=[("pb%d" % (2 + c % 2),), ("pb%d" % (2 + c % 2), "d0"), ("pb%d" % (2 + c % 2), "d1")],
                            tag="t%dD%d_%d_%d" % (t, g, c, j))
            for g in pair:
                p_ = g % 2
                for c in range(C):
                    si = (slot0 + c) % 5
                    so = (slot0 + c + 1) % 5
                    em.op("dve", lambda eng, g=g, c=c, si=si, so=so, p_=p_: eng.scalar_tensor_tensor(
                        out=A["Ssb"][:, (g * 5 + so) * D: (g * 5 + so) * D + D],
                        in0=A["Ssb"][:, (g * 5 + si) * D: (g * 5 + si) * D + D],
                        scalar=A["lam"][:, g * C + c: g * C + c + 1],
                        in1=ps[2 + (c % 2)].ap()[:, p_ * 128 + (c // 2) * CH: p_ * 128 + (c // 2) * CH + CH],
                        op0=MULT, op1=ADD),
                        rd=[("S", g, si), ("lam",), ("pb%d" % (2 + c % 2), "d%d" % p_)],
                        wr=[("S", g, so)],
                        tag="t%dSu%d_%d" % (t, g, c))
            if gp < 2:
                late.append(lambda gp=gp: emit_kp(gp + 2))
            for g in (pair if full else ()):
                p_ = g % 2
                for c in range(C):
                    for j in range(2):
                        hh = 2 * g + j
                        em.op("pe", lambda eng, g=g, c=c, j=j, p_=p_, hh=hh: eng.matmul(
                            ps[c % 2].ap()[D * j:D * j + D,
                                           256 + p_ * 128 + (c // 2) * CH: 256 + p_ * 128 + (c // 2) * CH + CH],
                            A["sA"][D * (c % 2):D * (c % 2) + D,
                                    (c // 2) * E + hh * D: (c // 2) * E + hh * D + D],
                            A["sEx"][D * (c % 2):D * (c % 2) + D,
                                     eb + p_ * 512 + j * 256 + c * CH: eb + p_ * 512 + j * 256 + c * CH + CH],
                            start=True, stop=True, tile_position=(D * (c % 2), D * j)),
                            rd=[("vtm", c // 2, hh // 8), ("sEx", gp % 2, 2 * p_ + j)],
                            wr=[("pb%d" % (c % 2),)], tag="t%dOI%d_%d_%d" % (t, g, c, j))
            for g in (pair if full else ()):
                p_ = g % 2
                for c in range(C):
                    for j in range(2):
                        si = (slot0 + c) % 5
                        em.op("pe", lambda eng, g=g, c=c, j=j, si=si, p_=p_: eng.matmul(
                            ps[4 + j].ap()[D * j:D * j + D, p_ * 256 + c * CH: p_ * 256 + c * CH + CH],
                            A["Ssb"][D * j:D * j + D, (g * 5 + si) * D: (g * 5 + si) * D + D],
                            fmh("sC", j, g, c * CH, CH),
                            start=True, stop=True, tile_position=(D * j, D * j)),
                            rd=[("S", g, si), ("sC", g)],
                            wr=[("pb%d" % (4 + j),)],
                            tag="t%dOC%d_%d_%d" % (t, g, c, j))
            if full:
                def oie(g):
                    p_ = g % 2
                    sf3 = fm("sF", g, 0, TT).rearrange("p (a l) -> p a l", l=CH)
                    for p2 in range(2):
                        em.op("act", lambda eng, p2=p2, p_=p_, o3=sf3: eng.activation(
                            out=o3[:, p2::2, :],
                            in_=ps[p2].ap()[:, 256 + p_ * 128: 256 + p_ * 128 + 128], func=AF.Copy),
                            rd=[("pb%d" % p2,)], wr=[("sF", g)], tag="t%doie%d_%d" % (t, g, p2))

                def ocv(g):
                    p_ = g % 2
                    for j in range(2):
                        em.op("act", lambda eng, g=g, j=j, p_=p_: eng.activation(
                            out=fmh("sG", j, g, 0, TT),
                            in_=ps[4 + j].ap()[D * j:D * j + D, p_ * 256: p_ * 256 + 256],
                            func=AF.Copy),
                            rd=[("pb%d" % (4 + j),)], wr=[("sG", g)], tag="t%doc%d_%d" % (t, g, j))

                def omg(g):
                    em.op("dve", lambda eng, g=g: eng.tensor_add(
                        out=fm("sF", g, 0, TT), in0=fm("sF", g, 0, TT), in1=fm("sG", g, 0, TT)),
                        rd=[("sF", g), ("sG", g)], wr=[("sF", g)], tag="t%dom%d" % (t, g))
                    em.op("dve", lambda eng, g=g: eng.tensor_mul(
                        out=fm("sF", g, 0, TT), in0=fm("sF", g, 0, TT), in1=fm("sB", g, 0, TT)),
                        rd=[("sF", g), ("sB", g)], wr=[("sF", g)], tag="t%dog%d" % (t, g))
                g0, g1 = pair
                oie(g0)
                oie(g1)
                ocv(g0)
                ocv(g1)
                late.append(lambda g0=g0: omg(g0))
                late.append(lambda g1=g1: omg(g1))

        scan_A(0)
        scan_A(1)
        scan_B(0)
        for th in late:
            th()
        late = []
        scan_A(2)
        scan_B(1)
        for th in late:
            th()
        late = []
        scan_A(3)
        scan_B(2)
        for th in late:
            th()
        late = []
        scan_B(3)

        for th in late:
            th()
        late = []

    def emit_tile_back_a1(t, mode):
        full = mode != "pre"
        xio, base, xio3 = tile_ctx(t)
        if not full:
            return
        for g in range(NS):
            em.op("act", lambda eng, g=g: eng.activation(
                out=fm("sG", g, 0, TT), in_=fm("sF", g, 0, TT), func=AF.Square),
                rd=[("sF", g)], wr=[("sG", g)], tag="t%dq2_%d" % (t, g))
        for g in range(NS):
            em.op("pe", lambda eng, g=g: eng.matmul(
                ps[7].ap()[0:16, 0:TT], A["gnacc_s"][:, g * 16:(g + 1) * 16],
                fm("sG", g, 0, TT), start=(g == 0), stop=(g == NS - 1)),
                rd=[("sG", g), ("gnacc_s",)],
                wr=[("pb7",)] if g in (0, NS - 1) else [], tag="t%dgn%d" % (t, g))
        em.op("act", lambda eng: eng.activation(
            out=A["gns"][:, :], in_=ps[7].ap()[0:16, 0:TT], func=AF.Ln,
            scale=1.0 / D, bias=A["cEps"][0:16, 0:1]),
            rd=[("pb7",), ("cEps",)], wr=[("gns",)], tag="t%dgl" % t)
        em.op("act", lambda eng: eng.activation(
            out=A["gnsb"][:, :], in_=A["gns"][:, :], func=AF.Exp, scale=-0.5),
            rd=[("gns",)], wr=[("gnsb",)], tag="t%dgr" % t)
        for g in range(NS):
            bank = 7 if g % 2 == 0 else 2
            em.op("pe", lambda eng, g=g, bank=bank: eng.matmul(
                ps[bank].ap()[0:P, 0:TT], A["gnsel_s"][0:16, g * P:(g + 1) * P],
                A["gnsb"][0:16, 0:TT], start=True, stop=True),
                rd=[("gnsb",), ("gnsel_s",)],
                wr=[("pb%d" % bank,), ("pb%d" % bank, "d0"), ("pb%d" % bank, "d1")],
                tag="t%dgb%d" % (t, g))
            em.op("dve", lambda eng, g=g, bank=bank: eng.scalar_tensor_tensor(
                out=fm("sF", g, 0, TT), in0=fm("sF", g, 0, TT),
                scalar=A["nw_s"][:, g:g + 1], in1=ps[bank].ap()[0:P, 0:TT],
                op0=MULT, op1=MULT),
                rd=[("sF", g), ("nw",), ("pb%d" % bank,)], wr=[("sF", g)],
                tag="t%dga%d" % (t, g))
    def emit_tile_back_a2(t, mode):
        full = mode != "pre"
        xio, base, xio3 = tile_ctx(t)
        if not full:
            return
        for bp in range(4):
            bank = (4, 5, 0, 1)[bp]
            for j in range(2):
                do = 2 * bp + j
                for c in range(NS):
                    em.op("pe", lambda eng, bank=bank, j=j, do=do, c=c: eng.matmul(
                        ps[bank].ap()[:, j * TT:j * TT + TT],
                        A["wo_s"][:, c * E + do * P: c * E + do * P + P],
                        fm("sF", c, 0, TT), start=(c == 0), stop=(c == NS - 1)),
                        rd=[("sF", c), ("wo_s",)],
                        wr=[("pb%d" % bank,)] if ((c == 0 and j == 0) or (c == NS - 1 and j == 1)) else [],
                        tag="t%dwo%d_%d_%d" % (t, bp, j, c))
            for j in range(2):
                do = 2 * bp + j
                em.op("dve", lambda eng, bank=bank, j=j, do=do, xio=xio: eng.tensor_add(
                    out=fm(xio, do, 0, TT), in0=fm(xio, do, 0, TT),
                    in1=ps[bank].ap()[:, j * TT:j * TT + TT]),
                    rd=[("pb%d" % bank,), (xio, do)], wr=[(xio, do)],
                    tag="t%dx2_%d_%d" % (t, bp, j))
        if t > 0:
            em.op("dve", lambda eng: eng.tensor_copy(
                out=ln2_3[:, :, 0:1], in_=ln2_3[:, :, TT:TT1]),
                rd=[("lnx2", g) for g in range(NS)], wr=[("lnx2c0",)], tag="t%dc0b" % t)
        layer_norm(xio, "lnx2", 1, "t%dln3_" % t, part=1, cast="sH", sqb="sG")

    def emit_tile_back_b(t, mode):
        if mode == "pre":
            return
        layer_norm(None, "lnx2", 1, "t%dln3_" % t, part=2,
                   after_slab=ffn_mixes("lnx2", "lnx2c0", "t%df3" % t, "sC", "sF"))

    def emit_tile_back_c(t, mode):
        full = mode != "pre"
        xio, base, xio3 = tile_ctx(t)
        if full:
            ffn_a("t%df3" % t, bufR="sC", bufK="sF", rbuf="sH", kbuf="sC", k_first=False)
            ffn_b(xio, xio, "t%df3" % t, rbuf="sH", kbuf="sC", ts_="st2")
        if mode == "own":
            ob = base - (NTILE // 2) * TT
            em.op("sp", lambda eng, b=ob, s=xio3: eng.dma_start(
                out=outT3[:, :, b:b + TT], in_=s[:, :, :]),
                rd=[(xio, g) for g in range(NS)], dma="sp", tag="t%dout" % t)

    NPRE = NTILE // 2

    def mode_of(t):
        return "pre" if t < NPRE - 1 else ("pre_last" if t == NPRE - 1 else "own")

    emit_tile_front_a(0, mode_of(0))
    emit_tile_front_b(0, mode_of(0))
    emit_tile_front2a(0, mode_of(0))
    emit_tile_front2b(0, mode_of(0))
    emit_tile_front2c(0, mode_of(0))
    for t in range(NTILE):
        m1 = mode_of(t + 1) if t + 1 < NTILE else None
        emit_tile_mid(t, mode_of(t))
        if m1:
            emit_tile_front_a(t + 1, m1)
        emit_tile_back_a1(t, mode_of(t))
        if m1:
            emit_tile_front_b(t + 1, m1)
        emit_tile_back_a2(t, mode_of(t))
        if m1:
            emit_tile_front2a(t + 1, m1)
        if m1:
            emit_tile_front2b(t + 1, m1)
        emit_tile_back_b(t, mode_of(t))
        emit_tile_back_c(t, mode_of(t))
        if m1:
            emit_tile_front2c(t + 1, m1)

    if limit is not None:
        import sys
        for i, o in enumerate(em.ops):
            if i % 50 == 0:
                print("op[%d] %s %s" % (i, o.eng, o.tag), file=sys.stderr)
        em.ops = em.ops[:limit]
    em.finalize(sems, dsems)
    with nc.Block() as block:
        @block.sync
        def _(eng):
            em.emit("sp", eng)
            for q in sorted(dsems):
                if em.dcounts.get(q):
                    eng.wait_ge(dsems[q], em.dcounts[q])

        @block.tensor
        def _(eng):
            em.emit("pe", eng)

        @block.vector
        def _(eng):
            em.emit("dve", eng)

        @block.scalar
        def _(eng):
            em.emit("act", eng)

        @block.gpsimd
        def _(eng):
            em.emit("pool", eng)

    ctx.close()
    return nc


def _consts(TT):
    bf = ml_dtypes.bfloat16
    ident = np.eye(P, dtype=np.float32).astype(bf)
    col = np.arange(256) % CH
    row = np.arange(P) % CH
    mask = (row[:, None] <= col[None, :]).astype(np.float32).astype(bf)
    gate = np.ascontiguousarray(np.ones((P, 1), np.float32) * ((np.arange(TT) % CH) != 0))
    gate = gate.astype(bf)
    ones1 = np.ones((P, 1), np.float32).astype(bf)
    onesb = np.ones((1, P), np.float32).astype(bf)
    gnacc = np.zeros((P, P), np.float32)
    for g in range(NS):
        for p in range(P):
            gnacc[p, g * 16 + 2 * g + p // D] = 1
    gnsel = np.zeros((16, NS * P), np.float32)
    for g in range(NS):
        for p in range(P):
            gnsel[2 * g + p // D, g * P + p] = 1
    return dict(ident=ident, mask=mask, gate=gate, ones1=ones1, onesb=onesb,
                gnacc=gnacc.astype(bf), gnsel=gnsel.astype(bf))


def _wmaps(inputs):
    def c(x):
        return np.ascontiguousarray(np.asarray(x, dtype=np.float32))
    return {
        "wr": c(np.asarray(inputs["L0_ffn_W_r"]).T),
        "wk": c(np.asarray(inputs["L0_ffn_W_k"]).T),
        "wv": c(np.asarray(inputs["L0_ffn_W_v"]).T),
        "wq": c(inputs["L0_attn_Wq"]),
        "wka": c(inputs["L0_attn_Wk"]),
        "wva": c(inputs["L0_attn_Wv"]),
        "wo": c(inputs["L0_attn_Wo"]),
        "wf1": c(inputs["L0_attn_Wf1"]),
        "wf2": c(inputs["L0_attn_Wf2"]),
        "wg1": c(inputs["L0_attn_Wg1"]),
        "wg2": c(inputs["L0_attn_Wg2"]),
        "mu_r": c(np.asarray(inputs["L0_ffn_mu_r"]).reshape(NS, P).T),
        "mu_k": c(np.asarray(inputs["L0_ffn_mu_k"]).reshape(NS, P).T),
        "nw": c(np.asarray(inputs["L0_attn_norm_weight"]).reshape(NS, P).T),
    }


def kernel(**inputs):
    x = np.asarray(inputs["L0_attn_x"], dtype=np.float32)
    B, T, _ = x.shape
    TT = 256
    HALF = T // 2
    nc = build_program(T, TT)
    wm = _wmaps(inputs)
    wm.update(_consts(TT))
    in_maps = []
    for cid in range(8):
        b, h = cid // 2, cid % 2
        mp = dict(wm)
        xt = np.asarray(x[b]).T
        if h == 1:
            mp["xT"] = np.ascontiguousarray(xt)
        else:
            mp["xT"] = np.ascontiguousarray(
                np.concatenate([np.zeros((E, HALF), np.float32), xt[:, :HALF]], axis=1))
        in_maps.append(mp)
    res = run_bass_kernel_spmd(nc, in_maps, core_ids=list(range(8)))
    out = np.empty((B, T, E), dtype=np.float32)
    for b in range(B):
        out[b, :HALF] = res.results[2 * b]["outT"].T
        out[b, HALF:] = res.results[2 * b + 1]["outT"].T
    return out


# version 36
# speedup vs baseline: 1.0908x; 1.0026x over previous
import math
import numpy as np
import ml_dtypes
import concourse.bass as bass
import concourse.mybir as mybir
from concourse.bass_utils import run_bass_kernel_spmd

F32 = mybir.dt.float32
BF16 = mybir.dt.bfloat16
AF = mybir.ActivationFunctionType
OP = mybir.AluOpType
MULT = OP.mult
ADD = OP.add

E = 1024
P = 128
NS = 8
D = 64
H = 16
CH = 64
EPS = 1e-5
SCALE = D ** -0.5

ENGINES = ("sp", "pe", "act", "dve", "pool")


class _Op:
    __slots__ = ("idx", "eng", "fn", "deps", "dma", "need_inc", "inc_val", "waits", "tag")

    def __init__(self, idx, eng, fn, dma, tag):
        self.idx = idx
        self.eng = eng
        self.fn = fn
        self.deps = set()
        self.dma = dma
        self.need_inc = False
        self.inc_val = None
        self.waits = []
        self.tag = tag


class Emitter:
    def __init__(self):
        self.ops = []
        self.writer = {}
        self.readers = {}

    def op(self, eng, fn, rd=(), wr=(), dma=None, tag=""):
        o = _Op(len(self.ops), eng, fn, dma, tag)
        for r in rd:
            w = self.writer.get(r)
            if w is not None:
                o.deps.add(w)
            self.readers.setdefault(r, []).append(o.idx)
        for r in wr:
            w = self.writer.get(r)
            if w is not None:
                o.deps.add(w)
            for ri in self.readers.get(r, ()):
                o.deps.add(ri)
            self.writer[r] = o.idx
            self.readers[r] = []
        o.deps.discard(o.idx)
        self.ops.append(o)
        return o

    def finalize(self, sems, dsems):
        for o in self.ops:
            for d in o.deps:
                p = self.ops[d]
                if p.dma is not None or p.eng != o.eng:
                    p.need_inc = True
        counts = {e: 0 for e in ENGINES}
        dcounts = {q: 0 for q in dsems}
        for o in self.ops:
            if o.dma is not None:
                dcounts[o.dma] += 16
                o.inc_val = dcounts[o.dma]
            elif o.need_inc:
                counts[o.eng] += 1
                o.inc_val = counts[o.eng]
        for o in self.ops:
            need = {}
            if o.dma is not None and o.inc_val > 16:
                need[("d", o.dma)] = o.inc_val - 16
            for d in sorted(o.deps):
                p = self.ops[d]
                if p.dma is not None:
                    k = ("d", p.dma)
                    need[k] = max(need.get(k, 0), p.inc_val)
                elif p.eng != o.eng:
                    k = ("e", p.eng)
                    need[k] = max(need.get(k, 0), p.inc_val)
            o.waits = sorted(need.items())
        self._sems = sems
        self._dsems = dsems
        self.dcounts = dcounts

    def emit(self, eng_name, eng):
        water = {}
        for o in self.ops:
            if o.eng != eng_name:
                continue
            for (kind, key), val in o.waits:
                sem = self._dsems[key] if kind == "d" else self._sems[key]
                if val > water.get((kind, key), 0):
                    eng.wait_ge(sem, val)
                    water[(kind, key)] = val
            inst = o.fn(eng)
            if o.dma is not None:
                inst.then_inc(self._dsems[o.dma], 16)
            elif o.need_inc:
                inst.then_inc(self._sems[eng_name], 1)


def build_program(T, TT, limit=None):
    assert T % TT == 0 and TT % CH == 0 and TT % P == 0
    NTILE = T // TT
    C = TT // CH
    TT1 = TT + 1
    NSUB = TT // P
    nc = bass.Bass("TRN2", target_bir_lowering=False, debug=False,
                   detect_race_conditions=False)

    dt = {}

    def din(name, shape, dtype=F32):
        dt[name] = nc.dram_tensor(name, list(shape), dtype, kind="ExternalInput").ap()
        return dt[name]

    din("xT", (E, T))
    outT = nc.dram_tensor("outT", [E, T // 2], F32, kind="ExternalOutput").ap()
    for w in ("wr", "wk", "wv", "wq", "wka", "wva", "wo"):
        din(w, (E, E))
    din("wf1", (E, D)); din("wg1", (E, D))
    din("wf2", (D, E)); din("wg2", (D, E))
    din("mu_r", (P, NS)); din("mu_k", (P, NS)); din("nw", (P, NS))
    din("ident", (P, P), BF16)
    din("mask", (P, 256), BF16)
    din("gate", (P, TT), BF16)
    din("ones1", (P, 1), BF16)
    din("onesb", (1, P), BF16)
    din("gnacc", (P, P), BF16)
    din("gnsel", (16, NS * P), BF16)

    em = Emitter()
    import contextlib
    ctx = contextlib.ExitStack()
    sb = {}

    def sbuf(name, shape, dtype):
        sb[name] = ctx.enter_context(nc.sbuf_tensor(name, list(shape), dtype))

    for w in ("wr", "wk", "wv", "wq", "wka", "wva", "wo"):
        sbuf(w + "_s", (P, NS * E), BF16)
    sbuf("wf1_s", (P, NS * D), BF16); sbuf("wg1_s", (P, NS * D), BF16)
    sbuf("wf2_s", (D, E), BF16); sbuf("wg2_s", (D, E), BF16)
    for v in ("mu_r", "mu_k", "nw"):
        sbuf(v + "_s", (P, NS), F32)
    sbuf("ident_s", (P, P), BF16)
    sbuf("mask_s", (P, 256), BF16)
    sbuf("gate_s", (P, TT), BF16)
    sbuf("ones1_s", (P, 1), BF16)
    sbuf("onesb_s", (1, P), BF16)
    sbuf("gnacc_s", (P, P), BF16)
    sbuf("gnsel_s", (16, NS * P), BF16)

    sbuf("xio0", (P, NS * TT), F32)
    sbuf("xio1", (P, NS * TT), F32)
    sbuf("lnx", (P, NS * TT1), BF16)
    sbuf("lnx2", (P, NS * TT1), BF16)
    for s in ("sA", "sB", "sC", "sD", "sF", "sG", "sH"):
        sbuf(s, (P, NS * TT), BF16)
    sbuf("sEx", (P, 2048), BF16)
    sbuf("kt", (P, 2048), BF16)
    sbuf("st1", (P, 512), F32)
    sbuf("st2", (P, 512), F32)
    sbuf("Ssb", (P, NS * 5 * D), BF16)
    sbuf("lam", (P, NS * C), F32)
    sbuf("mid2", (D, 2 * TT), BF16)
    sbuf("lnm", (1, TT), F32); sbuf("lnv", (1, TT), F32); sbuf("lnt", (1, TT), F32)
    for s_ in (0, 1):
        sbuf("lnAb%d" % s_, (1, TT), BF16); sbuf("lnBb%d" % s_, (1, TT), BF16)
    sbuf("gns", (16, TT), F32); sbuf("gnsb", (16, TT), BF16)
    sbuf("cEps", (P, 1), F32); sbuf("cOne", (P, 1), F32); sbuf("cLnS", (P, 1), F32)
    sbuf("kp", (P, 1024), BF16)
    sbuf("lnab", (P, 4 * TT), BF16)

    ps = {}
    for i in range(6):
        ps[i] = ctx.enter_context(nc.psum_tensor("pb%d" % i, [P, 512], F32))
    ps[6] = ctx.enter_context(nc.psum_tensor("pt6", [P, 1024], BF16))
    ps[7] = ctx.enter_context(nc.psum_tensor("pb7", [P, 512], F32))

    sems = {e: ctx.enter_context(nc.semaphore("sem_" + e)) for e in ENGINES}
    dsems = {"sp": ctx.enter_context(nc.semaphore("dma_sp")),
             "spc": ctx.enter_context(nc.semaphore("dma_spc")),
             "pool": ctx.enter_context(nc.semaphore("dma_pool"))}
    for _w in ("wr", "wk", "wv", "wq", "wka", "wva", "wo", "wsm"):
        dsems["w_" + _w] = ctx.enter_context(nc.semaphore("dma_" + _w))

    A = {k: v.ap() for k, v in sb.items()}
    rotc = [0]

    def nextbank():
        b = (4, 5, 7)[rotc[0] % 3]
        rotc[0] += 1
        return b

    rotc6 = [0]

    def nextbank6():
        b = (4, 5, 7, 0, 1, 3)[rotc6[0] % 6]
        rotc6[0] += 1
        return b
    STRIDE1 = ("lnx", "lnx2")

    def fm(name, g, c0, n):
        stride = TT1 if name in STRIDE1 else TT
        return A[name][:, g * stride + c0: g * stride + c0 + n]

    def fmh(name, j, g, c0, n):
        stride = TT1 if name in STRIDE1 else TT
        return A[name][j * D:(j + 1) * D, g * stride + c0: g * stride + c0 + n]

    def vtm_keys(sub, half):
        return [("sA", 4 * sub + 2 * half), ("sA", 4 * sub + 2 * half + 1), ("vtm", sub, half)]

    def pm_keys(p_):
        return [("sEx", 2 * p_), ("sEx", 2 * p_ + 1)]

    def wload(name, src_key, rearr, q):
        src = dt[src_key]
        dst = A[name]
        if rearr:
            src = src.rearrange("(c p) d -> p c d", p=P)
            dst = dst.rearrange("p (c d) -> p c d", c=NS)
        eng_name = "sp" if q == "spc" else "pool"
        em.op(eng_name, lambda eng, s=src, d=dst: eng.dma_start(out=d, in_=s),
              wr=[(name,)], dma=q, tag="wload:" + name)

    _prev_w = [None]

    def wload_chain(name, src_key, rearr, q):
        wload(name, src_key, rearr, q)
        if _prev_w[0] is not None:
            em.ops[-1].deps.add(_prev_w[0].idx)
        _prev_w[0] = em.ops[-1]

    def wload_half(name, src_key, q, h):
        src = dt[src_key].rearrange("(c p) d -> p c d", p=P)
        dst = A[name].rearrange("p (c d) -> p c d", c=NS)
        keys = [(name, h)] if h == 0 else [(name,), (name, 1)]
        em.op("pool", lambda eng, s=src, d=dst, h=h: eng.dma_start(
            out=d[:, 4 * h:4 * h + 4, :], in_=s[:, 4 * h:4 * h + 4, :]),
            wr=keys, dma=q, tag="wload:%s:%d" % (name, h))
        if _prev_w[0] is not None:
            em.ops[-1].deps.add(_prev_w[0].idx)
        _prev_w[0] = em.ops[-1]

    for w in ("wk", "wr", "wv"):
        for h in (0, 1):
            wload_half(w + "_s", w, "w_" + w, h)
    wload_chain("wf1_s", "wf1", True, "w_wsm")
    wload_chain("wf2_s", "wf2", False, "w_wsm")
    for w in ("wka", "wva", "wq", "wo"):
        wload_chain(w + "_s", w, True, "w_" + w)
    wload_chain("wg1_s", "wg1", True, "w_wsm")
    wload_chain("wg2_s", "wg2", False, "w_wsm")
    for v in ("mu_r", "mu_k", "nw"):
        em.op("sp", lambda eng, s=dt[v], d=A[v + "_s"]: eng.dma_start(out=d, in_=s),
              wr=[(v,)], dma="spc", tag="vload:" + v)
    for cname in ("ones1", "onesb", "gate", "mask", "ident", "gnacc", "gnsel"):
        wload(cname + "_s", cname, False, "spc")

    em.op("dve", lambda eng: eng.memset(A["Ssb"][:, :], 0.0),
          wr=[("S", g, s5) for g in range(NS) for s5 in range(5)], tag="S0")
    em.op("dve", lambda eng: eng.memset(A["st1"][:, :], 0.0), wr=[("st1", 0), ("st1", 1)], tag="z0")
    em.op("dve", lambda eng: eng.memset(A["cEps"][:, :], EPS), wr=[("cEps",)], tag="zE")
    em.op("dve", lambda eng: eng.memset(A["cOne"][:, :], 1.0), wr=[("cOne",)], tag="z1")
    em.op("dve", lambda eng: eng.memset(A["cLnS"][:, :], math.log(SCALE)), wr=[("cLnS",)], tag="zS")
    for i in (0, 1, 2, 3):
        em.op("dve", lambda eng, i=i: eng.tensor_copy(out=ps[i].ap()[:, :], in_=A["st1"][:, :]),
              rd=[("st1", 0), ("st1", 1)],
              wr=[("pb%d" % i,), ("pb%d" % i, "d0"), ("pb%d" % i, "d1")], tag="psz%d" % i)
    em.op("dve", lambda eng: eng.memset(A["sEx"][:, 0:512], 0.0),
          wr=[("sEx", 0), ("sEx", 1)], tag="zb")
    for k in range(8):
        em.op("pe", lambda eng, k=k: eng.transpose(
            ps[6].ap()[:, k * P:(k + 1) * P], A["sEx"][:, 0:P], A["ident_s"][:, :]),
            rd=[("sEx", 0), ("sEx", 1), ("ident_s",)], wr=[("pt6", k // 4)], tag="ptz%d" % k)
    ln1_3 = A["lnx"].rearrange("p (s n) -> p s n", n=TT1)
    ln2_3 = A["lnx2"].rearrange("p (s n) -> p s n", n=TT1)
    em.op("dve", lambda eng: eng.memset(ln1_3[:, :, 0:1], 0.0), wr=[("lnxc0",)], tag="c0a")
    em.op("dve", lambda eng: eng.memset(ln2_3[:, :, 0:1], 0.0), wr=[("lnx2c0",)], tag="c0b")

    xT3 = dt["xT"].rearrange("(s p) n -> p s n", p=P)
    outT3 = outT.rearrange("(s p) n -> p s n", p=P)

    lncnt = [0]

    ln_ctx = {}

    def layer_norm(src, out_name, out_c0, tag, after_slab=None, part=0,
                   cast="sD", sqb="sG"):
        if part in (0, 1):
            si_ = lncnt[0] % 2
            lncnt[0] += 1
            ln_ctx[tag] = (si_, cast)
            LM, LV, LT = "lnm", "lnv", "lnt"
            LA, LB = "lnAb%d" % si_, "lnBb%d" % si_
            for g in range(NS):
                em.op("act", lambda eng, g=g: eng.activation(
                    out=fm(cast, g, 0, TT), in_=fm(src, g, 0, TT), func=AF.Copy),
                    rd=[(src, g)], wr=[(cast, g)], tag=tag + "cast%d" % g)
            for g in range(NS):
                em.op("act", lambda eng, g=g: eng.activation(
                    out=fm(sqb, g, 0, TT), in_=fm(cast, g, 0, TT), func=AF.Square),
                    rd=[(cast, g)], wr=[(sqb, g)], tag=tag + "sq%d" % g)
            for g in range(NS):
                em.op("pe", lambda eng, g=g: eng.matmul(
                    ps[4].ap()[0:1, 0:TT], A["ones1_s"][:, :], fm(cast, g, 0, TT),
                    start=(g == 0), stop=(g == NS - 1)),
                    rd=[(cast, g), ("ones1_s",)],
                    wr=[("pb4",)] if g in (0, NS - 1) else [], tag=tag + "sum%d" % g)
            for g in range(NS):
                em.op("pe", lambda eng, g=g: eng.matmul(
                    ps[5].ap()[0:1, 0:TT], A["ones1_s"][:, :], fm(sqb, g, 0, TT),
                    start=(g == 0), stop=(g == NS - 1)),
                    rd=[(sqb, g), ("ones1_s",)],
                    wr=[("pb5",)] if g in (0, NS - 1) else [], tag=tag + "ssq%d" % g)
            em.op("dve", lambda eng: eng.tensor_scalar_mul(
                out=A[LM][:, :], in0=ps[4].ap()[0:1, 0:TT], scalar1=1.0 / E),
                rd=[("pb4",)], wr=[(LM,)], tag=tag + "mean")
            em.op("dve", lambda eng: eng.tensor_mul(
                out=A[LT][:, :], in0=A[LM][:, :], in1=A[LM][:, :]),
                rd=[(LM,)], wr=[(LT,)], tag=tag + "msq")
            em.op("dve", lambda eng: eng.scalar_tensor_tensor(
                out=A[LV][:, :], in0=ps[5].ap()[0:1, 0:TT], scalar=1.0 / E,
                in1=A[LT][:, :], op0=MULT, op1=OP.subtract),
                rd=[("pb5",), (LT,)], wr=[(LV,)], tag=tag + "var")
            em.op("act", lambda eng: eng.activation(
                out=A[LT][:, :], in_=A[LV][:, :], func=AF.Ln, bias=A["cEps"][0:1, 0:1]),
                rd=[(LV,), ("cEps",)], wr=[(LT,)], tag=tag + "log")
            em.op("act", lambda eng: eng.activation(
                out=A[LA][:, :], in_=A[LT][:, :], func=AF.Exp, scale=-0.5),
                rd=[(LT,)], wr=[(LA,)], tag=tag + "rstd")
            em.op("dve", lambda eng: eng.tensor_mul(
                out=A[LB][:, :], in0=A[LM][:, :], in1=A[LA][:, :]),
                rd=[(LM,), (LA,)], wr=[(LB,)], tag=tag + "Bb")
        if part in (0, 2):
            si_, cast = ln_ctx[tag]
            LA, LB = "lnAb%d" % si_, "lnBb%d" % si_
            abo = si_ * 512
            em.op("pe", lambda eng: eng.matmul(
                ps[7].ap()[0:P, 0:TT], A["onesb_s"][:, :], A[LA][:, :], start=True, stop=True),
                rd=[(LA,), ("onesb_s",)], wr=[("pb7",)], tag=tag + "bcA")
            em.op("pe", lambda eng: eng.matmul(
                ps[2].ap()[0:P, 0:TT], A["onesb_s"][:, :], A[LB][:, :], start=True, stop=True),
                rd=[(LB,), ("onesb_s",)],
                wr=[("pb2",), ("pb2", "d0"), ("pb2", "d1")], tag=tag + "bcB")
            em.op("act", lambda eng: eng.activation(
                out=A["lnab"][:, abo:abo + TT], in_=ps[7].ap()[0:P, 0:TT], func=AF.Copy),
                rd=[("pb7",)], wr=[("lnabA", si_)], tag=tag + "evA")
            em.op("act", lambda eng: eng.activation(
                out=A["lnab"][:, abo + TT:abo + 2 * TT], in_=ps[2].ap()[0:P, 0:TT], func=AF.Copy),
                rd=[("pb2",)], wr=[("lnabB", si_)], tag=tag + "evB")
            for g in range(NS):
                em.op("dve", lambda eng, g=g: eng.tensor_mul(
                    out=fm(cast, g, 0, TT), in0=fm(cast, g, 0, TT), in1=A["lnab"][:, abo:abo + TT]),
                    rd=[(cast, g), ("lnabA", si_)], wr=[(cast, g)], tag=tag + "ap1_%d" % g)
                em.op("dve", lambda eng, g=g: eng.tensor_sub(
                    out=fm(out_name, g, out_c0, TT), in0=fm(cast, g, 0, TT),
                    in1=A["lnab"][:, abo + TT:abo + 2 * TT]),
                    rd=[(cast, g), ("lnabB", si_)], wr=[(out_name, g)], tag=tag + "ap2_%d" % g)
                if after_slab is not None:
                    after_slab(g)

    def ffn_mixes(src_ln, c0key, tag, bufR, bufK):
        xkeys = [("vtm", 0, 0), ("vtm", 0, 1), ("vtm", 1, 0), ("vtm", 1, 1)] if bufR == "sA" else []

        def mix(g):
            em.op("dve", lambda eng, g=g: eng.tensor_sub(
                out=fm("sG", g, 0, TT), in0=fm(src_ln, g, 1, TT), in1=fm(src_ln, g, 0, TT)),
                rd=[(src_ln, g), (c0key,)], wr=[("sG", g)], tag=tag + "d%d" % g)
            em.op("dve", lambda eng, g=g: eng.scalar_tensor_tensor(
                out=fm(bufR, g, 0, TT), in0=fm("sG", g, 0, TT),
                scalar=A["mu_r_s"][:, g:g + 1], in1=fm(src_ln, g, 0, TT), op0=MULT, op1=ADD),
                rd=[("sG", g), (src_ln, g), (c0key,), ("mu_r",)],
                wr=[(bufR, g)] + ([("vtm", g // 4, (g % 4) // 2)] if bufR == "sA" else []),
                tag=tag + "xr%d" % g)
            em.op("dve", lambda eng, g=g: eng.scalar_tensor_tensor(
                out=fm(bufK, g, 0, TT), in0=fm("sG", g, 0, TT),
                scalar=A["mu_k_s"][:, g:g + 1], in1=fm(src_ln, g, 0, TT), op0=MULT, op1=ADD),
                rd=[("sG", g), (src_ln, g), (c0key,), ("mu_k",)],
                wr=[(bufK, g)], tag=tag + "xk%d" % g)
        return mix

    def _proj(wname, rhs_name, out_cb, tag2):
        for bp in range(4):
            bank = nextbank6()
            for j in range(2):
                do = 2 * bp + j
                for c in range(NS):
                    em.op("pe", lambda eng, bank=bank, j=j, do=do, c=c: eng.matmul(
                        ps[bank].ap()[:, j * TT:j * TT + TT],
                        A[wname][:, c * E + do * P: c * E + do * P + P],
                        fm(rhs_name, c, 0, TT),
                        start=(c == 0), stop=(c == NS - 1)),
                        rd=[(rhs_name, c), (wname, c // 4)],
                        wr=[("pb%d" % bank,)] if ((c == 0 and j == 0) or (c == NS - 1 and j == 1)) else [],
                        tag=tag2 + "%d_%d_%d" % (bp, j, c))
            out_cb(bp, bank)

    def ffn_a(tag, bufR="sA", bufK="sB", rbuf="sH", kbuf="sC", k_first=False):
        def r_out(bp, bank):
            em.op("act", lambda eng, bp=bp, bank=bank: eng.activation(
                out=fm(rbuf, 2 * bp, 0, 2 * TT), in_=ps[bank].ap()[:, 0:2 * TT], func=AF.Sigmoid),
                rd=[("pb%d" % bank,)], wr=[(rbuf, 2 * bp), (rbuf, 2 * bp + 1)], tag=tag + "rs%d" % bp)

        def k_out(bp, bank):
            em.op("act", lambda eng, bp=bp, bank=bank: eng.activation(
                out=fm(kbuf, 2 * bp, 0, 2 * TT), in_=ps[bank].ap()[:, 0:2 * TT], func=AF.Relu),
                rd=[("pb%d" % bank,)], wr=[(kbuf, 2 * bp), (kbuf, 2 * bp + 1)], tag=tag + "kr%d" % bp)
        if k_first:
            _proj("wk_s", bufK, k_out, tag + "k")
            _proj("wr_s", bufR, r_out, tag + "r")
        else:
            _proj("wr_s", bufR, r_out, tag + "r")
            _proj("wk_s", bufK, k_out, tag + "k")

    def ffn_b(dst_name, resid_name, tag, rbuf="sH", kbuf="sC", ts_="st1"):
        for q4 in range(4):
            em.op("dve", lambda eng, q4=q4: eng.tensor_mul(
                out=A[kbuf][:, q4 * 2 * TT:(q4 + 1) * 2 * TT],
                in0=A[kbuf][:, q4 * 2 * TT:(q4 + 1) * 2 * TT],
                in1=A[kbuf][:, q4 * 2 * TT:(q4 + 1) * 2 * TT]),
                rd=[(kbuf, 2 * q4), (kbuf, 2 * q4 + 1)],
                wr=[(kbuf, 2 * q4), (kbuf, 2 * q4 + 1)], tag=tag + "k2_%d" % q4)

        def v_out(bp, bank):
            for j in range(2):
                do = 2 * bp + j
                em.op("dve", lambda eng, j=j, bank=bank, do=do: eng.tensor_mul(
                    out=A[ts_][:, 0:TT], in0=ps[bank].ap()[:, j * TT:j * TT + TT],
                    in1=fm(rbuf, do, 0, TT)),
                    rd=[("pb%d" % bank,), (rbuf, do)], wr=[(ts_, 0)], tag=tag + "vm%d_%d" % (bp, j))
                em.op("dve", lambda eng, do=do: eng.scalar_tensor_tensor(
                    out=fm(dst_name, do, 0, TT), in0=A[ts_][:, 0:TT], scalar=0.5,
                    in1=fm(resid_name, do, 0, TT), op0=MULT, op1=ADD),
                    rd=[(ts_, 0), (resid_name, do)], wr=[(dst_name, do)], tag=tag + "vo%d_%d" % (bp, j))
        _proj("wv_s", kbuf, v_out, tag + "v")

    def tile_ctx(t):
        xio = "xio0" if t % 2 == 0 else "xio1"
        return xio, t * TT, A[xio].rearrange("p (s n) -> p s n", n=TT)

    def emit_tile_front_a(t, mode):
        xio, base, xio3 = tile_ctx(t)
        em.op("sp", lambda eng, b=base, d=xio3: eng.dma_start(
            out=d[:, :, :], in_=xT3[:, :, b:b + TT]),
            wr=[(xio, g) for g in range(NS)], dma="sp", tag="t%dxin" % t)
        if t > 0:
            em.op("dve", lambda eng: eng.tensor_copy(
                out=ln1_3[:, :, 0:1], in_=ln1_3[:, :, TT:TT1]),
                rd=[("lnx", g) for g in range(NS)], wr=[("lnxc0",)], tag="t%dc0a" % t)
        layer_norm(xio, "lnx", 1, "t%dln1_" % t, part=1)

    def emit_tile_front_b(t, mode):
        layer_norm(None, "lnx", 1, "t%dln1_" % t, part=2,
                   after_slab=ffn_mixes("lnx", "lnxc0", "t%df1" % t, "sA", "sB"))

    def emit_tile_front2a(t, mode):
        ffn_a("t%df1" % t, bufR="sA", bufK="sB", rbuf="sB", kbuf="sD", k_first=True)

    def emit_tile_front2b(t, mode):
        xio, base, xio3 = tile_ctx(t)
        ffn_b(xio, xio, "t%df1" % t, rbuf="sB", kbuf="sD", ts_="st1")
        layer_norm(xio, "lnx", 0, "t%dln2_" % t, part=1)

    def emit_tile_front2c(t, mode):
        layer_norm(None, "lnx", 0, "t%dln2_" % t, part=2)

    def emit_tile_mid(t, mode):
        full = mode != "pre"
        xio, base, xio3 = tile_ctx(t)
        LN1 = "lnx"
        for g in range(NS):
            em.op("pe", lambda eng, g=g: eng.matmul(
                ps[7].ap()[0:D, 0:TT], A["wf1_s"][:, g * D:(g + 1) * D], fm(LN1, g, 0, TT),
                start=(g == 0), stop=(g == NS - 1)),
                rd=[(LN1, g), ("wf1_s",)],
                wr=[("pb7",)] if g in (0, NS - 1) else [], tag="t%dfm%d" % (t, g))
        em.op("act", lambda eng: eng.activation(
            out=A["mid2"][0:D, 0:TT], in_=ps[7].ap()[0:D, 0:TT], func=AF.Copy),
            rd=[("pb7",)], wr=[("mid2f",)], tag="t%dfmc" % t)
        if full:
            for g in range(NS):
                em.op("pe", lambda eng, g=g: eng.matmul(
                    ps[7].ap()[0:D, 256:256 + TT], A["wg1_s"][:, g * D:(g + 1) * D], fm(LN1, g, 0, TT),
                    start=(g == 0), stop=(g == NS - 1)),
                    rd=[(LN1, g), ("wg1_s",)],
                    wr=[("pb7",)] if g in (0, NS - 1) else [], tag="t%dgm%d" % (t, g))
            em.op("act", lambda eng: eng.activation(
                out=A["mid2"][0:D, TT:2 * TT], in_=ps[7].ap()[0:D, 256:256 + TT], func=AF.Copy),
                rd=[("pb7",)], wr=[("mid2g",)], tag="t%dgmc" % t)

        def aproj(wname, rhs_name, out_cb, tag2):
            for bp in range(4):
                bank = nextbank()
                for j in range(2):
                    do = 2 * bp + j
                    for c in range(NS):
                        em.op("pe", lambda eng, bank=bank, j=j, do=do, c=c: eng.matmul(
                            ps[bank].ap()[:, j * TT:j * TT + TT],
                            A[wname][:, c * E + do * P: c * E + do * P + P],
                            fm(rhs_name, c, 0, TT),
                            start=(c == 0), stop=(c == NS - 1)),
                            rd=[(rhs_name, c), (wname,)],
                            wr=[("pb%d" % bank,)] if ((c == 0 and j == 0) or (c == NS - 1 and j == 1)) else [],
                            tag=tag2 + "%d_%d_%d" % (bp, j, c))
                out_cb(bp, bank)

        def q_out(bp, bank):
            em.op("act", lambda eng, bp=bp, bank=bank: eng.activation(
                out=fm("sC", 2 * bp, 0, 2 * TT), in_=ps[bank].ap()[:, 0:2 * TT], func=AF.Silu),
                rd=[("pb%d" % bank,)], wr=[("sC", 2 * bp), ("sC", 2 * bp + 1)],
                tag="t%dqs%d" % (t, bp))
        if full:
            aproj("wq_s", LN1, q_out, "t%dq" % t)

        def ka_out(bp, bank):
            em.op("act", lambda eng, bp=bp, bank=bank: eng.activation(
                out=fm("sD", 2 * bp, 0, 2 * TT), in_=ps[bank].ap()[:, 0:2 * TT], func=AF.Silu),
                rd=[("pb%d" % bank,)], wr=[("sD", 2 * bp), ("sD", 2 * bp + 1)],
                tag="t%dks%d" % (t, bp))
        aproj("wka_s", LN1, ka_out, "t%dk" % t)

        for bp in range(4):
            bank = nextbank()
            for j in range(2):
                do = 2 * bp + j
                em.op("pe", lambda eng, bank=bank, j=j, do=do: eng.matmul(
                    ps[bank].ap()[:, j * TT:j * TT + TT],
                    A["wf2_s"][0:D, do * P:do * P + P], A["mid2"][0:D, 0:TT],
                    start=True, stop=True),
                    rd=[("mid2f",), ("wf2_s",)], wr=[("pb%d" % bank,)],
                    tag="t%dfz%d_%d" % (t, bp, j))
            em.op("act", lambda eng, bank=bank, bp=bp: eng.activation(
                out=fm("sH", 2 * bp, 0, 2 * TT), in_=ps[bank].ap()[:, 0:512], func=AF.Copy),
                rd=[("pb%d" % bank,)], wr=[("sH", 2 * bp), ("sH", 2 * bp + 1)], tag="t%dfze%d" % (t, bp))
        for bp in (range(4) if full else ()):
            bank = nextbank()
            for j in range(2):
                do = 2 * bp + j
                em.op("pe", lambda eng, bank=bank, j=j, do=do: eng.matmul(
                    ps[bank].ap()[:, j * TT:j * TT + TT],
                    A["wg2_s"][0:D, do * P:do * P + P], A["mid2"][0:D, TT:2 * TT],
                    start=True, stop=True),
                    rd=[("mid2g",), ("wg2_s",)], wr=[("pb%d" % bank,)],
                    tag="t%dgz%d_%d" % (t, bp, j))
            em.op("act", lambda eng, bank=bank, bp=bp: eng.activation(
                out=fm("sB", 2 * bp, 0, 2 * TT), in_=ps[bank].ap()[:, 0:512], func=AF.Sigmoid),
                rd=[("pb%d" % bank,)], wr=[("sB", 2 * bp), ("sB", 2 * bp + 1)],
                tag="t%dgs%d" % (t, bp))

        for bp in (range(4) if full else ()):
            bank = nextbank()
            for j in range(2):
                do = 2 * bp + j
                em.op("pe", lambda eng, bank=bank, j=j, do=do: eng.matmul(
                    ps[bank].ap()[:, j * TT:j * TT + TT],
                    A["wg2_s"][0:D, do * P:do * P + P], A["mid2"][0:D, TT:2 * TT],
                    start=True, stop=True),
                    rd=[("mid2g",), ("wg2_s",)], wr=[("pb%d" % bank,)],
                    tag="t%dgz%d_%d" % (t, bp, j))
            em.op("act", lambda eng, bank=bank, bp=bp: eng.activation(
                out=fm("sB", 2 * bp, 0, 2 * TT), in_=ps[bank].ap()[:, 0:512], func=AF.Sigmoid),
                rd=[("pb%d" % bank,)], wr=[("sB", 2 * bp), ("sB", 2 * bp + 1)],
                tag="t%dgs%d" % (t, bp))

        def emit_kp(bp):
            for g in (2 * bp, 2 * bp + 1):
                p_ = g % 2
                sidx = (bp % 2) * 2 + p_
                for c in range(C):
                    em.op("dve", lambda eng, g=g, c=c, sidx=sidx: eng.tensor_scalar_mul(
                        out=A["kp"][:, sidx * 256 + c * CH: sidx * 256 + c * CH + CH],
                        in0=fm("sD", g, c * CH, CH),
                        scalar1=A["lam"][:, g * C + c: g * C + c + 1]),
                        rd=[("sD", g), ("lam",)], wr=[("kp", sidx)], tag="t%dKp%d_%d" % (t, g, c))


        def emit_chain_pair(bp):
                for j in range(2):
                    g = 2 * bp + j
                    em.op("act", lambda eng, g=g, j=j: eng.activation(
                        out=A["st1"][:, j * TT:j * TT + TT], in_=fm("sH", g, 0, TT),
                        func=AF.Exp, scale=-1.0),
                        rd=[("sH", g)], wr=[("st1", j)], tag="t%dfe%d_%d" % (t, bp, j))
                    em.op("act", lambda eng, j=j: eng.activation(
                        out=A["st2"][:, j * TT:j * TT + TT], in_=A["st1"][:, j * TT:j * TT + TT],
                        func=AF.Ln, bias=A["cOne"][:, 0:1]),
                        rd=[("st1", j), ("cOne",)], wr=[("st2", j)], tag="t%dfl%d_%d" % (t, bp, j))
                    em.op("dve", lambda eng, g=g, j=j: eng.tensor_tensor_scan(
                        out=A["st1"][:, j * TT:j * TT + TT], data0=A["gate_s"][:, 0:TT],
                        data1=A["st2"][:, j * TT:j * TT + TT], initial=0.0, op0=MULT, op1=ADD),
                        rd=[("st2", j), ("gate_s",)], wr=[("st1", j)], tag="t%dsc%d" % (t, g))
                pr = [("sEx", 0, 2), ("sEx", 0, 3)]
                if full:
                    for j in range(2):
                        em.op("act", lambda eng, j=j: eng.activation(
                            out=A["sEx"][:, 512 + j * TT:512 + j * TT + TT],
                            in_=A["st1"][:, j * TT:j * TT + TT],
                            func=AF.Exp, scale=-1.0, bias=A["cLnS"][:, 0:1]),
                            rd=[("st1", j), ("cLnS",)], wr=[("sEx", 0, 2 + j)],
                            tag="t%dEp%d_%d" % (t, bp, j))
                    em.op("dve", lambda eng, bp=bp: eng.tensor_mul(
                        out=fm("sC", 2 * bp, 0, 2 * TT), in0=fm("sC", 2 * bp, 0, 2 * TT),
                        in1=A["sEx"][:, 512:1024]),
                        rd=[("sC", 2 * bp), ("sC", 2 * bp + 1)] + pr,
                        wr=[("sC", 2 * bp), ("sC", 2 * bp + 1)], tag="t%dQh%d" % (t, bp))
                em.op("act", lambda eng, bp=bp: eng.activation(
                    out=A["sEx"][:, 512:1024], in_=A["st1"][:, 0:512], func=AF.Exp),
                    rd=[("st1", 0), ("st1", 1)], wr=pr, tag="t%dEm%d" % (t, bp))
                em.op("dve", lambda eng, bp=bp: eng.tensor_mul(
                    out=fm("sD", 2 * bp, 0, 2 * TT), in0=fm("sD", 2 * bp, 0, 2 * TT),
                    in1=A["sEx"][:, 512:1024]),
                    rd=[("sD", 2 * bp), ("sD", 2 * bp + 1)] + pr,
                    wr=[("sD", 2 * bp), ("sD", 2 * bp + 1)], tag="t%dKh%d" % (t, bp))
                st1v = A["st1"].rearrange("p (s c l) -> p s c l", s=2, c=C)
                lam3 = A["lam"].rearrange("p (g c) -> p g c", g=NS)
                em.op("act", lambda eng, bp=bp, sv=st1v, l3=lam3: eng.activation(
                    out=l3[:, 2 * bp:2 * bp + 2, :], in_=sv[:, :, :, CH - 1],
                    func=AF.Exp, scale=-1.0),
                    rd=[("st1", 0), ("st1", 1)], wr=[("lam",)], tag="t%dlam%d" % (t, bp))
                if bp < 2:
                    emit_kp(bp)

        def emit_vtm_fill(sub, half):
            bank = nextbank()
            for c in range(NS):
                em.op("pe", lambda eng, bank=bank, sub=sub, half=half, c=c: eng.matmul(
                    ps[bank].ap()[:, 0:512], fm(LN1, c, sub * P, P),
                    A["wva_s"][:, c * E + half * 512: c * E + half * 512 + 512],
                    start=(c == 0), stop=(c == NS - 1)),
                    rd=[(LN1, c), ("wva_s",)],
                    wr=[("pb%d" % bank,)] if c in (0, NS - 1) else [],
                    tag="t%dvt%d_%d_%d" % (t, sub, half, c))
            em.op("dve", lambda eng, bank=bank, sub=sub, half=half: eng.tensor_copy(
                out=A["sA"][:, sub * E + half * 512: sub * E + half * 512 + 512],
                in_=ps[bank].ap()[:, 0:512]),
                rd=[("pb%d" % bank,)], wr=vtm_keys(sub, half),
                tag="t%dve%d_%d" % (t, sub, half))

        for p4 in range(4):
            emit_chain_pair(p4)
            emit_vtm_fill(p4 // 2, p4 % 2)

        slot0 = (4 * t) % 5

        late = []

        def scan_A(gp):
            pair = (2 * gp, 2 * gp + 1)
            kb = (gp % 2) * 1024
            for g in pair:
                b0 = 4 * (g % 2)
                for c in (range(C) if full else ()):
                    for j in range(2):
                        em.op("pe", lambda eng, g=g, c=c, j=j, b0=b0: eng.matmul(
                            ps[b0 + j].ap()[D * (c % 2):D * (c % 2) + D, c * CH: c * CH + CH],
                            fmh("sD", j, g, c * CH, CH), fmh("sC", j, g, c * CH, CH),
                            start=True, stop=True, tile_position=(D * j, D * (c % 2))),
                            rd=[("sD", g), ("sC", g)], wr=[("pb%d" % (b0 + j),)],
                            tag="t%dP%d_%d_%d" % (t, g, c, j))
            eb = (gp % 2) * 1024
            for g in (pair if full else ()):
                p_ = g % 2
                b0 = 4 * p_
                for j in range(2):
                    em.op("dve", lambda eng, j=j, b0=b0, p_=p_, eb=eb: eng.tensor_mul(
                        out=A["sEx"][:, eb + p_ * 512 + j * 256: eb + p_ * 512 + j * 256 + 256],
                        in0=ps[b0 + j].ap()[:, 0:256], in1=A["mask_s"][:, 0:256]),
                        rd=[("pb%d" % (b0 + j),), ("mask_s",)], wr=[("sEx", gp % 2, 2 * p_ + j)],
                        tag="t%dPM%d_%d" % (t, g, j))
            for g in pair:
                p_ = g % 2
                sidx = (gp % 2) * 2 + p_
                for c in range(C):
                    em.op("pe", lambda eng, g=g, c=c, p_=p_, sidx=sidx: eng.transpose(
                        ps[6].ap()[D * (c % 2):D * (c % 2) + D, p_ * 512 + c * P: p_ * 512 + c * P + P],
                        A["kp"][:, sidx * 256 + c * CH: sidx * 256 + c * CH + CH], A["ident_s"][:, :],
                        tile_position=(0, D * (c % 2))),
                        rd=[("kp", sidx), ("ident_s",)], wr=[("pt6", p_)], tag="t%dT%d_%d" % (t, g, c))
            for p_ in range(2):
                em.op("dve", lambda eng, p_=p_, kb=kb: eng.tensor_copy(
                    out=A["kt"][:, kb + p_ * 512: kb + p_ * 512 + 512],
                    in_=ps[6].ap()[:, p_ * 512: p_ * 512 + 512]),
                    rd=[("pt6", 0), ("pt6", 1)], wr=[("kt", gp % 2, p_)],
                    tag="t%dTe%d_%d" % (t, gp, p_))

        def scan_B(gp):
            pair = (2 * gp, 2 * gp + 1)
            kb = (gp % 2) * 1024
            eb = (gp % 2) * 1024
            for g in pair:
                p_ = g % 2
                for c in range(C):
                    for j in range(2):
                        hh = 2 * g + j
                        em.op("pe", lambda eng, g=g, c=c, j=j, p_=p_, hh=hh, kb=kb: eng.matmul(
                            ps[2 + (c % 2)].ap()[D * j:D * j + D,
                                                 p_ * 128 + (c // 2) * CH: p_ * 128 + (c // 2) * CH + CH],
                            A["kt"][D * (c % 2):D * (c % 2) + D,
                                    kb + p_ * 512 + c * P + j * D: kb + p_ * 512 + c * P + j * D + D],
                            A["sA"][D * (c % 2):D * (c % 2) + D,
                                    (c // 2) * E + hh * D: (c // 2) * E + hh * D + D],
                            start=True, stop=True, tile_position=(D * (c % 2), D * j)),
                            rd=[("kt", gp % 2, p_), ("vtm", c // 2, hh // 8)],
                            wr=[("pb%d" % (2 + c % 2),), ("pb%d" % (2 + c % 2), "d0"), ("pb%d" % (2 + c % 2), "d1")],
                            tag="t%dD%d_%d_%d" % (t, g, c, j))
            for g in pair:
                p_ = g % 2
                for c in range(C):
                    si = (slot0 + c) % 5
                    so = (slot0 + c + 1) % 5
                    em.op("dve", lambda eng, g=g, c=c, si=si, so=so, p_=p_: eng.scalar_tensor_tensor(
                        out=A["Ssb"][:, (g * 5 + so) * D: (g * 5 + so) * D + D],
                        in0=A["Ssb"][:, (g * 5 + si) * D: (g * 5 + si) * D + D],
                        scalar=A["lam"][:, g * C + c: g * C + c + 1],
                        in1=ps[2 + (c % 2)].ap()[:, p_ * 128 + (c // 2) * CH: p_ * 128 + (c // 2) * CH + CH],
                        op0=MULT, op1=ADD),
                        rd=[("S", g, si), ("lam",), ("pb%d" % (2 + c % 2), "d%d" % p_)],
                        wr=[("S", g, so)],
                        tag="t%dSu%d_%d" % (t, g, c))
            if gp < 2:
                late.append(lambda gp=gp: emit_kp(gp + 2))
            for g in (pair if full else ()):
                p_ = g % 2
                for c in range(C):
                    for j in range(2):
                        hh = 2 * g + j
                        em.op("pe", lambda eng, g=g, c=c, j=j, p_=p_, hh=hh: eng.matmul(
                            ps[c % 2].ap()[D * j:D * j + D,
                                           256 + p_ * 128 + (c // 2) * CH: 256 + p_ * 128 + (c // 2) * CH + CH],
                            A["sA"][D * (c % 2):D * (c % 2) + D,
                                    (c // 2) * E + hh * D: (c // 2) * E + hh * D + D],
                            A["sEx"][D * (c % 2):D * (c % 2) + D,
                                     eb + p_ * 512 + j * 256 + c * CH: eb + p_ * 512 + j * 256 + c * CH + CH],
                            start=True, stop=True, tile_position=(D * (c % 2), D * j)),
                            rd=[("vtm", c // 2, hh // 8), ("sEx", gp % 2, 2 * p_ + j)],
                            wr=[("pb%d" % (c % 2),)], tag="t%dOI%d_%d_%d" % (t, g, c, j))
            for g in (pair if full else ()):
                p_ = g % 2
                for c in range(C):
                    for j in range(2):
                        si = (slot0 + c) % 5
                        em.op("pe", lambda eng, g=g, c=c, j=j, si=si, p_=p_: eng.matmul(
                            ps[4 + j].ap()[D * j:D * j + D, p_ * 256 + c * CH: p_ * 256 + c * CH + CH],
                            A["Ssb"][D * j:D * j + D, (g * 5 + si) * D: (g * 5 + si) * D + D],
                            fmh("sC", j, g, c * CH, CH),
                            start=True, stop=True, tile_position=(D * j, D * j)),
                            rd=[("S", g, si), ("sC", g)],
                            wr=[("pb%d" % (4 + j),)],
                            tag="t%dOC%d_%d_%d" % (t, g, c, j))
            if full:
                def oie(g):
                    p_ = g % 2
                    sf3 = fm("sF", g, 0, TT).rearrange("p (a l) -> p a l", l=CH)
                    for p2 in range(2):
                        em.op("act", lambda eng, p2=p2, p_=p_, o3=sf3: eng.activation(
                            out=o3[:, p2::2, :],
                            in_=ps[p2].ap()[:, 256 + p_ * 128: 256 + p_ * 128 + 128], func=AF.Copy),
                            rd=[("pb%d" % p2,)], wr=[("sF", g)], tag="t%doie%d_%d" % (t, g, p2))

                def ocv(g):
                    p_ = g % 2
                    for j in range(2):
                        em.op("act", lambda eng, g=g, j=j, p_=p_: eng.activation(
                            out=fmh("sG", j, g, 0, TT),
                            in_=ps[4 + j].ap()[D * j:D * j + D, p_ * 256: p_ * 256 + 256],
                            func=AF.Copy),
                            rd=[("pb%d" % (4 + j),)], wr=[("sG", g)], tag="t%doc%d_%d" % (t, g, j))

                def omg(g):
                    em.op("dve", lambda eng, g=g: eng.tensor_add(
                        out=fm("sF", g, 0, TT), in0=fm("sF", g, 0, TT), in1=fm("sG", g, 0, TT)),
                        rd=[("sF", g), ("sG", g)], wr=[("sF", g)], tag="t%dom%d" % (t, g))
                    em.op("dve", lambda eng, g=g: eng.tensor_mul(
                        out=fm("sF", g, 0, TT), in0=fm("sF", g, 0, TT), in1=fm("sB", g, 0, TT)),
                        rd=[("sF", g), ("sB", g)], wr=[("sF", g)], tag="t%dog%d" % (t, g))
                g0, g1 = pair
                oie(g0)
                oie(g1)
                ocv(g0)
                ocv(g1)
                late.append(lambda g0=g0: omg(g0))
                late.append(lambda g1=g1: omg(g1))

        scan_A(0)
        scan_A(1)
        scan_B(0)
        for th in late:
            th()
        late = []
        scan_A(2)
        scan_B(1)
        for th in late:
            th()
        late = []
        scan_A(3)
        scan_B(2)
        for th in late:
            th()
        late = []
        scan_B(3)

        for th in late:
            th()
        late = []

    def emit_tile_back_a1(t, mode):
        full = mode != "pre"
        xio, base, xio3 = tile_ctx(t)
        if not full:
            return
        for g in range(NS):
            em.op("act", lambda eng, g=g: eng.activation(
                out=fm("sG", g, 0, TT), in_=fm("sF", g, 0, TT), func=AF.Square),
                rd=[("sF", g)], wr=[("sG", g)], tag="t%dq2_%d" % (t, g))
        for g in range(NS):
            em.op("pe", lambda eng, g=g: eng.matmul(
                ps[7].ap()[0:16, 0:TT], A["gnacc_s"][:, g * 16:(g + 1) * 16],
                fm("sG", g, 0, TT), start=(g == 0), stop=(g == NS - 1)),
                rd=[("sG", g), ("gnacc_s",)],
                wr=[("pb7",)] if g in (0, NS - 1) else [], tag="t%dgn%d" % (t, g))
        em.op("act", lambda eng: eng.activation(
            out=A["gns"][:, :], in_=ps[7].ap()[0:16, 0:TT], func=AF.Ln,
            scale=1.0 / D, bias=A["cEps"][0:16, 0:1]),
            rd=[("pb7",), ("cEps",)], wr=[("gns",)], tag="t%dgl" % t)
        em.op("act", lambda eng: eng.activation(
            out=A["gnsb"][:, :], in_=A["gns"][:, :], func=AF.Exp, scale=-0.5),
            rd=[("gns",)], wr=[("gnsb",)], tag="t%dgr" % t)
        for g in range(NS):
            bank = 7 if g % 2 == 0 else 2
            em.op("pe", lambda eng, g=g, bank=bank: eng.matmul(
                ps[bank].ap()[0:P, 0:TT], A["gnsel_s"][0:16, g * P:(g + 1) * P],
                A["gnsb"][0:16, 0:TT], start=True, stop=True),
                rd=[("gnsb",), ("gnsel_s",)],
                wr=[("pb%d" % bank,), ("pb%d" % bank, "d0"), ("pb%d" % bank, "d1")],
                tag="t%dgb%d" % (t, g))
            em.op("dve", lambda eng, g=g, bank=bank: eng.scalar_tensor_tensor(
                out=fm("sF", g, 0, TT), in0=fm("sF", g, 0, TT),
                scalar=A["nw_s"][:, g:g + 1], in1=ps[bank].ap()[0:P, 0:TT],
                op0=MULT, op1=MULT),
                rd=[("sF", g), ("nw",), ("pb%d" % bank,)], wr=[("sF", g)],
                tag="t%dga%d" % (t, g))
    def emit_tile_back_a2(t, mode):
        full = mode != "pre"
        xio, base, xio3 = tile_ctx(t)
        if not full:
            return
        for bp in range(4):
            bank = (4, 5, 0, 1)[bp]
            for j in range(2):
                do = 2 * bp + j
                for c in range(NS):
                    em.op("pe", lambda eng, bank=bank, j=j, do=do, c=c: eng.matmul(
                        ps[bank].ap()[:, j * TT:j * TT + TT],
                        A["wo_s"][:, c * E + do * P: c * E + do * P + P],
                        fm("sF", c, 0, TT), start=(c == 0), stop=(c == NS - 1)),
                        rd=[("sF", c), ("wo_s",)],
                        wr=[("pb%d" % bank,)] if ((c == 0 and j == 0) or (c == NS - 1 and j == 1)) else [],
                        tag="t%dwo%d_%d_%d" % (t, bp, j, c))
            for j in range(2):
                do = 2 * bp + j
                em.op("dve", lambda eng, bank=bank, j=j, do=do, xio=xio: eng.tensor_add(
                    out=fm(xio, do, 0, TT), in0=fm(xio, do, 0, TT),
                    in1=ps[bank].ap()[:, j * TT:j * TT + TT]),
                    rd=[("pb%d" % bank,), (xio, do)], wr=[(xio, do)],
                    tag="t%dx2_%d_%d" % (t, bp, j))
        if t > 0:
            em.op("dve", lambda eng: eng.tensor_copy(
                out=ln2_3[:, :, 0:1], in_=ln2_3[:, :, TT:TT1]),
                rd=[("lnx2", g) for g in range(NS)], wr=[("lnx2c0",)], tag="t%dc0b" % t)
        layer_norm(xio, "lnx2", 1, "t%dln3_" % t, part=1, cast="sH", sqb="sG")

    def emit_tile_back_b(t, mode):
        if mode == "pre":
            return
        layer_norm(None, "lnx2", 1, "t%dln3_" % t, part=2,
                   after_slab=ffn_mixes("lnx2", "lnx2c0", "t%df3" % t, "sC", "sF"))

    def emit_tile_back_c(t, mode):
        full = mode != "pre"
        xio, base, xio3 = tile_ctx(t)
        if full:
            ffn_a("t%df3" % t, bufR="sC", bufK="sF", rbuf="sH", kbuf="sC", k_first=False)
            ffn_b(xio, xio, "t%df3" % t, rbuf="sH", kbuf="sC", ts_="st2")
        if mode == "own":
            ob = base - (NTILE // 2) * TT
            em.op("sp", lambda eng, b=ob, s=xio3: eng.dma_start(
                out=outT3[:, :, b:b + TT], in_=s[:, :, :]),
                rd=[(xio, g) for g in range(NS)], dma="sp", tag="t%dout" % t)

    NPRE = NTILE // 2

    def mode_of(t):
        return "pre" if t < NPRE - 1 else ("pre_last" if t == NPRE - 1 else "own")

    emit_tile_front_a(0, mode_of(0))
    emit_tile_front_b(0, mode_of(0))
    emit_tile_front2a(0, mode_of(0))
    emit_tile_front2b(0, mode_of(0))
    emit_tile_front2c(0, mode_of(0))
    for t in range(NTILE):
        m1 = mode_of(t + 1) if t + 1 < NTILE else None
        emit_tile_mid(t, mode_of(t))
        if m1:
            emit_tile_front_a(t + 1, m1)
        emit_tile_back_a1(t, mode_of(t))
        if m1:
            emit_tile_front_b(t + 1, m1)
        emit_tile_back_a2(t, mode_of(t))
        if m1:
            emit_tile_front2a(t + 1, m1)
        if m1:
            emit_tile_front2b(t + 1, m1)
        emit_tile_back_b(t, mode_of(t))
        emit_tile_back_c(t, mode_of(t))
        if m1:
            emit_tile_front2c(t + 1, m1)

    if limit is not None:
        import sys
        for i, o in enumerate(em.ops):
            if i % 50 == 0:
                print("op[%d] %s %s" % (i, o.eng, o.tag), file=sys.stderr)
        em.ops = em.ops[:limit]
    em.finalize(sems, dsems)
    with nc.Block() as block:
        @block.sync
        def _(eng):
            em.emit("sp", eng)
            for q in sorted(dsems):
                if em.dcounts.get(q):
                    eng.wait_ge(dsems[q], em.dcounts[q])

        @block.tensor
        def _(eng):
            em.emit("pe", eng)

        @block.vector
        def _(eng):
            em.emit("dve", eng)

        @block.scalar
        def _(eng):
            em.emit("act", eng)

        @block.gpsimd
        def _(eng):
            em.emit("pool", eng)

    ctx.close()
    return nc


def _consts(TT):
    bf = ml_dtypes.bfloat16
    ident = np.eye(P, dtype=np.float32).astype(bf)
    col = np.arange(256) % CH
    row = np.arange(P) % CH
    mask = (row[:, None] <= col[None, :]).astype(np.float32).astype(bf)
    gate = np.ascontiguousarray(np.ones((P, 1), np.float32) * ((np.arange(TT) % CH) != 0))
    gate = gate.astype(bf)
    ones1 = np.ones((P, 1), np.float32).astype(bf)
    onesb = np.ones((1, P), np.float32).astype(bf)
    gnacc = np.zeros((P, P), np.float32)
    for g in range(NS):
        for p in range(P):
            gnacc[p, g * 16 + 2 * g + p // D] = 1
    gnsel = np.zeros((16, NS * P), np.float32)
    for g in range(NS):
        for p in range(P):
            gnsel[2 * g + p // D, g * P + p] = 1
    return dict(ident=ident, mask=mask, gate=gate, ones1=ones1, onesb=onesb,
                gnacc=gnacc.astype(bf), gnsel=gnsel.astype(bf))


def _wmaps(inputs):
    def c(x):
        return np.ascontiguousarray(np.asarray(x, dtype=np.float32))
    return {
        "wr": c(np.asarray(inputs["L0_ffn_W_r"]).T),
        "wk": c(np.asarray(inputs["L0_ffn_W_k"]).T),
        "wv": c(np.asarray(inputs["L0_ffn_W_v"]).T),
        "wq": c(inputs["L0_attn_Wq"]),
        "wka": c(inputs["L0_attn_Wk"]),
        "wva": c(inputs["L0_attn_Wv"]),
        "wo": c(inputs["L0_attn_Wo"]),
        "wf1": c(inputs["L0_attn_Wf1"]),
        "wf2": c(inputs["L0_attn_Wf2"]),
        "wg1": c(inputs["L0_attn_Wg1"]),
        "wg2": c(inputs["L0_attn_Wg2"]),
        "mu_r": c(np.asarray(inputs["L0_ffn_mu_r"]).reshape(NS, P).T),
        "mu_k": c(np.asarray(inputs["L0_ffn_mu_k"]).reshape(NS, P).T),
        "nw": c(np.asarray(inputs["L0_attn_norm_weight"]).reshape(NS, P).T),
    }


def kernel(**inputs):
    x = np.asarray(inputs["L0_attn_x"], dtype=np.float32)
    B, T, _ = x.shape
    TT = 256
    HALF = T // 2
    nc = build_program(T, TT)
    wm = _wmaps(inputs)
    wm.update(_consts(TT))
    in_maps = []
    for cid in range(8):
        b, h = cid // 2, cid % 2
        mp = dict(wm)
        xt = np.asarray(x[b]).T
        if h == 1:
            mp["xT"] = np.ascontiguousarray(xt)
        else:
            mp["xT"] = np.ascontiguousarray(
                np.concatenate([np.zeros((E, HALF), np.float32), xt[:, :HALF]], axis=1))
        in_maps.append(mp)
    res = run_bass_kernel_spmd(nc, in_maps, core_ids=list(range(8)))
    out = np.empty((B, T, E), dtype=np.float32)
    for b in range(B):
        out[b, :HALF] = res.results[2 * b]["outT"].T
        out[b, HALF:] = res.results[2 * b + 1]["outT"].T
    return out
